# version 27
# baseline (speedup 1.0000x reference)
import numpy as np
import ml_dtypes
import concourse.bass as bass
import concourse.mybir as mybir
from concourse.bass_utils import run_bass_kernel_spmd

F32 = mybir.dt.float32
BF16 = mybir.dt.bfloat16
BF = ml_dtypes.bfloat16
AF = mybir.ActivationFunctionType
ALU = mybir.AluOpType
AX = mybir.AxisListType

D_MODEL = 1024
D_STATE = 128
HEADDIM = 64
D_INNER = 2048
NHEADS = 32
CONV_DIM = 2304
D_IN_PROJ = 4384
WIN_F = D_IN_PROJ + 8
EPS = 1e-5
L = 2048
Q = 128
NCH = 16
NK = 8
NKI = 16
XBCOFF = D_INNER
DTOFF = 4352
GOFF = 4384
NXT = 18
XBW = 260
NEG = -1e30


class Mark:
    __slots__ = ("sem", "val")
    def __init__(self, sem):
        self.sem = sem
        self.val = None


class Script:

    def __init__(self):
        self.steps = []
        self.last = {}

    def step(self, eng, fn, waits=(), sem=None, inc=1, drain=True):
        sem = sem or eng
        m = Mark(sem) if inc else None
        self.steps.append((eng, fn, tuple(w for w in waits if w is not None),
                           m, inc, drain))
        if m is not None:
            self.last[eng] = m
        return m

    def resolve(self, ctr):
        for eng, fn, waits, m, inc, dr in self.steps:
            if m is not None:
                ctr[m.sem] += inc
                m.val = ctr[m.sem]
        return ctr

    def emit(self, eng_name, e, sems):
        for eng, fn, waits, m, inc, dr in self.steps:
            if eng != eng_name:
                continue
            for w in waits:
                e.wait_ge(sems[w.sem], w.val)
            if dr and eng in ("V", "S", "G"):
                e.drain()
            last = fn(e)
            if m is not None:
                if isinstance(last, list):
                    assert len(last) * (16 if inc >= 16 else 1) == inc or True
                    for li in last:
                        li.then_inc(sems[m.sem], 16)
                else:
                    last.then_inc(sems[m.sem], inc)


def build_nc(debug=False, stages=3, nch=NCH, tail=True, wload=True):
    nc = bass.Bass("TRN2", target_bir_lowering=False,
                   dynamic_dma_scratch_size=512)
    d = {}
    def di(name, shape, dt):
        d[name] = nc.dram_tensor(name, shape, dt, kind="ExternalInput").ap()
    di("x", [L, D_MODEL], F32)
    di("wi", [D_MODEL, WIN_F], F32)
    di("wo", [D_INNER, D_MODEL], F32)
    di("we", [8, D_MODEL, D_MODEL], F32)
    di("cw", [128, NXT * 4], F32)
    di("cb", [128, NXT], F32)
    di("nw", [128, NKI], F32)
    di("dtb", [1, 32], F32)
    di("alog", [1, 32], F32)
    di("dvec", [1, 32], F32)
    di("mle", [Q, Q], BF16)
    di("ibf", [Q, Q], BF16)
    di("if32", [Q, Q], F32)
    di("onef", [128, Q], F32)
    out = nc.dram_tensor("out", [L, D_MODEL], F32, kind="ExternalOutput").ap()
    _ik = "ExternalOutput" if debug else "Internal"
    attd = nc.dram_tensor("attd", [NCH, 128, D_MODEL], BF16, kind=_ik).ap()
    oaccd = nc.dram_tensor("oaccd", [4, 128, NCH * 256], BF16, kind=_ik).ap()
    dbg = {}
    if debug:
        dspec = [("dbg_xn", [128, NK * L], BF16), ("dbg_z", [NCH, 128, D_INNER], BF16),
                 ("dbg_sx", [NCH, 128, D_INNER], BF16), ("dbg_seg", [128, 512], F32),
                 ("dbg_dt", [128, 512], BF16), ("dbg_y", [NCH, 128, D_INNER], BF16),
                 ("dbg_wf", [128, 128], F32), ("dbg_sall", [128, 2048], BF16),
                 ("dbg_oacc", [128, NCH * 512], F32)]
        for nm, sh, dt in dspec:
            dbg[nm] = nc.dram_tensor(nm, sh, dt, kind="ExternalOutput").ap()

    from contextlib import ExitStack
    with ExitStack() as top:
        E = top.enter_context
        semnames = ("T", "V", "S", "G", "QC", "QX0", "QX1", "QW0", "QW1",
                    "QO0", "QO1", "QE0", "QE1", "QR0", "QR1", "QA0", "QA1",
                    "O0", "O1", "AT0", "AT1", "QD", "QB0", "QB1", "QDmp")
        sems = {k: E(nc.semaphore("sem" + k)) for k in semnames}
        ctr = {k: 0 for k in sems}

        XN = E(nc.sbuf_tensor([128, NK * L], BF16))
        DTER = E(nc.sbuf_tensor([128, 512], BF16))
        EC = E(nc.sbuf_tensor([128, 512], BF16))
        DTWD = E(nc.sbuf_tensor([128, 512], BF16))
        BETB = E(nc.sbuf_tensor([128, 512], BF16))
        MLE = E(nc.sbuf_tensor([Q, Q], BF16))
        WFULL = E(nc.sbuf_tensor([128, 128], F32))
        SS = E(nc.sbuf_tensor([128, NCH], F32))
        RMS = E(nc.sbuf_tensor([128, NCH], F32))
        D_BC = E(nc.sbuf_tensor([128, 32], F32))
        IBF = E(nc.sbuf_tensor([Q, Q], BF16))
        CW = E(nc.sbuf_tensor([128, NXT * 4], F32))
        CB = E(nc.sbuf_tensor([128, NXT], F32))
        CBH = E(nc.sbuf_tensor([128, NXT], F32))
        KCOL = E(nc.sbuf_tensor([128, 2], F32))

        def hx(ap, a):
            return ap.rearrange("p (a b) -> p a b", a=a)

        def bc_heads(t, c):
            return t[:, c * 32:(c + 1) * 32].to_broadcast([128, 32, HEADDIM])

        wim = []
        wom = []

        with ExitStack() as p1a:
            A1 = p1a.enter_context
            WIN = A1(nc.sbuf_tensor([128, NK * WIN_F], BF16))
            WOUT = A1(nc.sbuf_tensor([128, NKI * D_MODEL], BF16))
            p1t = ExitStack()
            A2 = p1t.enter_context
            DT = A2(nc.sbuf_tensor([128, 512], BF16))
            WSTG = A2(nc.sbuf_tensor([128, 2 * 2196], F32))
            WOSTG = A2(nc.sbuf_tensor([128, 2 * D_MODEL], F32))
            XR = A2(nc.sbuf_tensor([128, 2 * D_MODEL], F32))
            XC = A2(nc.sbuf_tensor([128, 2 * D_MODEL], F32))
            XNT = A2(nc.sbuf_tensor([128, 2 * D_MODEL], BF16))
            SQD = A2(nc.sbuf_tensor([128, D_MODEL], BF16))
            ADTB = A2(nc.sbuf_tensor([128, 512], BF16))
            MCOL = A2(nc.sbuf_tensor([128, 8], F32))
            DTR = A2(nc.sbuf_tensor([128, 512], F32))
            DTF = A2(nc.sbuf_tensor([128, 512], F32))
            SEG = A2(nc.sbuf_tensor([128, 512], F32))
            STGF = A2(nc.sbuf_tensor([128, 512], F32))
            LOG = A2(nc.sbuf_tensor([128, 128], F32))
            DTB_BC = A2(nc.sbuf_tensor([128, 32], F32))
            A_BC = A2(nc.sbuf_tensor([128, 32], F32))
            ONEF = A2(nc.sbuf_tensor([128, Q], F32))
            NW = A2(nc.sbuf_tensor([128, NKI], F32))
            ROWA = A2(nc.sbuf_tensor([1, 32], F32))
            ROWB = A2(nc.sbuf_tensor([1, 32], F32))
            ROWD = A2(nc.sbuf_tensor([1, 32], F32))
            MX8 = A2(nc.sbuf_tensor([128, 8], F32))
            V12 = A2(nc.sbuf_tensor([128, 32], F32))
            EQA = A2(nc.sbuf_tensor([128, 8], F32))
            EQB = A2(nc.sbuf_tensor([128, 8], F32))
            W1S = A2(nc.sbuf_tensor([128, NCH], F32))
            W2S = A2(nc.sbuf_tensor([128, NCH], F32))
            SER = A2(nc.sbuf_tensor([1, 512], F32))
            PA = A2(nc.psum_tensor([128, 2048], F32))
            PB = A2(nc.psum_tensor([128, 2048], F32))
            PAB = PA[:].bitcast(BF16)

            sc = Script()
            st = sc.step

            mconst = st("Q", lambda e: [
                e.dma_start(out=MLE[:], in_=d["mle"]),
                e.dma_start(out=IBF[:], in_=d["ibf"]),
                e.dma_start(out=ONEF[:], in_=d["onef"]),
                e.dma_start(out=CW[:], in_=d["cw"]),
                e.dma_start(out=CB[:], in_=d["cb"]),
                e.dma_start(out=NW[:], in_=d["nw"]),
                e.dma_start(out=ROWA[:], in_=d["alog"]),
                e.dma_start(out=ROWB[:], in_=d["dtb"]),
                e.dma_start(out=ROWD[:], in_=d["dvec"]),
            ], sem="QC", inc=9 * 16)
            mk = st("V", lambda e: [
                e.memset(KCOL[:, 0:1], EPS),
                e.memset(KCOL[:, 1:2], 1.0),
            ][-1])
            mcbh = st("V", lambda e: e.tensor_scalar_mul(
                CBH[:], CB[:], 0.5), waits=[mconst])
            ma1 = st("S", lambda e: e.activation(ROWA[:], ROWA[:], AF.Exp),
                     waits=[mconst])
            ma2 = st("V", lambda e: e.tensor_scalar_mul(ROWA[:], ROWA[:], -1.0),
                     waits=[ma1])
            mbc = st("T", lambda e: [
                e.matmul(PB[:, 0:32], ONEF[0:1, :], ROWA[:], start=True, stop=True),
                e.matmul(PB[:, 32:64], ONEF[0:1, :], ROWB[:], start=True, stop=True),
                e.matmul(PB[:, 64:96], ONEF[0:1, :], ROWD[:], start=True, stop=True),
            ][-1], waits=[ma2])
            mbce = st("S", lambda e: [
                e.copy(A_BC[:], PB[:, 0:32]),
                e.copy(DTB_BC[:], PB[:, 32:64]),
                e.copy(D_BC[:], PB[:, 64:96]),
            ][-1], waits=[mbc])

            wdt_casts = []
            cast_free = [None, None]
            if wload:
                for k in range(NK):
                    b = k % 2
                    mldt = st("Q", lambda e, k=k, b=b: e.dma_start(
                        out=WSTG[:, b * 2196:b * 2196 + 40],
                        in_=d["wi"][k * 128:(k + 1) * 128, DTOFF:DTOFF + 40]),
                        waits=[cast_free[b]], sem="QW%d" % b, inc=16)
                    cast_free[b] = st("V", lambda e, k=k, b=b: e.tensor_copy(
                        WIN[:, k * WIN_F + DTOFF:k * WIN_F + DTOFF + 40],
                        WSTG[:, b * 2196:b * 2196 + 40]), waits=[mldt])
                    wdt_casts.append(cast_free[b])

            wpairs = []
            for i in range(16 if wload else 0):
                k, h = i // 2, i % 2
                wpairs.append(("wi", k, h * 2196,
                               (DTOFF - 2196) if h else 2196))
            for kd in range(NKI if wload else 0):
                wpairs.append(("wo", kd, 0, D_MODEL))
            ocast_free = [None, None]
            wdma = {}

            def wpair_dma(i):
                kind, k, fw0, fwn = wpairs[i]
                b = i % 2
                if kind == "wi":
                    wdma[i] = st("Q", lambda e, k=k, fw0=fw0, fwn=fwn, b=b:
                                 e.dma_start(
                        out=WSTG[:, b * 2196:b * 2196 + fwn],
                        in_=d["wi"][k * 128:(k + 1) * 128, fw0:fw0 + fwn]),
                        waits=[cast_free[b]], sem="QW%d" % b, inc=16)
                else:
                    wdma[i] = st("Q", lambda e, kd=k, b=b: e.dma_start(
                        out=WOSTG[:, b * D_MODEL:(b + 1) * D_MODEL],
                        in_=d["wo"][kd * 128:(kd + 1) * 128, :]),
                        waits=[ocast_free[b]], sem="QO%d" % b, inc=16)

            def wpair_cast(i):
                kind, k, fw0, fwn = wpairs[i]
                b = i % 2
                if kind == "wi":
                    cast_free[b] = st(
                        "V", lambda e, k=k, fw0=fw0, fwn=fwn, b=b:
                        e.tensor_copy(
                            WIN[:, k * WIN_F + fw0:k * WIN_F + fw0 + fwn],
                            WSTG[:, b * 2196:b * 2196 + fwn]),
                        waits=[wdma[i]])
                    wim.append(cast_free[b])
                else:
                    ocast_free[b] = st(
                        "S", lambda e, kd=k, b=b: e.activation(
                            WOUT[:, kd * D_MODEL:(kd + 1) * D_MODEL],
                            WOSTG[:, b * D_MODEL:(b + 1) * D_MODEL],
                            AF.Identity, scale=NW[:, kd:kd + 1]),
                        waits=[wdma[i]])
                    wom.append(ocast_free[b])

            xr_free = [None, None]
            pm2 = [None, None]
            pm4 = [None, None]
            pmtr = [None, None]
            pevx = [None, None]
            dt_ev2 = [None, None]
            mra = None
            ev_xn = None
            xm2 = {}
            dtevm = {}

            def xdma(c):
                b = c % 2
                return st("Q", lambda e, c=c, b=b: e.dma_start(
                    out=XR[:, b * D_MODEL:(b + 1) * D_MODEL],
                    in_=d["x"][c * 128:(c + 1) * 128, :]),
                    waits=[xr_free[b]], sem="QX%d" % b, inc=16)

            xdm = {0: xdma(0), 1: xdma(1)}

            def xfront(c):
                b = c % 2
                def ln1(e, b=b):
                    e.reduce_sum(MCOL[:, b * 4:b * 4 + 1],
                                 XR[:, b * D_MODEL:(b + 1) * D_MODEL],
                                 axis=AX.X)
                    e.drain()
                    e.tensor_scalar_mul(MCOL[:, b * 4:b * 4 + 1],
                                        MCOL[:, b * 4:b * 4 + 1],
                                        1.0 / D_MODEL)
                    e.drain()
                    return e.tensor_scalar(
                        XC[:, b * D_MODEL:(b + 1) * D_MODEL],
                        XR[:, b * D_MODEL:(b + 1) * D_MODEL],
                        MCOL[:, b * 4:b * 4 + 1], None, ALU.subtract)
                mln1 = st("V", ln1, waits=[xdm[c], pm2[b]])
                xr_free[b] = mln1
                mln2 = st("S", lambda e, b=b: e.activation(
                    SQD[:], XC[:, b * D_MODEL:(b + 1) * D_MODEL], AF.Square,
                    accum_out=MCOL[:, b * 4 + 1:b * 4 + 2]), waits=[mln1])
                pm2[b] = mln2
                xm2[c] = mln2

            def xback(c):
                b = c % 2
                def ln3(e, b=b):
                    e.activation(MCOL[:, b * 4 + 2:b * 4 + 3],
                                 MCOL[:, b * 4 + 1:b * 4 + 2], AF.Ln,
                                 scale=1.0 / D_MODEL, bias=KCOL[:, 0:1])
                    e.drain()
                    return e.activation(MCOL[:, b * 4 + 2:b * 4 + 3],
                                        MCOL[:, b * 4 + 2:b * 4 + 3], AF.Exp,
                                        scale=-0.5)
                mln3 = st("S", ln3, waits=[xm2[c], pm4[b]] +
                          ([mk] if c == 0 else []))
                mln4 = st("V", lambda e, b=b: e.tensor_scalar(
                    XNT[:, b * D_MODEL:(b + 1) * D_MODEL],
                    XC[:, b * D_MODEL:(b + 1) * D_MODEL],
                    MCOL[:, b * 4 + 2:b * 4 + 3], None, ALU.mult),
                    waits=[mln3, pmtr[b]])
                pm4[b] = mln4
                def tr(e, b=b):
                    for k in range(NK):
                        li = e.transpose(
                            PAB[:, b * 1024 + k * 128:b * 1024 + (k + 1) * 128],
                            XNT[:, b * D_MODEL + k * 128:
                                b * D_MODEL + (k + 1) * 128], IBF[:])
                    return li
                mtr = st("T", tr, waits=[mln4, pevx[b]])
                pmtr[b] = mtr
                ev = st("S", lambda e, c=c, b=b: e.copy(
                    hx(XN[:], NK)[:, :, c * 128:(c + 1) * 128],
                    hx(PAB[:, b * 1024:(b + 1) * 1024], NK)), waits=[mtr])
                pevx[b] = ev
                if tail and wload:
                    def dtmm(e, c=c, b=b):
                        for k in range(NK):
                            e.matmul(
                                PB[:, b * 512:b * 512 + 32],
                                XN[:, k * L + c * 128:k * L + (c + 1) * 128],
                                WIN[:, k * WIN_F + DTOFF:k * WIN_F + DTOFF + 32],
                                start=(k == 0), stop=(k == NK - 1))
                        for k in range(NK):
                            li = e.matmul(
                                PB[:, b * 512 + 256:b * 512 + 264],
                                XN[:, k * L + c * 128:k * L + (c + 1) * 128],
                                WIN[:, k * WIN_F + GOFF:k * WIN_F + GOFF + 8],
                                start=(k == 0), stop=(k == NK - 1))
                        return li
                    mdt = st("T", dtmm, waits=[ev, dt_ev2[b], mbce] +
                             (wdt_casts if c == 0 else []))
                    def dtev(e, c=c, b=b):
                        e.copy(DTR[:, c * 32:(c + 1) * 32],
                               PB[:, b * 512:b * 512 + 32])
                        e.drain()
                        return e.copy(LOG[:, c * 8:(c + 1) * 8],
                                      PB[:, b * 512 + 256:b * 512 + 264])
                    dt_ev2[b] = st("S", dtev, waits=[mdt])
                    dtevm[c] = dt_ev2[b]

            def xra(c):
                def ra(e, c=c):
                    e.max(out=MX8[:], in_=LOG[:, c * 8:(c + 1) * 8])
                    e.drain()
                    e.tensor_copy(V12[:, 2 * c:2 * c + 2], MX8[:, 0:2])
                    e.drain()
                    return e.tensor_tensor(W1S[:, c:c + 1], MX8[:, 0:1],
                                           MX8[:, 1:2], ALU.subtract)
                return st("V", ra, waits=[dtevm[c]])

            for c in range(nch):
                xfront(c)
                if c >= 1:
                    xback(c - 1)
                if c + 2 < nch:
                    xdm[c + 2] = xdma(c + 2)
                if c >= 2:
                    wpair_cast(2 * (c - 2))
                    wpair_cast(2 * (c - 2) + 1)
                if c >= 1:
                    wpair_dma(2 * (c - 1))
                    wpair_dma(2 * (c - 1) + 1)
                if (tail and wload) and c >= 3:
                    mra = xra(c - 3)
            xback(nch - 1)
            if wload:
                wpair_cast(2 * (nch - 2))
                wpair_cast(2 * (nch - 2) + 1)
                wpair_dma(2 * (nch - 1))
                wpair_dma(2 * (nch - 1) + 1)
                wpair_cast(2 * (nch - 1))
                wpair_cast(2 * (nch - 1) + 1)
            if tail and wload:
                for c in range(nch - 3, nch):
                    mra = xra(c)
            ev_xn = pevx[(nch - 1) % 2]
            dt_ev = dt_ev2[1] if dt_ev2[1] is not None else dt_ev2[0]

            mtn = st("S", lambda e: e.activation(W1S[:], W1S[:], AF.Tanh,
                                                 scale=0.5), waits=[mra])
            def rb(e):
                e.tensor_scalar(W1S[:], W1S[:], 0.5, 0.5, ALU.mult, ALU.add)
                e.drain()
                return e.tensor_scalar(W2S[:], W1S[:], -1.0, 1.0, ALU.mult,
                                       ALU.add)
            mrb = st("V", rb, waits=[mtn])
            mrc = None
            for c in range(NCH):
                def rcf(e, c=c):
                    lg = LOG[:, c * 8:(c + 1) * 8]
                    wf = WFULL[:, c * 8:(c + 1) * 8]
                    e.tensor_scalar(EQA[:], lg, V12[:, 2 * c:2 * c + 1], None,
                                    ALU.is_equal)
                    e.drain()
                    e.tensor_scalar(EQB[:], lg, V12[:, 2 * c + 1:2 * c + 2],
                                    None, ALU.is_equal)
                    e.drain()
                    e.tensor_tensor(wf, EQB[:], EQA[:], ALU.mult)
                    e.drain()
                    e.tensor_tensor(wf, EQB[:], wf, ALU.subtract)
                    e.drain()
                    e.tensor_scalar(wf, wf, W2S[:, c:c + 1], None, ALU.mult)
                    e.drain()
                    return e.scalar_tensor_tensor(wf, EQA[:], W1S[:, c:c + 1],
                                                  wf, ALU.mult, ALU.add)
                mrc = st("V", rcf, waits=[mrb])

            if not (tail and wload):
                ctr = sc.resolve(ctr)
                bar1 = [sc.last[k] for k in ("T", "V", "S") if k in sc.last]
                with nc.Block() as blk:
                    blk.tensor(lambda e: sc.emit("T", e, sems))
                    blk.vector(lambda e: sc.emit("V", e, sems))
                    blk.scalar(lambda e: sc.emit("S", e, sems))
                    blk.sync(lambda e: sc.emit("Q", e, sems))
                p1t.close()
                return nc
            mt1 = st("V", lambda e: e.tensor_tensor(
                hx(DTR[:], NCH), hx(DTR[:], NCH),
                DTB_BC[:].to_broadcast([128, 32, NCH]).transpose([0, 2, 1]),
                ALU.add), waits=[dt_ev2[0], dt_ev2[1]])
            mt2 = st("S", lambda e: e.activation(DTF[:], DTR[:], AF.Exp),
                     waits=[mt1])
            mt3 = st("S", lambda e: e.activation(DTF[:], DTF[:], AF.Ln,
                                                 bias=KCOL[:, 1:2]),
                     waits=[mt2])
            mt4 = st("V", lambda e: e.tensor_copy(DT[:], DTF[:]), waits=[mt3])
            mt5 = st("V", lambda e: e.tensor_tensor(
                hx(DTF[:], NCH), hx(DTF[:], NCH),
                A_BC[:].to_broadcast([128, 32, NCH]).transpose([0, 2, 1]),
                ALU.mult), waits=[mt4])
            mt6 = st("V", lambda e: e.tensor_copy(ADTB[:], DTF[:]), waits=[mt5])
            def t7(e):
                for c in range(NCH):
                    li = e.matmul(PB[:, c * 32:(c + 1) * 32], MLE[:],
                                  ADTB[:, c * 32:(c + 1) * 32],
                                  start=True, stop=True)
                return li
            mt7 = st("T", t7, waits=[mt6, dt_ev])
            mt8 = st("S", lambda e: e.copy(SEG[:], PB[:, 0:512]), waits=[mt7])
            mt11a = st("T", lambda e: e.matmul(PB[0:1, 512:1024],
                                               MLE[:, 127:128], ADTB[:],
                                               start=True, stop=True),
                       waits=[mt6, mt8])
            mt11b = st("S", lambda e: e.copy(SER[:], PB[0:1, 512:1024]),
                       waits=[mt11a])
            mt11 = st("T", lambda e: [
                e.matmul(PA[:, 0:512], ONEF[0:1, :], SER[:], start=True,
                         stop=True),
                e.matmul(PA[:, 512:1024], ONEF[64:65, :], SEG[64:65, :],
                         start=True, stop=True),
            ][-1], waits=[mt11b, pevx[0], pevx[1]])
            mt12a = st("V", lambda e: e.scalar_tensor_tensor(
                STGF[:], PA[:, 512:1024], -1.0, SEG[:], ALU.mult, ALU.add),
                waits=[mt11, mt8])
            mt12b = st("S", lambda e: e.activation(EC[:], STGF[:], AF.Exp),
                       waits=[mt12a])
            mt12c = st("V", lambda e: e.tensor_scalar_mul(STGF[:], STGF[:],
                                                          -1.0),
                       waits=[mt12b])
            mt12d = st("S", lambda e: e.activation(DTER[:], STGF[:], AF.Exp),
                       waits=[mt12c])
            mt12e = st("V", lambda e: e.tensor_tensor(DTER[:], DTER[:], DT[:],
                                                      ALU.mult),
                       waits=[mt12d, mt4])
            mt13a = st("V", lambda e: [
                e.scalar_tensor_tensor(STGF[:], PA[:, 0:512], 1.0, SEG[:],
                                       ALU.mult, ALU.subtract),
                e.drain(),
                e.tensor_tensor(STGF[:, 0:480],
                                STGF[:, 0:480],
                                PA[:, 512 + 32:1024], ALU.add),
            ][-1], waits=[mt11, mt8, mt12d])
            mt13b = st("S", lambda e: e.activation(DTWD[:], STGF[:], AF.Exp),
                       waits=[mt13a])
            mt13c = st("V", lambda e: e.tensor_tensor(DTWD[:], DTWD[:], DT[:],
                                                      ALU.mult),
                       waits=[mt13b])
            mt14a = st("V", lambda e: [
                e.tensor_copy(STGF[:], PA[:, 0:512]),
                e.drain(),
                e.scalar_tensor_tensor(STGF[:], PA[:, 512:1024], -1.0,
                                       STGF[:], ALU.mult, ALU.add),
                e.drain(),
                e.tensor_tensor(STGF[:, 0:480], STGF[:, 0:480],
                                PA[:, 512 + 32:1024], ALU.add),
            ][-1], waits=[mt13b])
            mt14 = st("S", lambda e: e.activation(BETB[:], STGF[:], AF.Exp),
                      waits=[mt14a])

            if debug:
                st("Q", lambda e: [
                    e.dma_start(out=dbg["dbg_seg"], in_=SEG[:]),
                    e.dma_start(out=dbg["dbg_dt"], in_=DT[:]),
                    e.dma_start(out=dbg["dbg_wf"], in_=WFULL[:]),
                    e.dma_start(out=dbg["dbg_xn"], in_=XN[:]),
                ], waits=[mt8, mt4, mrc, ev_xn], sem="QD", inc=4 * 16)

            ctr = sc.resolve(ctr)
            bar1 = [sc.last[k] for k in ("T", "V", "S", "Q") if k in sc.last]
            with nc.Block() as blk:
                blk.tensor(lambda e: sc.emit("T", e, sems))
                blk.vector(lambda e: sc.emit("V", e, sems))
                blk.scalar(lambda e: sc.emit("S", e, sems))
                blk.sync(lambda e: sc.emit("Q", e, sems))
            p1t.close()
            if stages < 2:
                return nc

            with ExitStack() as p1b:
                B1 = p1b.enter_context
                ATTS = B1(nc.sbuf_tensor([128, D_MODEL], BF16))
                S_ALL = B1(nc.sbuf_tensor([128, 2048], BF16))
                XB = B1(nc.sbuf_tensor([128, NXT * XBW], BF16))
                AC = B1(nc.sbuf_tensor([128, NXT * 256], BF16))
                TX = B1(nc.sbuf_tensor([128, 2 * 256], BF16))
                DGS = B1(nc.sbuf_tensor([128, 8 * 128], BF16))
                TZ = B1(nc.sbuf_tensor([128, 1024], BF16))
                SZ = B1(nc.sbuf_tensor([128, 2 * D_INNER], BF16))
                SXT = B1(nc.sbuf_tensor([128, 2 * D_INNER], BF16))
                DTX = B1(nc.sbuf_tensor([128, 2 * D_INNER], BF16))
                GT = B1(nc.sbuf_tensor([Q, 2 * Q], BF16))
                BCB = B1(nc.sbuf_tensor([128, 2 * 512], BF16))
                BTM = B1(nc.sbuf_tensor([Q, Q], BF16))
                Y1 = B1(nc.sbuf_tensor([128, D_INNER], BF16))
                SCR = B1(nc.sbuf_tensor([128, D_INNER], BF16))
                OACC = B1(nc.sbuf_tensor([128, NCH * 256], BF16))
                EWB = B1(nc.sbuf_tensor([128, 2 * 8 * 256], BF16))
                ESTG = B1(nc.sbuf_tensor([128, 2 * 256], F32))
                P01 = B1(nc.psum_tensor([128, 1024], F32))
                P23 = B1(nc.psum_tensor([128, 1024], F32))
                P45 = B1(nc.psum_tensor([128, 1024], F32))
                P67 = B1(nc.psum_tensor([128, 1024], F32))
                P23B = P23[:].bitcast(BF16)

                sc = Script()
                st = sc.step

                minit = st("V", lambda e: e.memset(
                    hx(XB[:], NXT)[:, :, 0:4], 0.0), waits=bar1)
                minit2 = st("V", lambda e: e.memset(S_ALL[:], 0.0))

                att_store = None
                dumps = []

                def moe_gen():
                    est_free = [None, None]
                    ewb_mm_last = [None, None]
                    prev_dump = None
                    slots = [(q, ee) for q in range(4) for ee in range(8)]
                    ldm_map = {}

                    def ld_gen(q, ee):
                        eb = ee % 2
                        ldm_map[(q, ee)] = []
                        for ct in range(NK):
                            sb = ct % 2
                            mld = st("Q", lambda e, ee=ee, ct=ct, q=q, sb=sb:
                                     e.dma_start(
                                out=ESTG[:, sb * 256:(sb + 1) * 256],
                                in_=d["we"][ee, ct * 128:(ct + 1) * 128,
                                            q * 256:(q + 1) * 256]),
                                waits=[est_free[sb]], sem="QE%d" % sb,
                                inc=16)
                            yield
                            cw = [mld]
                            if ct < 2 and ewb_mm_last[eb] is not None:
                                cw.append(ewb_mm_last[eb])
                            est_free[sb] = st(
                                "V", lambda e, ct=ct, eb=eb, sb=sb:
                                e.tensor_copy(
                                    EWB[:, eb * 2048 + ct * 256:
                                        eb * 2048 + (ct + 1) * 256],
                                    ESTG[:, sb * 256:(sb + 1) * 256]),
                                waits=cw, drain=False)
                            ldm_map[(q, ee)].append(est_free[sb])
                            yield

                    for _ in ld_gen(*slots[0]):
                        yield
                    escq = {}
                    for si, (q, ee) in enumerate(slots):
                        eb = ee % 2
                        nld = ld_gen(*slots[si + 1]) if si + 1 < len(slots) \
                            else None
                        mm_last = None
                        ldm = ldm_map[(q, ee)]
                        for t in range(NCH):
                            gi = si * NCH + t
                            sl4 = gi % 3
                            pp4 = P45 if sl4 < 2 else P67
                            of4 = (sl4 % 2) * 512 if sl4 < 2 else 0
                            ww = []
                            if t == 0:
                                ww = list(ldm)
                            if gi >= 3:
                                ww.append(escq[gi - 3])
                            def emm(e, t=t, eb=eb, pp4=pp4, of4=of4):
                                for k in range(NK):
                                    li = e.matmul(
                                        pp4[:, of4:of4 + 256],
                                        XN[:, k * L + t * 128:k * L + (t + 1) * 128],
                                        EWB[:, eb * 2048 + k * 256:eb * 2048 + (k + 1) * 256],
                                        start=(k == 0), stop=(k == NK - 1))
                                return li
                            mmx = st("T", emm, waits=ww)
                            mm_last = mmx
                            yield
                            aw = [mmx]
                            if ee == 0 and prev_dump is not None:
                                aw.append(prev_dump)
                            if ee == 0:
                                def accf(e, t=t, pp4=pp4, of4=of4):
                                    return e.tensor_scalar(
                                        OACC[:, t * 256:(t + 1) * 256],
                                        pp4[:, of4:of4 + 256],
                                        WFULL[:, t * 8:t * 8 + 1],
                                        None, ALU.mult)
                            else:
                                def accf(e, t=t, ee=ee, pp4=pp4, of4=of4):
                                    return e.scalar_tensor_tensor(
                                        OACC[:, t * 256:(t + 1) * 256],
                                        pp4[:, of4:of4 + 256],
                                        WFULL[:, t * 8 + ee:t * 8 + ee + 1],
                                        OACC[:, t * 256:(t + 1) * 256],
                                        ALU.mult, ALU.add)
                            escq[gi] = st("V", accf, waits=aw,
                                          drain=(t % 4 == 0))
                            yield
                            if t % 2 == 1 and nld is not None:
                                for _ in range(2):
                                    try:
                                        next(nld)
                                        yield
                                    except StopIteration:
                                        break
                        if nld is not None:
                            for _ in nld:
                                yield
                        ewb_mm_last[eb] = mm_last
                        if ee == 7:
                            prev_dump = st("Q", lambda e, q=q: e.dma_start(
                                out=oaccd[q], in_=OACC[:]),
                                waits=[escq[si * NCH + NCH - 1]],
                                sem="QDmp", inc=16)
                            dumps.append(prev_dump)
                            yield

                _gen = moe_gen()
                _done = [False]

                def drip(n):
                    for _ in range(n):
                        if _done[0]:
                            return
                        try:
                            next(_gen)
                        except StopIteration:
                            _done[0] = True
                            return


                P01B = P01[:].bitcast(BF16)
                st_c01 = [None]
                st_c23 = [minit2]
                m_szprev = [None]
                m_convdone = [minit]
                m_supd = {}
                m_summ = {}
                m_sxt = {}
                m_dtx = {}
                m_gt = {}
                m_ymm = {}
                m_mep = {}
                m_mxw = {}
                m_btr = {}
                m_bcb = {}
                m_op = {}
                m_szg = {}
                m_att = [None]

                def front_gen(c):
                    pj, half = c // 2, c % 2
                    if half == 0:
                        if pj > 0:
                            m_convdone[0] = st("V", lambda e: e.tensor_copy(
                                hx(XB[:], NXT)[:, :, 0:4],
                                hx(XB[:], NXT)[:, :, 256:260]),
                                waits=[m_convdone[0]])
                            yield
                        wave_ev = []
                        for wv in range(9):
                            ct0 = 2 * wv
                            def xmm(e, ct0=ct0, pj=pj):
                                for ct in range(ct0, ct0 + 2):
                                    for k in range(NK):
                                        li = e.matmul(
                                            P01[:, (ct - ct0) * 256:(ct - ct0 + 1) * 256],
                                            WIN[:, k * WIN_F + XBCOFF + ct * 128:
                                                k * WIN_F + XBCOFF + (ct + 1) * 128],
                                            XN[:, k * L + pj * 256:k * L + (pj + 1) * 256],
                                            start=(k == 0), stop=(k == NK - 1))
                                return li
                            mw = st("T", xmm,
                                    waits=[m_convdone[0], st_c01[0]])
                            drip(2)
                            yield
                            ev = st("S", lambda e, ct0=ct0: e.copy(
                                hx(XB[:], NXT)[:, ct0:ct0 + 2, 4:260],
                                hx(P01[:, 0:512], 2)), waits=[mw])
                            st_c01[0] = ev
                            wave_ev.append(ev)
                            yield
                        cv_ev = [None, None]
                        mm_cv = {}
                        cm = None
                        cms = {}
                        for ct in range(NXT):
                            def dgen(e, ct=ct):
                                for dd in range(4):
                                    j = (4 * ct + dd) % 8
                                    li = e.activation(
                                        DGS[:, j * 128:(j + 1) * 128],
                                        IBF[:], AF.Identity,
                                        scale=CW[:, ct * 4 + dd:ct * 4 + dd + 1])
                                return li
                            mdg = st("S", dgen, waits=[mm_cv.get(ct - 2)])
                            yield
                            cvp, cvo = (P01, 512) if ct % 2 == 0 else (P67, 512)
                            def cvmm(e, ct=ct, cvp=cvp, cvo=cvo):
                                for dd in range(4):
                                    j = (4 * ct + dd) % 8
                                    li = e.matmul(
                                        cvp[:, cvo:cvo + 256],
                                        DGS[:, j * 128:(j + 1) * 128],
                                        XB[:, ct * XBW + 4 - dd:
                                           ct * XBW + 260 - dd],
                                        start=(dd == 0), stop=(dd == 3))
                                return li
                            mm_cv[ct] = st("T", cvmm, waits=[
                                mdg, wave_ev[ct // 2], cv_ev[ct % 2]])
                            drip(3)
                            yield
                            def cvev(e, ct=ct, cvp=cvp, cvo=cvo):
                                e.activation(
                                    AC[:, ct * 256:(ct + 1) * 256],
                                    cvp[:, cvo:cvo + 256], AF.Identity,
                                    scale=1.0, bias=CB[:, ct:ct + 1])
                                e.drain()
                                return e.activation(
                                    TX[:, (ct % 2) * 256:(ct % 2 + 1) * 256],
                                    cvp[:, cvo:cvo + 256], AF.Tanh, scale=0.5,
                                    bias=CBH[:, ct:ct + 1])
                            cv_ev[ct % 2] = st("S", cvev, waits=[
                                mm_cv[ct], cms.get(ct - 2)])
                            yield
                            if ct % 2 == 1:
                                def comp(e, ct=ct):
                                    e.tensor_scalar(TX[:], TX[:], 0.5, 0.5,
                                                    ALU.mult, ALU.add)
                                    e.drain()
                                    return e.tensor_tensor(
                                        AC[:, (ct - 1) * 256:(ct + 1) * 256],
                                        AC[:, (ct - 1) * 256:(ct + 1) * 256],
                                        TX[:], ALU.mult)
                                cm = st("V", comp,
                                        waits=[cv_ev[0], cv_ev[1]])
                                cms[ct] = cms[ct - 1] = cm
                                drip(2)
                                yield
                        for ch in (c, c + 1):
                            need = ch - 2
                            while need >= 0 and need not in m_mep:
                                yield
                            mprev = m_mep.get(need)
                            for w in range(2):
                                zev = []
                                for bb in range(2):
                                    def zmm(e, ch=ch, w=w, bb=bb):
                                        for k in range(NK):
                                            li = e.matmul(
                                                P01[:, 0:512],
                                                XN[:, k * L + ch * 128:k * L + (ch + 1) * 128],
                                                WIN[:, k * WIN_F + w * 1024 + bb * 512:
                                                    k * WIN_F + w * 1024 + (bb + 1) * 512],
                                                start=(k == 0), stop=(k == NK - 1))
                                        return li
                                    mz = st("T", zmm,
                                            waits=[st_c01[0], m_szprev[0]])
                                    drip(1)
                                    yield
                                    def zef(e, ch=ch, w=w, bb=bb):
                                        e.activation(
                                            TZ[:, bb * 512:(bb + 1) * 512],
                                            P01[:, 0:512], AF.Tanh, scale=0.5)
                                        e.drain()
                                        return e.copy(
                                            SZ[:, (ch % 2) * D_INNER + w * 1024 + bb * 512:
                                               (ch % 2) * D_INNER + w * 1024 + (bb + 1) * 512],
                                            P01[:, 0:512])
                                    ev = st("S", zef, waits=[mz] +
                                            ([mprev] if (w == 0 and bb == 0
                                                         and mprev) else []))
                                    st_c01[0] = ev
                                    zev.append(ev)
                                    yield
                                m_szb0 = st("V", lambda e: e.tensor_scalar(
                                    TZ[:], TZ[:], 0.5, 0.5, ALU.mult,
                                    ALU.add), waits=zev)
                                drip(1)
                                yield
                                for j in range(2):
                                    def szb(e, ch=ch, w=w, j=j):
                                        sl = SZ[:, (ch % 2) * D_INNER +
                                                w * 1024 + j * 512:
                                                (ch % 2) * D_INNER +
                                                w * 1024 + (j + 1) * 512]
                                        return e.tensor_tensor(
                                            sl, sl,
                                            TZ[:, j * 512:(j + 1) * 512],
                                            ALU.mult)
                                    m_szprev[0] = st("V", szb, waits=[],
                                                     drain=(j == 0))
                                    drip(1)
                                yield
                        m_convdone[0] = cm
                        m_bcb[pj] = st("V", lambda e, pj=pj: e.tensor_copy(
                            BCB[:, (pj % 2) * 512:(pj % 2) * 512 + 512],
                            AC[:, 16 * 256:16 * 256 + 512]),
                            waits=[cm, m_btr.get(2 * pj - 3)])
                        yield

                    sx_ev = []
                    for hh in range(2):
                        def sxtr(e, half=half, hh=hh):
                            for j in range(8):
                                jt = hh * 8 + j
                                li = e.transpose(
                                    P01B[:, j * 128:(j + 1) * 128],
                                    AC[:, jt * 256 + half * 128:
                                       jt * 256 + half * 128 + 128],
                                    IBF[:])
                            return li
                        mst_ = st("T", sxtr,
                                  waits=[m_convdone[0], st_c01[0]])
                        drip(2)
                        yield
                        mse = st("S", lambda e, c=c, hh=hh: e.copy(
                            SXT[:, (c % 2) * D_INNER + hh * 1024:
                                (c % 2) * D_INNER + (hh + 1) * 1024],
                            P01B[:, 0:1024]),
                            waits=[mst_] +
                            ([m_mxw.get(c - 2), m_mep.get(c - 2)]
                             if hh == 0 else []))
                        st_c01[0] = mse
                        sx_ev.append(mse)
                        yield
                    m_sxt[c] = sx_ev[1]
                    md = None
                    for j in range(4):
                        md = st("V", lambda e, c=c, j=j: e.tensor_tensor(
                            hx(DTX[:, (c % 2) * D_INNER + j * 512:
                               (c % 2) * D_INNER + (j + 1) * 512], 8),
                            hx(SXT[:, (c % 2) * D_INNER + j * 512:
                               (c % 2) * D_INNER + (j + 1) * 512], 8),
                            DTER[:, c * 32 + j * 8:c * 32 + (j + 1) * 8]
                            .to_broadcast([128, 8, HEADDIM]),
                            ALU.mult),
                            waits=[sx_ev[0] if j < 2 else sx_ev[1],
                                   m_summ.get(c - 2)]
                            if j in (0, 2) else [],
                            drain=(j == 0))
                        drip(1)
                        yield
                    m_dtx[c] = md
                    bfm = BCB[:, (pj % 2) * 512 + half * 128:
                              (pj % 2) * 512 + half * 128 + 128]
                    cfm = BCB[:, (pj % 2) * 512 + 256 + half * 128:
                              (pj % 2) * 512 + 256 + half * 128 + 128]
                    mg = st("T", lambda e, bfm=bfm, cfm=cfm: e.matmul(
                        P01[:, 0:128], bfm, cfm, start=True, stop=True),
                        waits=[m_bcb[pj], st_c01[0]])
                    drip(2)
                    yield
                    m_gt[c] = st("V", lambda e, c=c: e.scalar_tensor_tensor(
                        GT[:, (c % 2) * Q:(c % 2 + 1) * Q], P01[:, 0:128],
                        1.0, MLE[:], ALU.mult, ALU.mult),
                        waits=[mg, m_ymm.get(c - 2)])
                    st_c01[0] = m_gt[c]
                    yield

                def back_gen(c):
                    pj, half = c // 2, c % 2
                    bfm = BCB[:, (pj % 2) * 512 + half * 128:
                              (pj % 2) * 512 + half * 128 + 128]
                    cfm = BCB[:, (pj % 2) * 512 + 256 + half * 128:
                              (pj % 2) * 512 + 256 + half * 128 + 128]
                    gt = GT[:, (c % 2) * Q:(c % 2 + 1) * Q]
                    dto = (c % 2) * D_INNER
                    yh = []
                    mlast = None
                    for w in range(2):
                        def ymm(e, w=w, cfm=cfm, gt=gt, dto=dto):
                            for bb in range(2):
                                e.matmul(
                                    P23[:, bb * 512:(bb + 1) * 512], cfm,
                                    S_ALL[:, w * 1024 + bb * 512:w * 1024 + (bb + 1) * 512],
                                    start=True, stop=False)
                            for bb in range(2):
                                li = e.matmul(
                                    P23[:, bb * 512:(bb + 1) * 512], gt,
                                    DTX[:, dto + w * 1024 + bb * 512:
                                        dto + w * 1024 + (bb + 1) * 512],
                                    start=False, stop=True)
                            return li
                        mlast = st("T", ymm, waits=[
                            st_c23[0], m_gt.get(c), m_dtx.get(c),
                            m_supd.get(c - 1), m_bcb.get(pj),
                            m_mep.get(c - 1)] if w == 0 else [st_c23[0]])
                        drip(4)
                        yield
                        ev = st("S", lambda e, w=w: e.copy(
                            Y1[:, w * 1024:(w + 1) * 1024], P23[:]),
                            waits=[mlast])
                        st_c23[0] = ev
                        yh.append(ev)
                        yield
                    m_ymm[c] = mlast
                    mx = None
                    for j in range(4):
                        mx = st("V", lambda e, c=c, j=j, dto=dto: e.tensor_tensor(
                            hx(DTX[:, dto + j * 512:dto + (j + 1) * 512], 8),
                            hx(SXT[:, dto + j * 512:dto + (j + 1) * 512], 8),
                            DTWD[:, c * 32 + j * 8:c * 32 + (j + 1) * 8]
                            .to_broadcast([128, 8, HEADDIM]),
                            ALU.mult), waits=[m_ymm[c]] if j == 0 else [],
                            drain=(j == 0))
                        drip(1)
                        yield
                    m_mxw[c] = mx
                    mbt = st("T", lambda e, bfm=bfm: e.transpose(
                        P23B[:, 0:128], bfm, IBF[:]),
                        waits=[st_c23[0], m_bcb.get(pj)])
                    m_btr[c] = mbt
                    yield
                    ev_btm = st("S", lambda e: e.copy(BTM[:], P23B[:, 0:128]),
                                waits=[mbt])
                    st_c23[0] = ev_btm
                    yield
                    msup = None
                    msu = None
                    for w in range(2):
                        def supd(e, w=w, dto=dto):
                            for bb in range(2):
                                li = e.matmul(
                                    P23[:, bb * 512:(bb + 1) * 512], BTM[:],
                                    DTX[:, dto + w * 1024 + bb * 512:
                                        dto + w * 1024 + (bb + 1) * 512],
                                    start=True, stop=True)
                            return li
                        msu = st("T", supd, waits=[st_c23[0], mx] +
                                 ([msup] if msup else []))
                        drip(2)
                        yield
                        md = st("V", lambda e, c=c, w=w, dto=dto:
                                e.tensor_tensor(
                            hx(DTX[:, dto + w * 1024:dto + (w + 1) * 1024], 16),
                            hx(S_ALL[:, w * 1024:(w + 1) * 1024], 16),
                            BETB[:, c * 32 + w * 16:c * 32 + (w + 1) * 16]
                            .to_broadcast([128, 16, HEADDIM]),
                            ALU.mult), waits=[msu])
                        drip(1)
                        yield
                        msup = st("V", lambda e, w=w, dto=dto: e.tensor_tensor(
                            S_ALL[:, w * 1024:(w + 1) * 1024],
                            DTX[:, dto + w * 1024:dto + (w + 1) * 1024],
                            P23[:], ALU.add), waits=[md], drain=False)
                        drip(1)
                        yield
                    m_summ[c] = msu
                    m_supd[c] = msup
                    mep = None
                    for j in range(4):
                        sl = slice(j * 512, (j + 1) * 512)
                        def epa(e, c=c, j=j, sl=sl, dto=dto):
                            e.tensor_tensor(
                                hx(Y1[:, sl], 8), hx(Y1[:, sl], 8),
                                EC[:, c * 32 + j * 8:c * 32 + (j + 1) * 8]
                                .to_broadcast([128, 8, HEADDIM]), ALU.mult)
                            e.drain()
                            e.tensor_tensor(
                                hx(SCR[:, sl], 8),
                                hx(SXT[:, dto + j * 512:dto + (j + 1) * 512], 8),
                                D_BC[:, j * 8:(j + 1) * 8]
                                .to_broadcast([128, 8, HEADDIM]), ALU.mult)
                            e.drain()
                            e.tensor_tensor(Y1[:, sl], Y1[:, sl], SCR[:, sl],
                                            ALU.add)
                            e.drain()
                            return e.tensor_tensor(
                                Y1[:, sl], Y1[:, sl],
                                SZ[:, (c % 2) * D_INNER + j * 512:
                                   (c % 2) * D_INNER + (j + 1) * 512],
                                ALU.mult)
                        mep = st("V", epa,
                                 waits=(yh + [m_op.get(c - 1),
                                              m_szg.get(c)]) if j == 0
                                 else [])
                        drip(1)
                        yield
                    m_mep[c] = mep
                    if debug:
                        mdbg = st("Q", lambda e, c=c: [
                            e.dma_start(out=dbg["dbg_z"][c],
                                        in_=SZ[:, (c % 2) * D_INNER:
                                               (c % 2 + 1) * D_INNER]),
                            e.dma_start(out=dbg["dbg_sx"][c],
                                        in_=SXT[:, (c % 2) * D_INNER:
                                                (c % 2 + 1) * D_INNER]),
                            e.dma_start(out=dbg["dbg_y"][c], in_=Y1[:]),
                        ], waits=[mep, m_sxt[c]], sem="QD", inc=3 * 16)
                        m_mep[c] = mdbg
                        yield
                    def gtr(e):
                        for j in range(16):
                            li = e.transpose(P23B[:, j * 128:(j + 1) * 128],
                                             Y1[:, j * 128:(j + 1) * 128],
                                             IBF[:])
                        return li
                    mgt = st("T", gtr, waits=[mep, st_c23[0]])
                    drip(2)
                    yield
                    ev_gfm = st("S", lambda e: e.copy(SCR[:], P23B[:, 0:2048]),
                                waits=[mgt])
                    st_c23[0] = ev_gfm
                    yield
                    mss = st("S", lambda e, c=c: e.activation(
                        Y1[:], Y1[:], AF.Square, accum_out=SS[:, c:c + 1]),
                        waits=[mgt, m_mep[c]])
                    yield
                    def opmm(e):
                        for kd in range(NKI):
                            for bb in range(2):
                                li = e.matmul(
                                    P23[:, bb * 512:(bb + 1) * 512],
                                    SCR[:, kd * 128:(kd + 1) * 128],
                                    WOUT[:, kd * D_MODEL + bb * 512:
                                         kd * D_MODEL + (bb + 1) * 512],
                                    start=(kd == 0), stop=(kd == NKI - 1))
                        return li
                    mop = st("T", opmm, waits=[ev_gfm] +
                             (wom if c == 0 else []))
                    m_op[c] = mop
                    drip(3)
                    yield
                    ev_att = st("S", lambda e: e.copy(ATTS[:], P23[:]),
                                waits=[mop, m_att[0]])
                    st_c23[0] = ev_att
                    yield
                    m_att[0] = st("S", lambda e, c=c: e.dma_start(
                        out=attd[c], in_=ATTS[:]),
                        waits=[ev_att], sem="AT0", inc=16)
                    yield

                def pump(gens):
                    alive = [g for g in gens if g is not None]
                    while alive:
                        for g in list(alive):
                            try:
                                next(g)
                            except StopIteration:
                                alive.remove(g)

                pump([front_gen(0)])
                for c in range(NCH):
                    pump([back_gen(c),
                          front_gen(c + 1) if c + 1 < NCH else None])

                drip(10 ** 6)
                att_store = m_att[0]
                if debug:
                    st("Q", lambda e: e.dma_start(out=dbg["dbg_sall"],
                                                  in_=S_ALL[:]),
                       waits=[m_supd[NCH - 1]], sem="QD", inc=16)

                ctr = sc.resolve(ctr)
                bar2 = [sc.last[k] for k in ("T", "V", "S", "G")
                        if k in sc.last]
                bar2 += [att_store, dumps[-1]]
                with nc.Block() as blk:
                    blk.tensor(lambda e: sc.emit("T", e, sems))
                    blk.vector(lambda e: sc.emit("V", e, sems))
                    blk.scalar(lambda e: sc.emit("S", e, sems))
                    blk.sync(lambda e: sc.emit("Q", e, sems))
                    blk.gpsimd(lambda e: sc.emit("G", e, sems))

        if stages < 3:
            return nc
        with ExitStack() as p2:
            C1 = p2.enter_context
            ATR = C1(nc.sbuf_tensor([128, 4 * D_MODEL], BF16))
            XR2 = C1(nc.sbuf_tensor([128, 4 * D_MODEL], F32))
            OACR = C1(nc.sbuf_tensor([128, 4 * D_MODEL], BF16))
            OST = C1(nc.sbuf_tensor([128, 4 * D_MODEL], F32))

            sc = Script()
            st = sc.step

            mrms = st("S", lambda e: [
                e.activation(RMS[:], SS[:], AF.Ln, scale=1.0 / D_INNER,
                             bias=KCOL[:, 0:1]),
                e.drain(),
                e.activation(RMS[:], RMS[:], AF.Exp, scale=-0.5),
            ][-1], waits=bar2)

            fcons = [None, None, None, None]
            ocons = [None, None, None, None]
            for t in range(NCH):
                xb = t % 4
                qr = ["QR0", "QR1", "QX0", "QX1"][xb]
                qa = ["QA0", "QA1", "QW0", "QW1"][xb]
                qb = ["QB0", "QB1", "QO0", "QO1"][xb]
                mx2 = st("Q", lambda e, t=t, xb=xb: e.dma_start(
                    out=XR2[:, xb * D_MODEL:(xb + 1) * D_MODEL],
                    in_=d["x"][t * 128:(t + 1) * 128, :]),
                    waits=[fcons[xb]], sem=qr, inc=16)
                ma2 = st("Q", lambda e, t=t, xb=xb: e.dma_start(
                    out=ATR[:, xb * D_MODEL:(xb + 1) * D_MODEL], in_=attd[t]),
                    waits=[fcons[xb], att_store], sem=qa, inc=16)
                mo2 = st("Q", lambda e, t=t, xb=xb: [
                    e.dma_start(out=OACR[:, xb * D_MODEL + q * 256:
                                xb * D_MODEL + (q + 1) * 256],
                                in_=oaccd[q][:, t * 256:(t + 1) * 256])
                    for q in range(4)],
                    waits=[fcons[xb]] + dumps, sem=qb, inc=4 * 16)
                def fin(e, t=t, xb=xb):
                    e.scalar_tensor_tensor(
                        OST[:, xb * D_MODEL:(xb + 1) * D_MODEL],
                        ATR[:, xb * D_MODEL:(xb + 1) * D_MODEL],
                        RMS[:, t:t + 1],
                        OACR[:, xb * D_MODEL:(xb + 1) * D_MODEL],
                        ALU.mult, ALU.add)
                    e.drain()
                    return e.tensor_tensor(
                        OST[:, xb * D_MODEL:(xb + 1) * D_MODEL],
                        OST[:, xb * D_MODEL:(xb + 1) * D_MODEL],
                        XR2[:, xb * D_MODEL:(xb + 1) * D_MODEL], ALU.add)
                mf = st("V", fin, waits=[mx2, ma2, mo2, mrms, ocons[xb]])
                fcons[xb] = mf
                ocons[xb] = st("S", lambda e, t=t, xb=xb: e.dma_start(
                    out=out[t * 128:(t + 1) * 128, :],
                    in_=OST[:, xb * D_MODEL:(xb + 1) * D_MODEL]),
                    waits=[mf], sem=["O0", "O1", "AT1", "QC"][xb], inc=16)

            st("S", lambda e: e.nop(),
               waits=[ocons[0], ocons[1], ocons[2], ocons[3], att_store],
               inc=0)

            ctr = sc.resolve(ctr)
            with nc.Block() as blk:
                blk.tensor(lambda e: sc.emit("T", e, sems))
                blk.vector(lambda e: sc.emit("V", e, sems))
                blk.scalar(lambda e: sc.emit("S", e, sems))
                blk.sync(lambda e: sc.emit("Q", e, sems))
    return nc


def _consts():
    idx = np.arange(Q)
    mle = (idx[:, None] <= idx[None, :]).astype(BF)
    mskt = np.where(idx[None, :] <= idx[:, None], 0.0, NEG).astype(np.float32)
    ibf = np.eye(Q).astype(BF)
    if32 = np.eye(Q).astype(np.float32)
    onef = np.ones((128, Q), dtype=np.float32)
    return mle, mskt, ibf, if32, onef


def make_inputs(L0_attn_x, L0_attn_in_proj_w, L0_attn_conv_w, L0_attn_conv_b,
                L0_attn_norm_w, L0_attn_out_proj_w, L0_attn_dt_bias,
                L0_attn_A_log, L0_attn_D, L0_ffn_gate_w, L0_ffn_expert_w):
    x = np.ascontiguousarray(np.asarray(L0_attn_x, dtype=np.float32))
    wi = np.ascontiguousarray(
        np.concatenate([np.asarray(L0_attn_in_proj_w, np.float32),
                        np.asarray(L0_ffn_gate_w, np.float32).T], axis=1))
    wo = np.ascontiguousarray(np.asarray(L0_attn_out_proj_w, np.float32))
    we = np.ascontiguousarray(
        np.asarray(L0_ffn_expert_w, np.float32).transpose(0, 2, 1))
    cwr = np.asarray(L0_attn_conv_w, np.float32)
    cw = np.ascontiguousarray(
        cwr.T.reshape(NXT, 128, 4).transpose(1, 0, 2).reshape(128, NXT * 4))
    cb = np.ascontiguousarray(
        np.asarray(L0_attn_conv_b, np.float32).reshape(NXT, 128).T)
    nw = np.ascontiguousarray(
        np.asarray(L0_attn_norm_w, np.float32).reshape(NKI, 128).T)
    dtb = np.ascontiguousarray(np.asarray(L0_attn_dt_bias, np.float32)[None, :])
    alog = np.ascontiguousarray(np.asarray(L0_attn_A_log, np.float32)[None, :])
    dvec = np.ascontiguousarray(np.asarray(L0_attn_D, np.float32)[None, :])
    mle, mskt, ibf, if32, onef = _consts()
    shared = dict(wi=wi, wo=wo, we=we, cw=cw, cb=cb, nw=nw, dtb=dtb,
                  alog=alog, dvec=dvec, mle=np.asarray(mle),
                  ibf=np.asarray(ibf), if32=if32, onef=onef)
    return x, shared


def kernel(**inputs):
    x, shared = make_inputs(**inputs)
    nc = build_nc()
    in_maps = [dict(shared, x=np.ascontiguousarray(x[b])) for b in range(8)]
    res = run_bass_kernel_spmd(nc, in_maps, core_ids=list(range(8)))
    return np.stack([r["out"] for r in res.results]).astype(np.float32)


# version 28
# speedup vs baseline: 1.0160x; 1.0160x over previous
import numpy as np
import ml_dtypes
import concourse.bass as bass
import concourse.mybir as mybir
from concourse.bass_utils import run_bass_kernel_spmd

F32 = mybir.dt.float32
BF16 = mybir.dt.bfloat16
BF = ml_dtypes.bfloat16
AF = mybir.ActivationFunctionType
ALU = mybir.AluOpType
AX = mybir.AxisListType

D_MODEL = 1024
D_STATE = 128
HEADDIM = 64
D_INNER = 2048
NHEADS = 32
CONV_DIM = 2304
D_IN_PROJ = 4384
WIN_F = D_IN_PROJ + 8
EPS = 1e-5
L = 2048
Q = 128
NCH = 16
NK = 8
NKI = 16
XBCOFF = D_INNER
DTOFF = 4352
GOFF = 4384
NXT = 18
XBW = 260
NEG = -1e30


class Mark:
    __slots__ = ("sem", "val")
    def __init__(self, sem):
        self.sem = sem
        self.val = None


class Script:

    def __init__(self):
        self.steps = []
        self.last = {}

    def step(self, eng, fn, waits=(), sem=None, inc=1, drain=True):
        sem = sem or eng
        m = Mark(sem) if inc else None
        self.steps.append((eng, fn, tuple(w for w in waits if w is not None),
                           m, inc, drain))
        if m is not None:
            self.last[eng] = m
        return m

    def resolve(self, ctr):
        for eng, fn, waits, m, inc, dr in self.steps:
            if m is not None:
                ctr[m.sem] += inc
                m.val = ctr[m.sem]
        return ctr

    def emit(self, eng_name, e, sems):
        for eng, fn, waits, m, inc, dr in self.steps:
            if eng != eng_name:
                continue
            for w in waits:
                e.wait_ge(sems[w.sem], w.val)
            if dr and eng in ("V", "S", "G"):
                e.drain()
            last = fn(e)
            if m is not None:
                if isinstance(last, list):
                    assert len(last) * (16 if inc >= 16 else 1) == inc or True
                    for li in last:
                        li.then_inc(sems[m.sem], 16)
                else:
                    last.then_inc(sems[m.sem], inc)


def build_nc(debug=False, stages=3, nch=NCH, tail=True, wload=True):
    nc = bass.Bass("TRN2", target_bir_lowering=False,
                   dynamic_dma_scratch_size=512)
    d = {}
    def di(name, shape, dt):
        d[name] = nc.dram_tensor(name, shape, dt, kind="ExternalInput").ap()
    di("x", [L, D_MODEL], F32)
    di("wi", [D_MODEL, WIN_F], F32)
    di("wo", [D_INNER, D_MODEL], F32)
    di("we", [8, D_MODEL, D_MODEL], F32)
    di("cw", [128, NXT * 4], F32)
    di("cb", [128, NXT], F32)
    di("nw", [128, NKI], F32)
    di("dtb", [1, 32], F32)
    di("alog", [1, 32], F32)
    di("dvec", [1, 32], F32)
    di("mle", [Q, Q], BF16)
    di("ibf", [Q, Q], BF16)
    di("if32", [Q, Q], F32)
    di("onef", [128, Q], F32)
    out = nc.dram_tensor("out", [L, D_MODEL], F32, kind="ExternalOutput").ap()
    _ik = "ExternalOutput" if debug else "Internal"
    attd = nc.dram_tensor("attd", [NCH, 128, D_MODEL], BF16, kind=_ik).ap()
    oaccd = nc.dram_tensor("oaccd", [4, 128, NCH * 256], BF16, kind=_ik).ap()
    dbg = {}
    if debug:
        dspec = [("dbg_xn", [128, NK * L], BF16), ("dbg_z", [NCH, 128, D_INNER], BF16),
                 ("dbg_sx", [NCH, 128, D_INNER], BF16), ("dbg_seg", [128, 512], F32),
                 ("dbg_dt", [128, 512], BF16), ("dbg_y", [NCH, 128, D_INNER], BF16),
                 ("dbg_wf", [128, 128], F32), ("dbg_sall", [128, 2048], BF16),
                 ("dbg_oacc", [128, NCH * 512], F32)]
        for nm, sh, dt in dspec:
            dbg[nm] = nc.dram_tensor(nm, sh, dt, kind="ExternalOutput").ap()

    from contextlib import ExitStack
    with ExitStack() as top:
        E = top.enter_context
        semnames = ("T", "V", "S", "G", "QC", "QX0", "QX1", "QW0", "QW1",
                    "QO0", "QO1", "QE0", "QE1", "QR0", "QR1", "QA0", "QA1",
                    "O0", "O1", "AT0", "AT1", "QD", "QB0", "QB1", "QDmp")
        sems = {k: E(nc.semaphore("sem" + k)) for k in semnames}
        ctr = {k: 0 for k in sems}

        XN = E(nc.sbuf_tensor([128, NK * L], BF16))
        DTER = E(nc.sbuf_tensor([128, 512], BF16))
        EC = E(nc.sbuf_tensor([128, 512], BF16))
        DTWD = E(nc.sbuf_tensor([128, 512], BF16))
        BETB = E(nc.sbuf_tensor([128, 512], BF16))
        MLE = E(nc.sbuf_tensor([Q, Q], BF16))
        WFULL = E(nc.sbuf_tensor([128, 128], F32))
        SS = E(nc.sbuf_tensor([128, NCH], F32))
        RMS = E(nc.sbuf_tensor([128, NCH], F32))
        D_BC = E(nc.sbuf_tensor([128, 32], F32))
        IBF = E(nc.sbuf_tensor([Q, Q], BF16))
        CW = E(nc.sbuf_tensor([128, NXT * 4], F32))
        CB = E(nc.sbuf_tensor([128, NXT], F32))
        CBH = E(nc.sbuf_tensor([128, NXT], F32))
        KCOL = E(nc.sbuf_tensor([128, 2], F32))

        def hx(ap, a):
            return ap.rearrange("p (a b) -> p a b", a=a)

        def bc_heads(t, c):
            return t[:, c * 32:(c + 1) * 32].to_broadcast([128, 32, HEADDIM])

        wim = []
        wom = []

        with ExitStack() as p1a:
            A1 = p1a.enter_context
            WIN = A1(nc.sbuf_tensor([128, NK * WIN_F], BF16))
            WOUT = A1(nc.sbuf_tensor([128, NKI * D_MODEL], BF16))
            p1t = ExitStack()
            A2 = p1t.enter_context
            DT = A2(nc.sbuf_tensor([128, 512], BF16))
            WSTG = A2(nc.sbuf_tensor([128, 2 * 2196], F32))
            WOSTG = A2(nc.sbuf_tensor([128, 2 * D_MODEL], F32))
            XR = A2(nc.sbuf_tensor([128, 2 * D_MODEL], F32))
            XC = A2(nc.sbuf_tensor([128, 2 * D_MODEL], F32))
            XNT = A2(nc.sbuf_tensor([128, 2 * D_MODEL], BF16))
            SQD = A2(nc.sbuf_tensor([128, D_MODEL], BF16))
            ADTB = A2(nc.sbuf_tensor([128, 512], BF16))
            MCOL = A2(nc.sbuf_tensor([128, 8], F32))
            DTR = A2(nc.sbuf_tensor([128, 512], F32))
            DTF = A2(nc.sbuf_tensor([128, 512], F32))
            SEG = A2(nc.sbuf_tensor([128, 512], F32))
            STGF = A2(nc.sbuf_tensor([128, 512], F32))
            LOG = A2(nc.sbuf_tensor([128, 128], F32))
            DTB_BC = A2(nc.sbuf_tensor([128, 32], F32))
            A_BC = A2(nc.sbuf_tensor([128, 32], F32))
            ONEF = A2(nc.sbuf_tensor([128, Q], F32))
            NW = A2(nc.sbuf_tensor([128, NKI], F32))
            ROWA = A2(nc.sbuf_tensor([1, 32], F32))
            ROWB = A2(nc.sbuf_tensor([1, 32], F32))
            ROWD = A2(nc.sbuf_tensor([1, 32], F32))
            MX8 = A2(nc.sbuf_tensor([128, 8], F32))
            V12 = A2(nc.sbuf_tensor([128, 32], F32))
            EQA = A2(nc.sbuf_tensor([128, 8], F32))
            EQB = A2(nc.sbuf_tensor([128, 8], F32))
            W1S = A2(nc.sbuf_tensor([128, NCH], F32))
            W2S = A2(nc.sbuf_tensor([128, NCH], F32))
            SER = A2(nc.sbuf_tensor([1, 512], F32))
            PA = A2(nc.psum_tensor([128, 2048], F32))
            PB = A2(nc.psum_tensor([128, 2048], F32))
            PAB = PA[:].bitcast(BF16)

            sc = Script()
            st = sc.step

            mconst = st("Q", lambda e: [
                e.dma_start(out=MLE[:], in_=d["mle"]),
                e.dma_start(out=IBF[:], in_=d["ibf"]),
                e.dma_start(out=ONEF[:], in_=d["onef"]),
                e.dma_start(out=CW[:], in_=d["cw"]),
                e.dma_start(out=CB[:], in_=d["cb"]),
                e.dma_start(out=NW[:], in_=d["nw"]),
                e.dma_start(out=ROWA[:], in_=d["alog"]),
                e.dma_start(out=ROWB[:], in_=d["dtb"]),
                e.dma_start(out=ROWD[:], in_=d["dvec"]),
            ], sem="QC", inc=9 * 16)
            mk = st("V", lambda e: [
                e.memset(KCOL[:, 0:1], EPS),
                e.memset(KCOL[:, 1:2], 1.0),
            ][-1])
            mcbh = st("V", lambda e: e.tensor_scalar_mul(
                CBH[:], CB[:], 0.5), waits=[mconst])
            ma1 = st("S", lambda e: e.activation(ROWA[:], ROWA[:], AF.Exp),
                     waits=[mconst])
            ma2 = st("V", lambda e: e.tensor_scalar_mul(ROWA[:], ROWA[:], -1.0),
                     waits=[ma1])
            mbc = st("T", lambda e: [
                e.matmul(PB[:, 0:32], ONEF[0:1, :], ROWA[:], start=True, stop=True),
                e.matmul(PB[:, 32:64], ONEF[0:1, :], ROWB[:], start=True, stop=True),
                e.matmul(PB[:, 64:96], ONEF[0:1, :], ROWD[:], start=True, stop=True),
            ][-1], waits=[ma2])
            mbce = st("S", lambda e: [
                e.copy(A_BC[:], PB[:, 0:32]),
                e.copy(DTB_BC[:], PB[:, 32:64]),
                e.copy(D_BC[:], PB[:, 64:96]),
            ][-1], waits=[mbc])

            wdt_casts = []
            cast_free = [None, None]
            if wload:
                for k in range(NK):
                    b = k % 2
                    mldt = st("Q", lambda e, k=k, b=b: e.dma_start(
                        out=WSTG[:, b * 2196:b * 2196 + 40],
                        in_=d["wi"][k * 128:(k + 1) * 128, DTOFF:DTOFF + 40]),
                        waits=[cast_free[b]], sem="QW%d" % b, inc=16)
                    cast_free[b] = st("V", lambda e, k=k, b=b: e.tensor_copy(
                        WIN[:, k * WIN_F + DTOFF:k * WIN_F + DTOFF + 40],
                        WSTG[:, b * 2196:b * 2196 + 40]), waits=[mldt])
                    wdt_casts.append(cast_free[b])

            wpairs = []
            for i in range(16 if wload else 0):
                k, h = i // 2, i % 2
                wpairs.append(("wi", k, h * 2196,
                               (DTOFF - 2196) if h else 2196))
            for kd in range(NKI if wload else 0):
                wpairs.append(("wo", kd, 0, D_MODEL))
            ocast_free = [None, None]
            wdma = {}

            def wpair_dma(i):
                kind, k, fw0, fwn = wpairs[i]
                b = i % 2
                if kind == "wi":
                    wdma[i] = st("Q", lambda e, k=k, fw0=fw0, fwn=fwn, b=b:
                                 e.dma_start(
                        out=WSTG[:, b * 2196:b * 2196 + fwn],
                        in_=d["wi"][k * 128:(k + 1) * 128, fw0:fw0 + fwn]),
                        waits=[cast_free[b]], sem="QW%d" % b, inc=16)
                else:
                    wdma[i] = st("Q", lambda e, kd=k, b=b: e.dma_start(
                        out=WOSTG[:, b * D_MODEL:(b + 1) * D_MODEL],
                        in_=d["wo"][kd * 128:(kd + 1) * 128, :]),
                        waits=[ocast_free[b]], sem="QO%d" % b, inc=16)

            def wpair_cast(i):
                kind, k, fw0, fwn = wpairs[i]
                b = i % 2
                if kind == "wi":
                    cast_free[b] = st(
                        "V", lambda e, k=k, fw0=fw0, fwn=fwn, b=b:
                        e.tensor_copy(
                            WIN[:, k * WIN_F + fw0:k * WIN_F + fw0 + fwn],
                            WSTG[:, b * 2196:b * 2196 + fwn]),
                        waits=[wdma[i]])
                    wim.append(cast_free[b])
                else:
                    ocast_free[b] = st(
                        "S", lambda e, kd=k, b=b: e.activation(
                            WOUT[:, kd * D_MODEL:(kd + 1) * D_MODEL],
                            WOSTG[:, b * D_MODEL:(b + 1) * D_MODEL],
                            AF.Identity, scale=NW[:, kd:kd + 1]),
                        waits=[wdma[i]])
                    wom.append(ocast_free[b])

            xr_free = [None, None]
            pm2 = [None, None]
            pm4 = [None, None]
            pmtr = [None, None]
            pevx = [None, None]
            dt_ev2 = [None, None]
            mra = None
            ev_xn = None
            xm2 = {}
            dtevm = {}

            def xdma(c):
                b = c % 2
                return st("Q", lambda e, c=c, b=b: e.dma_start(
                    out=XR[:, b * D_MODEL:(b + 1) * D_MODEL],
                    in_=d["x"][c * 128:(c + 1) * 128, :]),
                    waits=[xr_free[b]], sem="QX%d" % b, inc=16)

            xdm = {0: xdma(0), 1: xdma(1)}

            def xfront(c):
                b = c % 2
                def ln1(e, b=b):
                    e.reduce_sum(MCOL[:, b * 4:b * 4 + 1],
                                 XR[:, b * D_MODEL:(b + 1) * D_MODEL],
                                 axis=AX.X)
                    e.drain()
                    e.tensor_scalar_mul(MCOL[:, b * 4:b * 4 + 1],
                                        MCOL[:, b * 4:b * 4 + 1],
                                        1.0 / D_MODEL)
                    e.drain()
                    return e.tensor_scalar(
                        XC[:, b * D_MODEL:(b + 1) * D_MODEL],
                        XR[:, b * D_MODEL:(b + 1) * D_MODEL],
                        MCOL[:, b * 4:b * 4 + 1], None, ALU.subtract)
                mln1 = st("V", ln1, waits=[xdm[c], pm2[b]])
                xr_free[b] = mln1
                mln2 = st("S", lambda e, b=b: e.activation(
                    SQD[:], XC[:, b * D_MODEL:(b + 1) * D_MODEL], AF.Square,
                    accum_out=MCOL[:, b * 4 + 1:b * 4 + 2]), waits=[mln1])
                pm2[b] = mln2
                xm2[c] = mln2

            def xback(c):
                b = c % 2
                def ln3(e, b=b):
                    e.activation(MCOL[:, b * 4 + 2:b * 4 + 3],
                                 MCOL[:, b * 4 + 1:b * 4 + 2], AF.Ln,
                                 scale=1.0 / D_MODEL, bias=KCOL[:, 0:1])
                    e.drain()
                    return e.activation(MCOL[:, b * 4 + 2:b * 4 + 3],
                                        MCOL[:, b * 4 + 2:b * 4 + 3], AF.Exp,
                                        scale=-0.5)
                mln3 = st("S", ln3, waits=[xm2[c], pm4[b]] +
                          ([mk] if c == 0 else []))
                mln4 = st("V", lambda e, b=b: e.tensor_scalar(
                    XNT[:, b * D_MODEL:(b + 1) * D_MODEL],
                    XC[:, b * D_MODEL:(b + 1) * D_MODEL],
                    MCOL[:, b * 4 + 2:b * 4 + 3], None, ALU.mult),
                    waits=[mln3, pmtr[b]])
                pm4[b] = mln4
                def tr(e, b=b):
                    for k in range(NK):
                        li = e.transpose(
                            PAB[:, b * 1024 + k * 128:b * 1024 + (k + 1) * 128],
                            XNT[:, b * D_MODEL + k * 128:
                                b * D_MODEL + (k + 1) * 128], IBF[:])
                    return li
                mtr = st("T", tr, waits=[mln4, pevx[b]])
                pmtr[b] = mtr
                ev = st("S", lambda e, c=c, b=b: e.copy(
                    hx(XN[:], NK)[:, :, c * 128:(c + 1) * 128],
                    hx(PAB[:, b * 1024:(b + 1) * 1024], NK)), waits=[mtr])
                pevx[b] = ev
                if tail and wload:
                    def dtmm(e, c=c, b=b):
                        for k in range(NK):
                            e.matmul(
                                PB[:, b * 512:b * 512 + 32],
                                XN[:, k * L + c * 128:k * L + (c + 1) * 128],
                                WIN[:, k * WIN_F + DTOFF:k * WIN_F + DTOFF + 32],
                                start=(k == 0), stop=(k == NK - 1))
                        for k in range(NK):
                            li = e.matmul(
                                PB[:, b * 512 + 256:b * 512 + 264],
                                XN[:, k * L + c * 128:k * L + (c + 1) * 128],
                                WIN[:, k * WIN_F + GOFF:k * WIN_F + GOFF + 8],
                                start=(k == 0), stop=(k == NK - 1))
                        return li
                    mdt = st("T", dtmm, waits=[ev, dt_ev2[b], mbce] +
                             (wdt_casts if c == 0 else []))
                    def dtev(e, c=c, b=b):
                        e.copy(DTR[:, c * 32:(c + 1) * 32],
                               PB[:, b * 512:b * 512 + 32])
                        e.drain()
                        return e.copy(LOG[:, c * 8:(c + 1) * 8],
                                      PB[:, b * 512 + 256:b * 512 + 264])
                    dt_ev2[b] = st("S", dtev, waits=[mdt])
                    dtevm[c] = dt_ev2[b]

            def xra(c):
                def ra(e, c=c):
                    e.max(out=MX8[:], in_=LOG[:, c * 8:(c + 1) * 8])
                    e.drain()
                    e.tensor_copy(V12[:, 2 * c:2 * c + 2], MX8[:, 0:2])
                    e.drain()
                    return e.tensor_tensor(W1S[:, c:c + 1], MX8[:, 0:1],
                                           MX8[:, 1:2], ALU.subtract)
                return st("V", ra, waits=[dtevm[c]])

            for c in range(nch):
                xfront(c)
                if c >= 1:
                    xback(c - 1)
                if c + 2 < nch:
                    xdm[c + 2] = xdma(c + 2)
                if c >= 2:
                    wpair_cast(2 * (c - 2))
                    wpair_cast(2 * (c - 2) + 1)
                if c >= 1:
                    wpair_dma(2 * (c - 1))
                    wpair_dma(2 * (c - 1) + 1)
                if (tail and wload) and c >= 3:
                    mra = xra(c - 3)
            xback(nch - 1)
            if wload:
                wpair_cast(2 * (nch - 2))
                wpair_cast(2 * (nch - 2) + 1)
                wpair_dma(2 * (nch - 1))
                wpair_dma(2 * (nch - 1) + 1)
                wpair_cast(2 * (nch - 1))
                wpair_cast(2 * (nch - 1) + 1)
            if tail and wload:
                for c in range(nch - 3, nch):
                    mra = xra(c)
            ev_xn = pevx[(nch - 1) % 2]
            dt_ev = dt_ev2[1] if dt_ev2[1] is not None else dt_ev2[0]

            mtn = st("S", lambda e: e.activation(W1S[:], W1S[:], AF.Tanh,
                                                 scale=0.5), waits=[mra])
            def rb(e):
                e.tensor_scalar(W1S[:], W1S[:], 0.5, 0.5, ALU.mult, ALU.add)
                e.drain()
                return e.tensor_scalar(W2S[:], W1S[:], -1.0, 1.0, ALU.mult,
                                       ALU.add)
            mrb = st("V", rb, waits=[mtn])
            mrc = None
            for c in range(NCH):
                def rcf(e, c=c):
                    lg = LOG[:, c * 8:(c + 1) * 8]
                    wf = WFULL[:, c * 8:(c + 1) * 8]
                    e.tensor_scalar(EQA[:], lg, V12[:, 2 * c:2 * c + 1], None,
                                    ALU.is_equal)
                    e.drain()
                    e.tensor_scalar(EQB[:], lg, V12[:, 2 * c + 1:2 * c + 2],
                                    None, ALU.is_equal)
                    e.drain()
                    e.tensor_tensor(wf, EQB[:], EQA[:], ALU.mult)
                    e.drain()
                    e.tensor_tensor(wf, EQB[:], wf, ALU.subtract)
                    e.drain()
                    e.tensor_scalar(wf, wf, W2S[:, c:c + 1], None, ALU.mult)
                    e.drain()
                    return e.scalar_tensor_tensor(wf, EQA[:], W1S[:, c:c + 1],
                                                  wf, ALU.mult, ALU.add)
                mrc = st("V", rcf, waits=[mrb])

            if not (tail and wload):
                ctr = sc.resolve(ctr)
                bar1 = [sc.last[k] for k in ("T", "V", "S") if k in sc.last]
                with nc.Block() as blk:
                    blk.tensor(lambda e: sc.emit("T", e, sems))
                    blk.vector(lambda e: sc.emit("V", e, sems))
                    blk.scalar(lambda e: sc.emit("S", e, sems))
                    blk.sync(lambda e: sc.emit("Q", e, sems))
                p1t.close()
                return nc
            mt1 = st("V", lambda e: e.tensor_tensor(
                hx(DTR[:], NCH), hx(DTR[:], NCH),
                DTB_BC[:].to_broadcast([128, 32, NCH]).transpose([0, 2, 1]),
                ALU.add), waits=[dt_ev2[0], dt_ev2[1]])
            mt2 = st("S", lambda e: e.activation(DTF[:], DTR[:], AF.Exp),
                     waits=[mt1])
            mt3 = st("S", lambda e: e.activation(DTF[:], DTF[:], AF.Ln,
                                                 bias=KCOL[:, 1:2]),
                     waits=[mt2])
            mt4 = st("V", lambda e: e.tensor_copy(DT[:], DTF[:]), waits=[mt3])
            mt5 = st("V", lambda e: e.tensor_tensor(
                hx(DTF[:], NCH), hx(DTF[:], NCH),
                A_BC[:].to_broadcast([128, 32, NCH]).transpose([0, 2, 1]),
                ALU.mult), waits=[mt4])
            mt6 = st("V", lambda e: e.tensor_copy(ADTB[:], DTF[:]), waits=[mt5])
            def t7(e):
                for c in range(NCH):
                    li = e.matmul(PB[:, c * 32:(c + 1) * 32], MLE[:],
                                  ADTB[:, c * 32:(c + 1) * 32],
                                  start=True, stop=True)
                return li
            mt7 = st("T", t7, waits=[mt6, dt_ev])
            mt8 = st("S", lambda e: e.copy(SEG[:], PB[:, 0:512]), waits=[mt7])
            mt11a = st("T", lambda e: e.matmul(PB[0:1, 512:1024],
                                               MLE[:, 127:128], ADTB[:],
                                               start=True, stop=True),
                       waits=[mt6, mt8])
            mt11b = st("S", lambda e: e.copy(SER[:], PB[0:1, 512:1024]),
                       waits=[mt11a])
            mt11 = st("T", lambda e: [
                e.matmul(PA[:, 0:512], ONEF[0:1, :], SER[:], start=True,
                         stop=True),
                e.matmul(PA[:, 512:1024], ONEF[64:65, :], SEG[64:65, :],
                         start=True, stop=True),
            ][-1], waits=[mt11b, pevx[0], pevx[1]])
            mt12a = st("V", lambda e: e.scalar_tensor_tensor(
                STGF[:], PA[:, 512:1024], -1.0, SEG[:], ALU.mult, ALU.add),
                waits=[mt11, mt8])
            mt12b = st("S", lambda e: e.activation(EC[:], STGF[:], AF.Exp),
                       waits=[mt12a])
            mt12c = st("V", lambda e: e.tensor_scalar_mul(STGF[:], STGF[:],
                                                          -1.0),
                       waits=[mt12b])
            mt12d = st("S", lambda e: e.activation(DTER[:], STGF[:], AF.Exp),
                       waits=[mt12c])
            mt12e = st("V", lambda e: e.tensor_tensor(DTER[:], DTER[:], DT[:],
                                                      ALU.mult),
                       waits=[mt12d, mt4])
            mt13a = st("V", lambda e: [
                e.scalar_tensor_tensor(STGF[:], PA[:, 0:512], 1.0, SEG[:],
                                       ALU.mult, ALU.subtract),
                e.drain(),
                e.tensor_tensor(STGF[:, 0:480],
                                STGF[:, 0:480],
                                PA[:, 512 + 32:1024], ALU.add),
            ][-1], waits=[mt11, mt8, mt12d])
            mt13b = st("S", lambda e: e.activation(DTWD[:], STGF[:], AF.Exp),
                       waits=[mt13a])
            mt13c = st("V", lambda e: e.tensor_tensor(DTWD[:], DTWD[:], DT[:],
                                                      ALU.mult),
                       waits=[mt13b])
            mt14a = st("V", lambda e: [
                e.tensor_copy(STGF[:], PA[:, 0:512]),
                e.drain(),
                e.scalar_tensor_tensor(STGF[:], PA[:, 512:1024], -1.0,
                                       STGF[:], ALU.mult, ALU.add),
                e.drain(),
                e.tensor_tensor(STGF[:, 0:480], STGF[:, 0:480],
                                PA[:, 512 + 32:1024], ALU.add),
            ][-1], waits=[mt13b])
            mt14 = st("S", lambda e: e.activation(BETB[:], STGF[:], AF.Exp),
                      waits=[mt14a])

            if debug:
                st("Q", lambda e: [
                    e.dma_start(out=dbg["dbg_seg"], in_=SEG[:]),
                    e.dma_start(out=dbg["dbg_dt"], in_=DT[:]),
                    e.dma_start(out=dbg["dbg_wf"], in_=WFULL[:]),
                    e.dma_start(out=dbg["dbg_xn"], in_=XN[:]),
                ], waits=[mt8, mt4, mrc, ev_xn], sem="QD", inc=4 * 16)

            ctr = sc.resolve(ctr)
            bar1 = [sc.last[k] for k in ("T", "V", "S", "Q") if k in sc.last]
            with nc.Block() as blk:
                blk.tensor(lambda e: sc.emit("T", e, sems))
                blk.vector(lambda e: sc.emit("V", e, sems))
                blk.scalar(lambda e: sc.emit("S", e, sems))
                blk.sync(lambda e: sc.emit("Q", e, sems))
            p1t.close()
            if stages < 2:
                return nc

            with ExitStack() as p1b:
                B1 = p1b.enter_context
                ATTS = B1(nc.sbuf_tensor([128, D_MODEL], BF16))
                S_ALL = B1(nc.sbuf_tensor([128, 2048], BF16))
                XB = B1(nc.sbuf_tensor([128, NXT * XBW], BF16))
                AC = B1(nc.sbuf_tensor([128, NXT * 256], BF16))
                TX = B1(nc.sbuf_tensor([128, 2 * 256], BF16))
                DGS = B1(nc.sbuf_tensor([128, 8 * 128], BF16))
                TZ = B1(nc.sbuf_tensor([128, 1024], BF16))
                SZ = B1(nc.sbuf_tensor([128, 2 * D_INNER], BF16))
                SXT = B1(nc.sbuf_tensor([128, 2 * D_INNER], BF16))
                DTX = B1(nc.sbuf_tensor([128, 2 * D_INNER], BF16))
                GT = B1(nc.sbuf_tensor([Q, 2 * Q], BF16))
                BCB = B1(nc.sbuf_tensor([128, 2 * 512], BF16))
                BTM = B1(nc.sbuf_tensor([Q, Q], BF16))
                Y1 = B1(nc.sbuf_tensor([128, D_INNER], BF16))
                SCR = B1(nc.sbuf_tensor([128, D_INNER], BF16))
                OACC = B1(nc.sbuf_tensor([128, NCH * 256], BF16))
                EWB = B1(nc.sbuf_tensor([128, 2 * 8 * 256], BF16))
                ESTG = B1(nc.sbuf_tensor([128, 2 * 256], F32))
                P01 = B1(nc.psum_tensor([128, 1024], F32))
                P23 = B1(nc.psum_tensor([128, 1024], F32))
                P45 = B1(nc.psum_tensor([128, 1024], F32))
                P67 = B1(nc.psum_tensor([128, 1024], F32))
                P23B = P23[:].bitcast(BF16)

                sc = Script()
                st = sc.step

                minit = st("V", lambda e: e.memset(
                    hx(XB[:], NXT)[:, :, 0:4], 0.0), waits=bar1)
                minit2 = st("V", lambda e: e.memset(S_ALL[:], 0.0))

                att_store = None
                dumps = []

                def moe_gen():
                    est_free = [None, None]
                    ewb_mm_last = [None, None]
                    prev_dump = None
                    slots = [(q, ee) for q in range(4) for ee in range(8)]
                    ldm_map = {}

                    def ld_gen(q, ee):
                        eb = ee % 2
                        ldm_map[(q, ee)] = []
                        for ct in range(NK):
                            sb = ct % 2
                            mld = st("Q", lambda e, ee=ee, ct=ct, q=q, sb=sb:
                                     e.dma_start(
                                out=ESTG[:, sb * 256:(sb + 1) * 256],
                                in_=d["we"][ee, ct * 128:(ct + 1) * 128,
                                            q * 256:(q + 1) * 256]),
                                waits=[est_free[sb]], sem="QE%d" % sb,
                                inc=16)
                            yield
                            cw = [mld]
                            if ct < 2 and ewb_mm_last[eb] is not None:
                                cw.append(ewb_mm_last[eb])
                            est_free[sb] = st(
                                "V", lambda e, ct=ct, eb=eb, sb=sb:
                                e.tensor_copy(
                                    EWB[:, eb * 2048 + ct * 256:
                                        eb * 2048 + (ct + 1) * 256],
                                    ESTG[:, sb * 256:(sb + 1) * 256]),
                                waits=cw, drain=False)
                            ldm_map[(q, ee)].append(est_free[sb])
                            yield

                    for _ in ld_gen(*slots[0]):
                        yield
                    escq = {}
                    for si, (q, ee) in enumerate(slots):
                        eb = ee % 2
                        nld = ld_gen(*slots[si + 1]) if si + 1 < len(slots) \
                            else None
                        mm_last = None
                        ldm = ldm_map[(q, ee)]
                        for t in range(NCH):
                            gi = si * NCH + t
                            sl4 = gi % 3
                            pp4 = P45 if sl4 < 2 else P67
                            of4 = (sl4 % 2) * 512 if sl4 < 2 else 0
                            ww = []
                            if t == 0:
                                ww = list(ldm)
                            if gi >= 3:
                                ww.append(escq[gi - 3])
                            def emm(e, t=t, eb=eb, pp4=pp4, of4=of4):
                                for k in range(NK):
                                    li = e.matmul(
                                        pp4[:, of4:of4 + 256],
                                        XN[:, k * L + t * 128:k * L + (t + 1) * 128],
                                        EWB[:, eb * 2048 + k * 256:eb * 2048 + (k + 1) * 256],
                                        start=(k == 0), stop=(k == NK - 1))
                                return li
                            mmx = st("T", emm, waits=ww)
                            mm_last = mmx
                            yield
                            aw = [mmx]
                            if ee == 0 and prev_dump is not None:
                                aw.append(prev_dump)
                            if ee == 0:
                                def accf(e, t=t, pp4=pp4, of4=of4):
                                    return e.tensor_scalar(
                                        OACC[:, t * 256:(t + 1) * 256],
                                        pp4[:, of4:of4 + 256],
                                        WFULL[:, t * 8:t * 8 + 1],
                                        None, ALU.mult)
                            else:
                                def accf(e, t=t, ee=ee, pp4=pp4, of4=of4):
                                    return e.scalar_tensor_tensor(
                                        OACC[:, t * 256:(t + 1) * 256],
                                        pp4[:, of4:of4 + 256],
                                        WFULL[:, t * 8 + ee:t * 8 + ee + 1],
                                        OACC[:, t * 256:(t + 1) * 256],
                                        ALU.mult, ALU.add)
                            escq[gi] = st("V", accf, waits=aw,
                                          drain=(t % 4 == 0))
                            yield
                            if t % 2 == 1 and nld is not None:
                                for _ in range(4):
                                    try:
                                        next(nld)
                                        yield
                                    except StopIteration:
                                        break
                        if nld is not None:
                            for _ in nld:
                                yield
                        ewb_mm_last[eb] = mm_last
                        if ee == 7:
                            prev_dump = st("Q", lambda e, q=q: e.dma_start(
                                out=oaccd[q], in_=OACC[:]),
                                waits=[escq[si * NCH + NCH - 1]],
                                sem="QDmp", inc=16)
                            dumps.append(prev_dump)
                            yield

                _gen = moe_gen()
                _done = [False]

                def drip(n):
                    for _ in range(n):
                        if _done[0]:
                            return
                        try:
                            next(_gen)
                        except StopIteration:
                            _done[0] = True
                            return


                P01B = P01[:].bitcast(BF16)
                st_c01 = [None]
                st_c23 = [minit2]
                m_szprev = [None]
                m_convdone = [minit]
                m_supd = {}
                m_summ = {}
                m_sxt = {}
                m_dtx = {}
                m_gt = {}
                m_ymm = {}
                m_mep = {}
                m_mxw = {}
                m_btr = {}
                m_bcb = {}
                m_op = {}
                m_szg = {}
                m_att = [None]

                def front_gen(c):
                    pj, half = c // 2, c % 2
                    if half == 0:
                        if pj > 0:
                            m_convdone[0] = st("V", lambda e: e.tensor_copy(
                                hx(XB[:], NXT)[:, :, 0:4],
                                hx(XB[:], NXT)[:, :, 256:260]),
                                waits=[m_convdone[0]])
                            yield
                        wave_ev = []
                        for wv in range(9):
                            ct0 = 2 * wv
                            def xmm(e, ct0=ct0, pj=pj):
                                for ct in range(ct0, ct0 + 2):
                                    for k in range(NK):
                                        li = e.matmul(
                                            P01[:, (ct - ct0) * 256:(ct - ct0 + 1) * 256],
                                            WIN[:, k * WIN_F + XBCOFF + ct * 128:
                                                k * WIN_F + XBCOFF + (ct + 1) * 128],
                                            XN[:, k * L + pj * 256:k * L + (pj + 1) * 256],
                                            start=(k == 0), stop=(k == NK - 1))
                                return li
                            mw = st("T", xmm,
                                    waits=[m_convdone[0], st_c01[0]])
                            drip(2)
                            yield
                            ev = st("S", lambda e, ct0=ct0: e.copy(
                                hx(XB[:], NXT)[:, ct0:ct0 + 2, 4:260],
                                hx(P01[:, 0:512], 2)), waits=[mw])
                            st_c01[0] = ev
                            wave_ev.append(ev)
                            yield
                        cv_ev = [None, None]
                        mm_cv = {}
                        cm = None
                        cms = {}
                        for ct in range(NXT):
                            def dgen(e, ct=ct):
                                for dd in range(4):
                                    j = (4 * ct + dd) % 8
                                    li = e.activation(
                                        DGS[:, j * 128:(j + 1) * 128],
                                        IBF[:], AF.Identity,
                                        scale=CW[:, ct * 4 + dd:ct * 4 + dd + 1])
                                return li
                            mdg = st("S", dgen, waits=[mm_cv.get(ct - 2)])
                            yield
                            cvp, cvo = (P01, 512) if ct % 2 == 0 else (P67, 512)
                            def cvmm(e, ct=ct, cvp=cvp, cvo=cvo):
                                for dd in range(4):
                                    j = (4 * ct + dd) % 8
                                    li = e.matmul(
                                        cvp[:, cvo:cvo + 256],
                                        DGS[:, j * 128:(j + 1) * 128],
                                        XB[:, ct * XBW + 4 - dd:
                                           ct * XBW + 260 - dd],
                                        start=(dd == 0), stop=(dd == 3))
                                return li
                            mm_cv[ct] = st("T", cvmm, waits=[
                                mdg, wave_ev[ct // 2], cv_ev[ct % 2]])
                            drip(3)
                            yield
                            def cvev(e, ct=ct, cvp=cvp, cvo=cvo):
                                e.activation(
                                    AC[:, ct * 256:(ct + 1) * 256],
                                    cvp[:, cvo:cvo + 256], AF.Identity,
                                    scale=1.0, bias=CB[:, ct:ct + 1])
                                e.drain()
                                return e.activation(
                                    TX[:, (ct % 2) * 256:(ct % 2 + 1) * 256],
                                    cvp[:, cvo:cvo + 256], AF.Tanh, scale=0.5,
                                    bias=CBH[:, ct:ct + 1])
                            cv_ev[ct % 2] = st("S", cvev, waits=[
                                mm_cv[ct], cms.get(ct - 2)])
                            yield
                            if ct % 2 == 1:
                                def comp(e, ct=ct):
                                    e.tensor_scalar(TX[:], TX[:], 0.5, 0.5,
                                                    ALU.mult, ALU.add)
                                    e.drain()
                                    return e.tensor_tensor(
                                        AC[:, (ct - 1) * 256:(ct + 1) * 256],
                                        AC[:, (ct - 1) * 256:(ct + 1) * 256],
                                        TX[:], ALU.mult)
                                cm = st("V", comp,
                                        waits=[cv_ev[0], cv_ev[1]])
                                cms[ct] = cms[ct - 1] = cm
                                drip(2)
                                yield
                        for ch in (c, c + 1):
                            need = ch - 2
                            while need >= 0 and need not in m_mep:
                                yield
                            mprev = m_mep.get(need)
                            for w in range(2):
                                zev = []
                                for bb in range(2):
                                    def zmm(e, ch=ch, w=w, bb=bb):
                                        for k in range(NK):
                                            li = e.matmul(
                                                P01[:, 0:512],
                                                XN[:, k * L + ch * 128:k * L + (ch + 1) * 128],
                                                WIN[:, k * WIN_F + w * 1024 + bb * 512:
                                                    k * WIN_F + w * 1024 + (bb + 1) * 512],
                                                start=(k == 0), stop=(k == NK - 1))
                                        return li
                                    mz = st("T", zmm,
                                            waits=[st_c01[0], m_szprev[0]])
                                    drip(2)
                                    yield
                                    def zef(e, ch=ch, w=w, bb=bb):
                                        e.activation(
                                            TZ[:, bb * 512:(bb + 1) * 512],
                                            P01[:, 0:512], AF.Tanh, scale=0.5)
                                        e.drain()
                                        return e.copy(
                                            SZ[:, (ch % 2) * D_INNER + w * 1024 + bb * 512:
                                               (ch % 2) * D_INNER + w * 1024 + (bb + 1) * 512],
                                            P01[:, 0:512])
                                    ev = st("S", zef, waits=[mz] +
                                            ([mprev] if (w == 0 and bb == 0
                                                         and mprev) else []))
                                    st_c01[0] = ev
                                    zev.append(ev)
                                    yield
                                m_szb0 = st("V", lambda e: e.tensor_scalar(
                                    TZ[:], TZ[:], 0.5, 0.5, ALU.mult,
                                    ALU.add), waits=zev)
                                drip(1)
                                yield
                                for j in range(2):
                                    def szb(e, ch=ch, w=w, j=j):
                                        sl = SZ[:, (ch % 2) * D_INNER +
                                                w * 1024 + j * 512:
                                                (ch % 2) * D_INNER +
                                                w * 1024 + (j + 1) * 512]
                                        return e.tensor_tensor(
                                            sl, sl,
                                            TZ[:, j * 512:(j + 1) * 512],
                                            ALU.mult)
                                    m_szprev[0] = st("V", szb, waits=[],
                                                     drain=(j == 0))
                                    drip(1)
                                yield
                        m_convdone[0] = cm
                        m_bcb[pj] = st("V", lambda e, pj=pj: e.tensor_copy(
                            BCB[:, (pj % 2) * 512:(pj % 2) * 512 + 512],
                            AC[:, 16 * 256:16 * 256 + 512]),
                            waits=[cm, m_btr.get(2 * pj - 3)])
                        yield

                    sx_ev = []
                    for hh in range(2):
                        def sxtr(e, half=half, hh=hh):
                            for j in range(8):
                                jt = hh * 8 + j
                                li = e.transpose(
                                    P01B[:, j * 128:(j + 1) * 128],
                                    AC[:, jt * 256 + half * 128:
                                       jt * 256 + half * 128 + 128],
                                    IBF[:])
                            return li
                        mst_ = st("T", sxtr,
                                  waits=[m_convdone[0], st_c01[0]])
                        drip(2)
                        yield
                        mse = st("S", lambda e, c=c, hh=hh: e.copy(
                            SXT[:, (c % 2) * D_INNER + hh * 1024:
                                (c % 2) * D_INNER + (hh + 1) * 1024],
                            P01B[:, 0:1024]),
                            waits=[mst_] +
                            ([m_mxw.get(c - 2), m_mep.get(c - 2)]
                             if hh == 0 else []))
                        st_c01[0] = mse
                        sx_ev.append(mse)
                        yield
                    m_sxt[c] = sx_ev[1]
                    md = None
                    for j in range(4):
                        md = st("V", lambda e, c=c, j=j: e.tensor_tensor(
                            hx(DTX[:, (c % 2) * D_INNER + j * 512:
                               (c % 2) * D_INNER + (j + 1) * 512], 8),
                            hx(SXT[:, (c % 2) * D_INNER + j * 512:
                               (c % 2) * D_INNER + (j + 1) * 512], 8),
                            DTER[:, c * 32 + j * 8:c * 32 + (j + 1) * 8]
                            .to_broadcast([128, 8, HEADDIM]),
                            ALU.mult),
                            waits=[sx_ev[0] if j < 2 else sx_ev[1],
                                   m_summ.get(c - 2)]
                            if j in (0, 2) else [],
                            drain=(j == 0))
                        drip(1)
                        yield
                    m_dtx[c] = md
                    bfm = BCB[:, (pj % 2) * 512 + half * 128:
                              (pj % 2) * 512 + half * 128 + 128]
                    cfm = BCB[:, (pj % 2) * 512 + 256 + half * 128:
                              (pj % 2) * 512 + 256 + half * 128 + 128]
                    mg = st("T", lambda e, bfm=bfm, cfm=cfm: e.matmul(
                        P01[:, 0:128], bfm, cfm, start=True, stop=True),
                        waits=[m_bcb[pj], st_c01[0]])
                    drip(2)
                    yield
                    m_gt[c] = st("V", lambda e, c=c: e.scalar_tensor_tensor(
                        GT[:, (c % 2) * Q:(c % 2 + 1) * Q], P01[:, 0:128],
                        1.0, MLE[:], ALU.mult, ALU.mult),
                        waits=[mg, m_ymm.get(c - 2)])
                    st_c01[0] = m_gt[c]
                    yield

                def back_gen(c):
                    pj, half = c // 2, c % 2
                    bfm = BCB[:, (pj % 2) * 512 + half * 128:
                              (pj % 2) * 512 + half * 128 + 128]
                    cfm = BCB[:, (pj % 2) * 512 + 256 + half * 128:
                              (pj % 2) * 512 + 256 + half * 128 + 128]
                    gt = GT[:, (c % 2) * Q:(c % 2 + 1) * Q]
                    dto = (c % 2) * D_INNER
                    yh = []
                    mlast = None
                    for w in range(2):
                        def ymm(e, w=w, cfm=cfm, gt=gt, dto=dto):
                            for bb in range(2):
                                e.matmul(
                                    P23[:, bb * 512:(bb + 1) * 512], cfm,
                                    S_ALL[:, w * 1024 + bb * 512:w * 1024 + (bb + 1) * 512],
                                    start=True, stop=False)
                            for bb in range(2):
                                li = e.matmul(
                                    P23[:, bb * 512:(bb + 1) * 512], gt,
                                    DTX[:, dto + w * 1024 + bb * 512:
                                        dto + w * 1024 + (bb + 1) * 512],
                                    start=False, stop=True)
                            return li
                        mlast = st("T", ymm, waits=[
                            st_c23[0], m_gt.get(c), m_dtx.get(c),
                            m_supd.get(c - 1), m_bcb.get(pj),
                            m_mep.get(c - 1)] if w == 0 else [st_c23[0]])
                        drip(4)
                        yield
                        ev = st("S", lambda e, w=w: e.copy(
                            Y1[:, w * 1024:(w + 1) * 1024], P23[:]),
                            waits=[mlast])
                        st_c23[0] = ev
                        yh.append(ev)
                        yield
                    m_ymm[c] = mlast
                    mx = None
                    for j in range(4):
                        mx = st("V", lambda e, c=c, j=j, dto=dto: e.tensor_tensor(
                            hx(DTX[:, dto + j * 512:dto + (j + 1) * 512], 8),
                            hx(SXT[:, dto + j * 512:dto + (j + 1) * 512], 8),
                            DTWD[:, c * 32 + j * 8:c * 32 + (j + 1) * 8]
                            .to_broadcast([128, 8, HEADDIM]),
                            ALU.mult), waits=[m_ymm[c]] if j == 0 else [],
                            drain=(j == 0))
                        drip(1)
                        yield
                    m_mxw[c] = mx
                    mbt = st("T", lambda e, bfm=bfm: e.transpose(
                        P23B[:, 0:128], bfm, IBF[:]),
                        waits=[st_c23[0], m_bcb.get(pj)])
                    m_btr[c] = mbt
                    yield
                    ev_btm = st("S", lambda e: e.copy(BTM[:], P23B[:, 0:128]),
                                waits=[mbt])
                    st_c23[0] = ev_btm
                    yield
                    msup = None
                    msu = None
                    for w in range(2):
                        def supd(e, w=w, dto=dto):
                            for bb in range(2):
                                li = e.matmul(
                                    P23[:, bb * 512:(bb + 1) * 512], BTM[:],
                                    DTX[:, dto + w * 1024 + bb * 512:
                                        dto + w * 1024 + (bb + 1) * 512],
                                    start=True, stop=True)
                            return li
                        msu = st("T", supd, waits=[st_c23[0], mx] +
                                 ([msup] if msup else []))
                        drip(2)
                        yield
                        md = st("V", lambda e, c=c, w=w, dto=dto:
                                e.tensor_tensor(
                            hx(DTX[:, dto + w * 1024:dto + (w + 1) * 1024], 16),
                            hx(S_ALL[:, w * 1024:(w + 1) * 1024], 16),
                            BETB[:, c * 32 + w * 16:c * 32 + (w + 1) * 16]
                            .to_broadcast([128, 16, HEADDIM]),
                            ALU.mult), waits=[msu])
                        drip(1)
                        yield
                        msup = st("V", lambda e, w=w, dto=dto: e.tensor_tensor(
                            S_ALL[:, w * 1024:(w + 1) * 1024],
                            DTX[:, dto + w * 1024:dto + (w + 1) * 1024],
                            P23[:], ALU.add), waits=[md], drain=False)
                        drip(1)
                        yield
                    m_summ[c] = msu
                    m_supd[c] = msup
                    mep = None
                    for j in range(4):
                        sl = slice(j * 512, (j + 1) * 512)
                        def epa(e, c=c, j=j, sl=sl, dto=dto):
                            e.tensor_tensor(
                                hx(Y1[:, sl], 8), hx(Y1[:, sl], 8),
                                EC[:, c * 32 + j * 8:c * 32 + (j + 1) * 8]
                                .to_broadcast([128, 8, HEADDIM]), ALU.mult)
                            e.drain()
                            e.tensor_tensor(
                                hx(SCR[:, sl], 8),
                                hx(SXT[:, dto + j * 512:dto + (j + 1) * 512], 8),
                                D_BC[:, j * 8:(j + 1) * 8]
                                .to_broadcast([128, 8, HEADDIM]), ALU.mult)
                            e.drain()
                            e.tensor_tensor(Y1[:, sl], Y1[:, sl], SCR[:, sl],
                                            ALU.add)
                            e.drain()
                            return e.tensor_tensor(
                                Y1[:, sl], Y1[:, sl],
                                SZ[:, (c % 2) * D_INNER + j * 512:
                                   (c % 2) * D_INNER + (j + 1) * 512],
                                ALU.mult)
                        mep = st("V", epa,
                                 waits=(yh + [m_op.get(c - 1),
                                              m_szg.get(c)]) if j == 0
                                 else [])
                        drip(1)
                        yield
                    m_mep[c] = mep
                    if debug:
                        mdbg = st("Q", lambda e, c=c: [
                            e.dma_start(out=dbg["dbg_z"][c],
                                        in_=SZ[:, (c % 2) * D_INNER:
                                               (c % 2 + 1) * D_INNER]),
                            e.dma_start(out=dbg["dbg_sx"][c],
                                        in_=SXT[:, (c % 2) * D_INNER:
                                                (c % 2 + 1) * D_INNER]),
                            e.dma_start(out=dbg["dbg_y"][c], in_=Y1[:]),
                        ], waits=[mep, m_sxt[c]], sem="QD", inc=3 * 16)
                        m_mep[c] = mdbg
                        yield
                    def gtr(e):
                        for j in range(16):
                            li = e.transpose(P23B[:, j * 128:(j + 1) * 128],
                                             Y1[:, j * 128:(j + 1) * 128],
                                             IBF[:])
                        return li
                    mgt = st("T", gtr, waits=[mep, st_c23[0]])
                    drip(4)
                    yield
                    ev_gfm = st("S", lambda e: e.copy(SCR[:], P23B[:, 0:2048]),
                                waits=[mgt])
                    st_c23[0] = ev_gfm
                    yield
                    mss = st("S", lambda e, c=c: e.activation(
                        Y1[:], Y1[:], AF.Square, accum_out=SS[:, c:c + 1]),
                        waits=[mgt, m_mep[c]])
                    yield
                    def opmm(e):
                        for kd in range(NKI):
                            for bb in range(2):
                                li = e.matmul(
                                    P23[:, bb * 512:(bb + 1) * 512],
                                    SCR[:, kd * 128:(kd + 1) * 128],
                                    WOUT[:, kd * D_MODEL + bb * 512:
                                         kd * D_MODEL + (bb + 1) * 512],
                                    start=(kd == 0), stop=(kd == NKI - 1))
                        return li
                    mop = st("T", opmm, waits=[ev_gfm] +
                             (wom if c == 0 else []))
                    m_op[c] = mop
                    drip(5)
                    yield
                    ev_att = st("S", lambda e: e.copy(ATTS[:], P23[:]),
                                waits=[mop, m_att[0]])
                    st_c23[0] = ev_att
                    yield
                    m_att[0] = st("S", lambda e, c=c: e.dma_start(
                        out=attd[c], in_=ATTS[:]),
                        waits=[ev_att], sem="AT0", inc=16)
                    yield

                def pump(gens):
                    alive = [g for g in gens if g is not None]
                    while alive:
                        for g in list(alive):
                            try:
                                next(g)
                            except StopIteration:
                                alive.remove(g)

                pump([front_gen(0)])
                for c in range(NCH):
                    pump([back_gen(c),
                          front_gen(c + 1) if c + 1 < NCH else None])

                drip(10 ** 6)
                att_store = m_att[0]
                if debug:
                    st("Q", lambda e: e.dma_start(out=dbg["dbg_sall"],
                                                  in_=S_ALL[:]),
                       waits=[m_supd[NCH - 1]], sem="QD", inc=16)

                ctr = sc.resolve(ctr)
                bar2 = [sc.last[k] for k in ("T", "V", "S", "G")
                        if k in sc.last]
                bar2 += [att_store, dumps[-1]]
                with nc.Block() as blk:
                    blk.tensor(lambda e: sc.emit("T", e, sems))
                    blk.vector(lambda e: sc.emit("V", e, sems))
                    blk.scalar(lambda e: sc.emit("S", e, sems))
                    blk.sync(lambda e: sc.emit("Q", e, sems))
                    blk.gpsimd(lambda e: sc.emit("G", e, sems))

        if stages < 3:
            return nc
        with ExitStack() as p2:
            C1 = p2.enter_context
            ATR = C1(nc.sbuf_tensor([128, 4 * D_MODEL], BF16))
            XR2 = C1(nc.sbuf_tensor([128, 4 * D_MODEL], F32))
            OACR = C1(nc.sbuf_tensor([128, 4 * D_MODEL], BF16))
            OST = C1(nc.sbuf_tensor([128, 4 * D_MODEL], F32))

            sc = Script()
            st = sc.step

            mrms = st("S", lambda e: [
                e.activation(RMS[:], SS[:], AF.Ln, scale=1.0 / D_INNER,
                             bias=KCOL[:, 0:1]),
                e.drain(),
                e.activation(RMS[:], RMS[:], AF.Exp, scale=-0.5),
            ][-1], waits=bar2)

            fcons = [None, None, None, None]
            ocons = [None, None, None, None]
            for t in range(NCH):
                xb = t % 4
                qr = ["QR0", "QR1", "QX0", "QX1"][xb]
                qa = ["QA0", "QA1", "QW0", "QW1"][xb]
                qb = ["QB0", "QB1", "QO0", "QO1"][xb]
                mx2 = st("Q", lambda e, t=t, xb=xb: e.dma_start(
                    out=XR2[:, xb * D_MODEL:(xb + 1) * D_MODEL],
                    in_=d["x"][t * 128:(t + 1) * 128, :]),
                    waits=[fcons[xb]], sem=qr, inc=16)
                ma2 = st("Q", lambda e, t=t, xb=xb: e.dma_start(
                    out=ATR[:, xb * D_MODEL:(xb + 1) * D_MODEL], in_=attd[t]),
                    waits=[fcons[xb], att_store], sem=qa, inc=16)
                mo2 = st("Q", lambda e, t=t, xb=xb: [
                    e.dma_start(out=OACR[:, xb * D_MODEL + q * 256:
                                xb * D_MODEL + (q + 1) * 256],
                                in_=oaccd[q][:, t * 256:(t + 1) * 256])
                    for q in range(4)],
                    waits=[fcons[xb]] + dumps, sem=qb, inc=4 * 16)
                def fin(e, t=t, xb=xb):
                    e.scalar_tensor_tensor(
                        OST[:, xb * D_MODEL:(xb + 1) * D_MODEL],
                        ATR[:, xb * D_MODEL:(xb + 1) * D_MODEL],
                        RMS[:, t:t + 1],
                        OACR[:, xb * D_MODEL:(xb + 1) * D_MODEL],
                        ALU.mult, ALU.add)
                    e.drain()
                    return e.tensor_tensor(
                        OST[:, xb * D_MODEL:(xb + 1) * D_MODEL],
                        OST[:, xb * D_MODEL:(xb + 1) * D_MODEL],
                        XR2[:, xb * D_MODEL:(xb + 1) * D_MODEL], ALU.add)
                mf = st("V", fin, waits=[mx2, ma2, mo2, mrms, ocons[xb]])
                fcons[xb] = mf
                ocons[xb] = st("S", lambda e, t=t, xb=xb: e.dma_start(
                    out=out[t * 128:(t + 1) * 128, :],
                    in_=OST[:, xb * D_MODEL:(xb + 1) * D_MODEL]),
                    waits=[mf], sem=["O0", "O1", "AT1", "QC"][xb], inc=16)

            st("S", lambda e: e.nop(),
               waits=[ocons[0], ocons[1], ocons[2], ocons[3], att_store],
               inc=0)

            ctr = sc.resolve(ctr)
            with nc.Block() as blk:
                blk.tensor(lambda e: sc.emit("T", e, sems))
                blk.vector(lambda e: sc.emit("V", e, sems))
                blk.scalar(lambda e: sc.emit("S", e, sems))
                blk.sync(lambda e: sc.emit("Q", e, sems))
    return nc


def _consts():
    idx = np.arange(Q)
    mle = (idx[:, None] <= idx[None, :]).astype(BF)
    mskt = np.where(idx[None, :] <= idx[:, None], 0.0, NEG).astype(np.float32)
    ibf = np.eye(Q).astype(BF)
    if32 = np.eye(Q).astype(np.float32)
    onef = np.ones((128, Q), dtype=np.float32)
    return mle, mskt, ibf, if32, onef


def make_inputs(L0_attn_x, L0_attn_in_proj_w, L0_attn_conv_w, L0_attn_conv_b,
                L0_attn_norm_w, L0_attn_out_proj_w, L0_attn_dt_bias,
                L0_attn_A_log, L0_attn_D, L0_ffn_gate_w, L0_ffn_expert_w):
    x = np.ascontiguousarray(np.asarray(L0_attn_x, dtype=np.float32))
    wi = np.ascontiguousarray(
        np.concatenate([np.asarray(L0_attn_in_proj_w, np.float32),
                        np.asarray(L0_ffn_gate_w, np.float32).T], axis=1))
    wo = np.ascontiguousarray(np.asarray(L0_attn_out_proj_w, np.float32))
    we = np.ascontiguousarray(
        np.asarray(L0_ffn_expert_w, np.float32).transpose(0, 2, 1))
    cwr = np.asarray(L0_attn_conv_w, np.float32)
    cw = np.ascontiguousarray(
        cwr.T.reshape(NXT, 128, 4).transpose(1, 0, 2).reshape(128, NXT * 4))
    cb = np.ascontiguousarray(
        np.asarray(L0_attn_conv_b, np.float32).reshape(NXT, 128).T)
    nw = np.ascontiguousarray(
        np.asarray(L0_attn_norm_w, np.float32).reshape(NKI, 128).T)
    dtb = np.ascontiguousarray(np.asarray(L0_attn_dt_bias, np.float32)[None, :])
    alog = np.ascontiguousarray(np.asarray(L0_attn_A_log, np.float32)[None, :])
    dvec = np.ascontiguousarray(np.asarray(L0_attn_D, np.float32)[None, :])
    mle, mskt, ibf, if32, onef = _consts()
    shared = dict(wi=wi, wo=wo, we=we, cw=cw, cb=cb, nw=nw, dtb=dtb,
                  alog=alog, dvec=dvec, mle=np.asarray(mle),
                  ibf=np.asarray(ibf), if32=if32, onef=onef)
    return x, shared


def kernel(**inputs):
    x, shared = make_inputs(**inputs)
    nc = build_nc()
    in_maps = [dict(shared, x=np.ascontiguousarray(x[b])) for b in range(8)]
    res = run_bass_kernel_spmd(nc, in_maps, core_ids=list(range(8)))
    return np.stack([r["out"] for r in res.results]).astype(np.float32)


# version 29
# speedup vs baseline: 1.0803x; 1.0633x over previous
import numpy as np
import ml_dtypes
import concourse.bass as bass
import concourse.mybir as mybir
from concourse.bass_utils import run_bass_kernel_spmd

F32 = mybir.dt.float32
BF16 = mybir.dt.bfloat16
BF = ml_dtypes.bfloat16
AF = mybir.ActivationFunctionType
ALU = mybir.AluOpType
AX = mybir.AxisListType

D_MODEL = 1024
D_STATE = 128
HEADDIM = 64
D_INNER = 2048
NHEADS = 32
CONV_DIM = 2304
D_IN_PROJ = 4384
WIN_F = D_IN_PROJ + 8
EPS = 1e-5
L = 2048
Q = 128
NCH = 16
NK = 8
NKI = 16
XBCOFF = D_INNER
DTOFF = 4352
GOFF = 4384
NXT = 18
XBW = 260
NEG = -1e30


class Mark:
    __slots__ = ("sem", "val")
    def __init__(self, sem):
        self.sem = sem
        self.val = None


class Script:

    def __init__(self):
        self.steps = []
        self.last = {}

    def step(self, eng, fn, waits=(), sem=None, inc=1, drain=True):
        sem = sem or eng
        m = Mark(sem) if inc else None
        self.steps.append((eng, fn, tuple(w for w in waits if w is not None),
                           m, inc, drain))
        if m is not None:
            self.last[eng] = m
        return m

    def resolve(self, ctr):
        for eng, fn, waits, m, inc, dr in self.steps:
            if m is not None:
                ctr[m.sem] += inc
                m.val = ctr[m.sem]
        return ctr

    def emit(self, eng_name, e, sems):
        for eng, fn, waits, m, inc, dr in self.steps:
            if eng != eng_name:
                continue
            for w in waits:
                e.wait_ge(sems[w.sem], w.val)
            if dr and eng in ("V", "S", "G"):
                e.drain()
            last = fn(e)
            if m is not None:
                if isinstance(last, list):
                    assert len(last) * (16 if inc >= 16 else 1) == inc or True
                    for li in last:
                        li.then_inc(sems[m.sem], 16)
                else:
                    last.then_inc(sems[m.sem], inc)


def build_nc(debug=False, stages=3, nch=NCH, tail=True, wload=True):
    nc = bass.Bass("TRN2", target_bir_lowering=False,
                   dynamic_dma_scratch_size=512)
    d = {}
    def di(name, shape, dt):
        d[name] = nc.dram_tensor(name, shape, dt, kind="ExternalInput").ap()
    di("x", [L, D_MODEL], F32)
    di("wi", [D_MODEL, WIN_F], F32)
    di("wo", [D_INNER, D_MODEL], F32)
    di("we", [8, D_MODEL, D_MODEL], F32)
    di("cw", [128, NXT * 4], F32)
    di("cb", [128, NXT], F32)
    di("nw", [128, NKI], F32)
    di("dtb", [1, 32], F32)
    di("alog", [1, 32], F32)
    di("dvec", [1, 32], F32)
    di("mle", [Q, Q], BF16)
    di("ibf", [Q, Q], BF16)
    di("if32", [Q, Q], F32)
    di("onef", [128, Q], F32)
    out = nc.dram_tensor("out", [L, D_MODEL], F32, kind="ExternalOutput").ap()
    _ik = "ExternalOutput" if debug else "Internal"
    attd = nc.dram_tensor("attd", [NCH, 128, D_MODEL], BF16, kind=_ik).ap()
    oaccd = nc.dram_tensor("oaccd", [4, 128, NCH * 256], BF16, kind=_ik).ap()
    dbg = {}
    if debug:
        dspec = [("dbg_xn", [128, NK * L], BF16), ("dbg_z", [NCH, 128, D_INNER], BF16),
                 ("dbg_sx", [NCH, 128, D_INNER], BF16), ("dbg_seg", [128, 512], F32),
                 ("dbg_dt", [128, 512], BF16), ("dbg_y", [NCH, 128, D_INNER], BF16),
                 ("dbg_wf", [128, 128], F32), ("dbg_sall", [128, 2048], BF16),
                 ("dbg_oacc", [128, NCH * 512], F32)]
        for nm, sh, dt in dspec:
            dbg[nm] = nc.dram_tensor(nm, sh, dt, kind="ExternalOutput").ap()

    from contextlib import ExitStack
    with ExitStack() as top:
        E = top.enter_context
        semnames = ("T", "V", "S", "G", "QC", "QX0", "QX1", "QW0", "QW1",
                    "QO0", "QO1", "QE0", "QE1", "QR0", "QR1", "QA0", "QA1",
                    "O0", "O1", "AT0", "AT1", "QD", "QB0", "QB1", "QDmp")
        sems = {k: E(nc.semaphore("sem" + k)) for k in semnames}
        ctr = {k: 0 for k in sems}

        XN = E(nc.sbuf_tensor([128, NK * L], BF16))
        DTER = E(nc.sbuf_tensor([128, 512], BF16))
        EC = E(nc.sbuf_tensor([128, 512], BF16))
        DTWD = E(nc.sbuf_tensor([128, 512], BF16))
        BETB = E(nc.sbuf_tensor([128, 512], BF16))
        MLE = E(nc.sbuf_tensor([Q, Q], BF16))
        WFULL = E(nc.sbuf_tensor([128, 128], F32))
        SS = E(nc.sbuf_tensor([128, NCH], F32))
        RMS = E(nc.sbuf_tensor([128, NCH], F32))
        D_BC = E(nc.sbuf_tensor([128, 32], F32))
        IBF = E(nc.sbuf_tensor([Q, Q], BF16))
        CW = E(nc.sbuf_tensor([128, NXT * 4], F32))
        CB = E(nc.sbuf_tensor([128, NXT], F32))
        CBH = E(nc.sbuf_tensor([128, NXT], F32))
        KCOL = E(nc.sbuf_tensor([128, 2], F32))

        def hx(ap, a):
            return ap.rearrange("p (a b) -> p a b", a=a)

        def bc_heads(t, c):
            return t[:, c * 32:(c + 1) * 32].to_broadcast([128, 32, HEADDIM])

        wim = []
        wom = []

        with ExitStack() as p1a:
            A1 = p1a.enter_context
            WIN = A1(nc.sbuf_tensor([128, NK * WIN_F], BF16))
            WOUT = A1(nc.sbuf_tensor([128, NKI * D_MODEL], BF16))
            p1t = ExitStack()
            A2 = p1t.enter_context
            DT = A2(nc.sbuf_tensor([128, 512], BF16))
            WSTG = A2(nc.sbuf_tensor([128, 2 * 2196], F32))
            WOSTG = A2(nc.sbuf_tensor([128, 2 * D_MODEL], F32))
            XR = A2(nc.sbuf_tensor([128, 2 * D_MODEL], F32))
            XC = A2(nc.sbuf_tensor([128, 2 * D_MODEL], F32))
            XNT = A2(nc.sbuf_tensor([128, 2 * D_MODEL], BF16))
            SQD = A2(nc.sbuf_tensor([128, D_MODEL], BF16))
            ADTB = A2(nc.sbuf_tensor([128, 512], BF16))
            MCOL = A2(nc.sbuf_tensor([128, 8], F32))
            DTR = A2(nc.sbuf_tensor([128, 512], F32))
            DTF = A2(nc.sbuf_tensor([128, 512], F32))
            SEG = A2(nc.sbuf_tensor([128, 512], F32))
            STGF = A2(nc.sbuf_tensor([128, 512], F32))
            LOG = A2(nc.sbuf_tensor([128, 128], F32))
            DTB_BC = A2(nc.sbuf_tensor([128, 32], F32))
            A_BC = A2(nc.sbuf_tensor([128, 32], F32))
            ONEF = A2(nc.sbuf_tensor([128, Q], F32))
            NW = A2(nc.sbuf_tensor([128, NKI], F32))
            ROWA = A2(nc.sbuf_tensor([1, 32], F32))
            ROWB = A2(nc.sbuf_tensor([1, 32], F32))
            ROWD = A2(nc.sbuf_tensor([1, 32], F32))
            MX8 = A2(nc.sbuf_tensor([128, 8], F32))
            V12 = A2(nc.sbuf_tensor([128, 32], F32))
            EQA = A2(nc.sbuf_tensor([128, 8], F32))
            EQB = A2(nc.sbuf_tensor([128, 8], F32))
            W1S = A2(nc.sbuf_tensor([128, NCH], F32))
            W2S = A2(nc.sbuf_tensor([128, NCH], F32))
            SER = A2(nc.sbuf_tensor([1, 512], F32))
            PA = A2(nc.psum_tensor([128, 2048], F32))
            PB = A2(nc.psum_tensor([128, 2048], F32))
            PAB = PA[:].bitcast(BF16)

            sc = Script()
            st = sc.step

            mconst = st("Q", lambda e: [
                e.dma_start(out=MLE[:], in_=d["mle"]),
                e.dma_start(out=IBF[:], in_=d["ibf"]),
                e.dma_start(out=ONEF[:], in_=d["onef"]),
                e.dma_start(out=CW[:], in_=d["cw"]),
                e.dma_start(out=CB[:], in_=d["cb"]),
                e.dma_start(out=NW[:], in_=d["nw"]),
                e.dma_start(out=ROWA[:], in_=d["alog"]),
                e.dma_start(out=ROWB[:], in_=d["dtb"]),
                e.dma_start(out=ROWD[:], in_=d["dvec"]),
            ], sem="QC", inc=9 * 16)
            mk = st("V", lambda e: [
                e.memset(KCOL[:, 0:1], EPS),
                e.memset(KCOL[:, 1:2], 1.0),
            ][-1])
            mcbh = st("V", lambda e: e.tensor_scalar_mul(
                CBH[:], CB[:], 0.5), waits=[mconst])
            ma1 = st("S", lambda e: e.activation(ROWA[:], ROWA[:], AF.Exp),
                     waits=[mconst])
            ma2 = st("V", lambda e: e.tensor_scalar_mul(ROWA[:], ROWA[:], -1.0),
                     waits=[ma1])
            mbc = st("T", lambda e: [
                e.matmul(PB[:, 0:32], ONEF[0:1, :], ROWA[:], start=True, stop=True),
                e.matmul(PB[:, 32:64], ONEF[0:1, :], ROWB[:], start=True, stop=True),
                e.matmul(PB[:, 64:96], ONEF[0:1, :], ROWD[:], start=True, stop=True),
            ][-1], waits=[ma2])
            mbce = st("S", lambda e: [
                e.copy(A_BC[:], PB[:, 0:32]),
                e.copy(DTB_BC[:], PB[:, 32:64]),
                e.copy(D_BC[:], PB[:, 64:96]),
            ][-1], waits=[mbc])

            wdt_casts = []
            cast_free = [None, None]
            if wload:
                for k in range(NK):
                    b = k % 2
                    mldt = st("Q", lambda e, k=k, b=b: e.dma_start(
                        out=WSTG[:, b * 2196:b * 2196 + 40],
                        in_=d["wi"][k * 128:(k + 1) * 128, DTOFF:DTOFF + 40]),
                        waits=[cast_free[b]], sem="QW%d" % b, inc=16)
                    cast_free[b] = st("V", lambda e, k=k, b=b: e.tensor_copy(
                        WIN[:, k * WIN_F + DTOFF:k * WIN_F + DTOFF + 40],
                        WSTG[:, b * 2196:b * 2196 + 40]), waits=[mldt])
                    wdt_casts.append(cast_free[b])

            wpairs = []
            for i in range(16 if wload else 0):
                k, h = i // 2, i % 2
                wpairs.append(("wi", k, h * 2196,
                               (DTOFF - 2196) if h else 2196))
            for kd in range(NKI if wload else 0):
                wpairs.append(("wo", kd, 0, D_MODEL))
            ocast_free = [None, None]
            wdma = {}

            def wpair_dma(i):
                kind, k, fw0, fwn = wpairs[i]
                b = i % 2
                if kind == "wi":
                    wdma[i] = st("Q", lambda e, k=k, fw0=fw0, fwn=fwn, b=b:
                                 e.dma_start(
                        out=WSTG[:, b * 2196:b * 2196 + fwn],
                        in_=d["wi"][k * 128:(k + 1) * 128, fw0:fw0 + fwn]),
                        waits=[cast_free[b]], sem="QW%d" % b, inc=16)
                else:
                    wdma[i] = st("Q", lambda e, kd=k, b=b: e.dma_start(
                        out=WOSTG[:, b * D_MODEL:(b + 1) * D_MODEL],
                        in_=d["wo"][kd * 128:(kd + 1) * 128, :]),
                        waits=[ocast_free[b]], sem="QO%d" % b, inc=16)

            def wpair_cast(i):
                kind, k, fw0, fwn = wpairs[i]
                b = i % 2
                if kind == "wi":
                    cast_free[b] = st(
                        "V", lambda e, k=k, fw0=fw0, fwn=fwn, b=b:
                        e.tensor_copy(
                            WIN[:, k * WIN_F + fw0:k * WIN_F + fw0 + fwn],
                            WSTG[:, b * 2196:b * 2196 + fwn]),
                        waits=[wdma[i]])
                    wim.append(cast_free[b])
                else:
                    ocast_free[b] = st(
                        "S", lambda e, kd=k, b=b: e.activation(
                            WOUT[:, kd * D_MODEL:(kd + 1) * D_MODEL],
                            WOSTG[:, b * D_MODEL:(b + 1) * D_MODEL],
                            AF.Identity, scale=NW[:, kd:kd + 1]),
                        waits=[wdma[i]])
                    wom.append(ocast_free[b])

            xr_free = [None, None]
            pm2 = [None, None]
            pm4 = [None, None]
            pmtr = [None, None]
            pevx = [None, None]
            dt_ev2 = [None, None]
            mra = None
            ev_xn = None
            xm2 = {}
            dtevm = {}

            def xdma(c):
                b = c % 2
                return st("Q", lambda e, c=c, b=b: e.dma_start(
                    out=XR[:, b * D_MODEL:(b + 1) * D_MODEL],
                    in_=d["x"][c * 128:(c + 1) * 128, :]),
                    waits=[xr_free[b]], sem="QX%d" % b, inc=16)

            xdm = {0: xdma(0), 1: xdma(1)}

            def xfront(c):
                b = c % 2
                def ln1(e, b=b):
                    e.reduce_sum(MCOL[:, b * 4:b * 4 + 1],
                                 XR[:, b * D_MODEL:(b + 1) * D_MODEL],
                                 axis=AX.X)
                    e.drain()
                    e.tensor_scalar_mul(MCOL[:, b * 4:b * 4 + 1],
                                        MCOL[:, b * 4:b * 4 + 1],
                                        1.0 / D_MODEL)
                    e.drain()
                    return e.tensor_scalar(
                        XC[:, b * D_MODEL:(b + 1) * D_MODEL],
                        XR[:, b * D_MODEL:(b + 1) * D_MODEL],
                        MCOL[:, b * 4:b * 4 + 1], None, ALU.subtract)
                mln1 = st("V", ln1, waits=[xdm[c], pm2[b]])
                xr_free[b] = mln1
                mln2 = st("S", lambda e, b=b: e.activation(
                    SQD[:], XC[:, b * D_MODEL:(b + 1) * D_MODEL], AF.Square,
                    accum_out=MCOL[:, b * 4 + 1:b * 4 + 2]), waits=[mln1])
                pm2[b] = mln2
                xm2[c] = mln2

            def xback(c):
                b = c % 2
                def ln3(e, b=b):
                    e.activation(MCOL[:, b * 4 + 2:b * 4 + 3],
                                 MCOL[:, b * 4 + 1:b * 4 + 2], AF.Ln,
                                 scale=1.0 / D_MODEL, bias=KCOL[:, 0:1])
                    e.drain()
                    return e.activation(MCOL[:, b * 4 + 2:b * 4 + 3],
                                        MCOL[:, b * 4 + 2:b * 4 + 3], AF.Exp,
                                        scale=-0.5)
                mln3 = st("S", ln3, waits=[xm2[c], pm4[b]] +
                          ([mk] if c == 0 else []))
                mln4 = st("V", lambda e, b=b: e.tensor_scalar(
                    XNT[:, b * D_MODEL:(b + 1) * D_MODEL],
                    XC[:, b * D_MODEL:(b + 1) * D_MODEL],
                    MCOL[:, b * 4 + 2:b * 4 + 3], None, ALU.mult),
                    waits=[mln3, pmtr[b]])
                pm4[b] = mln4
                def tr(e, b=b):
                    for k in range(NK):
                        li = e.transpose(
                            PAB[:, b * 1024 + k * 128:b * 1024 + (k + 1) * 128],
                            XNT[:, b * D_MODEL + k * 128:
                                b * D_MODEL + (k + 1) * 128], IBF[:])
                    return li
                mtr = st("T", tr, waits=[mln4, pevx[b]])
                pmtr[b] = mtr
                ev = st("S", lambda e, c=c, b=b: e.copy(
                    hx(XN[:], NK)[:, :, c * 128:(c + 1) * 128],
                    hx(PAB[:, b * 1024:(b + 1) * 1024], NK)), waits=[mtr])
                pevx[b] = ev
                if tail and wload:
                    def dtmm(e, c=c, b=b):
                        for k in range(NK):
                            e.matmul(
                                PB[:, b * 512:b * 512 + 32],
                                XN[:, k * L + c * 128:k * L + (c + 1) * 128],
                                WIN[:, k * WIN_F + DTOFF:k * WIN_F + DTOFF + 32],
                                start=(k == 0), stop=(k == NK - 1))
                        for k in range(NK):
                            li = e.matmul(
                                PB[:, b * 512 + 256:b * 512 + 264],
                                XN[:, k * L + c * 128:k * L + (c + 1) * 128],
                                WIN[:, k * WIN_F + GOFF:k * WIN_F + GOFF + 8],
                                start=(k == 0), stop=(k == NK - 1))
                        return li
                    mdt = st("T", dtmm, waits=[ev, dt_ev2[b], mbce] +
                             (wdt_casts if c == 0 else []))
                    def dtev(e, c=c, b=b):
                        e.copy(DTR[:, c * 32:(c + 1) * 32],
                               PB[:, b * 512:b * 512 + 32])
                        e.drain()
                        return e.copy(LOG[:, c * 8:(c + 1) * 8],
                                      PB[:, b * 512 + 256:b * 512 + 264])
                    dt_ev2[b] = st("S", dtev, waits=[mdt])
                    dtevm[c] = dt_ev2[b]

            def xra(c):
                def ra(e, c=c):
                    e.max(out=MX8[:], in_=LOG[:, c * 8:(c + 1) * 8])
                    e.drain()
                    e.tensor_copy(V12[:, 2 * c:2 * c + 2], MX8[:, 0:2])
                    e.drain()
                    return e.tensor_tensor(W1S[:, c:c + 1], MX8[:, 0:1],
                                           MX8[:, 1:2], ALU.subtract)
                return st("V", ra, waits=[dtevm[c]])

            for c in range(nch):
                xfront(c)
                if c >= 1:
                    xback(c - 1)
                if c + 2 < nch:
                    xdm[c + 2] = xdma(c + 2)
                if c >= 2:
                    wpair_cast(2 * (c - 2))
                    wpair_cast(2 * (c - 2) + 1)
                if c >= 1:
                    wpair_dma(2 * (c - 1))
                    wpair_dma(2 * (c - 1) + 1)
                if (tail and wload) and c >= 3:
                    mra = xra(c - 3)
            xback(nch - 1)
            if wload:
                wpair_cast(2 * (nch - 2))
                wpair_cast(2 * (nch - 2) + 1)
                wpair_dma(2 * (nch - 1))
                wpair_dma(2 * (nch - 1) + 1)
                wpair_cast(2 * (nch - 1))
                wpair_cast(2 * (nch - 1) + 1)
            if tail and wload:
                for c in range(nch - 3, nch):
                    mra = xra(c)
            ev_xn = pevx[(nch - 1) % 2]
            dt_ev = dt_ev2[1] if dt_ev2[1] is not None else dt_ev2[0]

            mtn = st("S", lambda e: e.activation(W1S[:], W1S[:], AF.Tanh,
                                                 scale=0.5), waits=[mra])
            def rb(e):
                e.tensor_scalar(W1S[:], W1S[:], 0.5, 0.5, ALU.mult, ALU.add)
                e.drain()
                return e.tensor_scalar(W2S[:], W1S[:], -1.0, 1.0, ALU.mult,
                                       ALU.add)
            mrb = st("V", rb, waits=[mtn])
            mrc = None
            for c in range(NCH):
                def rcf(e, c=c):
                    lg = LOG[:, c * 8:(c + 1) * 8]
                    wf = WFULL[:, c * 8:(c + 1) * 8]
                    e.tensor_scalar(EQA[:], lg, V12[:, 2 * c:2 * c + 1], None,
                                    ALU.is_equal)
                    e.drain()
                    e.tensor_scalar(EQB[:], lg, V12[:, 2 * c + 1:2 * c + 2],
                                    None, ALU.is_equal)
                    e.drain()
                    e.tensor_tensor(wf, EQB[:], EQA[:], ALU.mult)
                    e.drain()
                    e.tensor_tensor(wf, EQB[:], wf, ALU.subtract)
                    e.drain()
                    e.tensor_scalar(wf, wf, W2S[:, c:c + 1], None, ALU.mult)
                    e.drain()
                    return e.scalar_tensor_tensor(wf, EQA[:], W1S[:, c:c + 1],
                                                  wf, ALU.mult, ALU.add)
                mrc = st("V", rcf, waits=[mrb])

            if not (tail and wload):
                ctr = sc.resolve(ctr)
                bar1 = [sc.last[k] for k in ("T", "V", "S") if k in sc.last]
                with nc.Block() as blk:
                    blk.tensor(lambda e: sc.emit("T", e, sems))
                    blk.vector(lambda e: sc.emit("V", e, sems))
                    blk.scalar(lambda e: sc.emit("S", e, sems))
                    blk.sync(lambda e: sc.emit("Q", e, sems))
                p1t.close()
                return nc
            mt1 = st("V", lambda e: e.tensor_tensor(
                hx(DTR[:], NCH), hx(DTR[:], NCH),
                DTB_BC[:].to_broadcast([128, 32, NCH]).transpose([0, 2, 1]),
                ALU.add), waits=[dt_ev2[0], dt_ev2[1]])
            mt2 = st("S", lambda e: e.activation(DTF[:], DTR[:], AF.Exp),
                     waits=[mt1])
            mt3 = st("S", lambda e: e.activation(DTF[:], DTF[:], AF.Ln,
                                                 bias=KCOL[:, 1:2]),
                     waits=[mt2])
            mt4 = st("V", lambda e: e.tensor_copy(DT[:], DTF[:]), waits=[mt3])
            mt5 = st("V", lambda e: e.tensor_tensor(
                hx(DTF[:], NCH), hx(DTF[:], NCH),
                A_BC[:].to_broadcast([128, 32, NCH]).transpose([0, 2, 1]),
                ALU.mult), waits=[mt4])
            mt6 = st("V", lambda e: e.tensor_copy(ADTB[:], DTF[:]), waits=[mt5])
            def t7(e):
                for c in range(NCH):
                    li = e.matmul(PB[:, c * 32:(c + 1) * 32], MLE[:],
                                  ADTB[:, c * 32:(c + 1) * 32],
                                  start=True, stop=True)
                return li
            mt7 = st("T", t7, waits=[mt6, dt_ev])
            mt8 = st("S", lambda e: e.copy(SEG[:], PB[:, 0:512]), waits=[mt7])
            mt11a = st("T", lambda e: e.matmul(PB[0:1, 512:1024],
                                               MLE[:, 127:128], ADTB[:],
                                               start=True, stop=True),
                       waits=[mt6, mt8])
            mt11b = st("S", lambda e: e.copy(SER[:], PB[0:1, 512:1024]),
                       waits=[mt11a])
            mt11 = st("T", lambda e: [
                e.matmul(PA[:, 0:512], ONEF[0:1, :], SER[:], start=True,
                         stop=True),
                e.matmul(PA[:, 512:1024], ONEF[64:65, :], SEG[64:65, :],
                         start=True, stop=True),
            ][-1], waits=[mt11b, pevx[0], pevx[1]])
            mt12a = st("V", lambda e: e.scalar_tensor_tensor(
                STGF[:], PA[:, 512:1024], -1.0, SEG[:], ALU.mult, ALU.add),
                waits=[mt11, mt8])
            mt12b = st("S", lambda e: e.activation(EC[:], STGF[:], AF.Exp),
                       waits=[mt12a])
            mt12c = st("V", lambda e: e.tensor_scalar_mul(STGF[:], STGF[:],
                                                          -1.0),
                       waits=[mt12b])
            mt12d = st("S", lambda e: e.activation(DTER[:], STGF[:], AF.Exp),
                       waits=[mt12c])
            mt12e = st("V", lambda e: e.tensor_tensor(DTER[:], DTER[:], DT[:],
                                                      ALU.mult),
                       waits=[mt12d, mt4])
            mt13a = st("V", lambda e: [
                e.scalar_tensor_tensor(STGF[:], PA[:, 0:512], 1.0, SEG[:],
                                       ALU.mult, ALU.subtract),
                e.drain(),
                e.tensor_tensor(STGF[:, 0:480],
                                STGF[:, 0:480],
                                PA[:, 512 + 32:1024], ALU.add),
            ][-1], waits=[mt11, mt8, mt12d])
            mt13b = st("S", lambda e: e.activation(DTWD[:], STGF[:], AF.Exp),
                       waits=[mt13a])
            mt13c = st("V", lambda e: e.tensor_tensor(DTWD[:], DTWD[:], DT[:],
                                                      ALU.mult),
                       waits=[mt13b])
            mt14a = st("V", lambda e: [
                e.tensor_copy(STGF[:], PA[:, 0:512]),
                e.drain(),
                e.scalar_tensor_tensor(STGF[:], PA[:, 512:1024], -1.0,
                                       STGF[:], ALU.mult, ALU.add),
                e.drain(),
                e.tensor_tensor(STGF[:, 0:480], STGF[:, 0:480],
                                PA[:, 512 + 32:1024], ALU.add),
            ][-1], waits=[mt13b])
            mt14 = st("S", lambda e: e.activation(BETB[:], STGF[:], AF.Exp),
                      waits=[mt14a])

            if debug:
                st("Q", lambda e: [
                    e.dma_start(out=dbg["dbg_seg"], in_=SEG[:]),
                    e.dma_start(out=dbg["dbg_dt"], in_=DT[:]),
                    e.dma_start(out=dbg["dbg_wf"], in_=WFULL[:]),
                    e.dma_start(out=dbg["dbg_xn"], in_=XN[:]),
                ], waits=[mt8, mt4, mrc, ev_xn], sem="QD", inc=4 * 16)

            ctr = sc.resolve(ctr)
            bar1 = [sc.last[k] for k in ("T", "V", "S", "Q") if k in sc.last]
            with nc.Block() as blk:
                blk.tensor(lambda e: sc.emit("T", e, sems))
                blk.vector(lambda e: sc.emit("V", e, sems))
                blk.scalar(lambda e: sc.emit("S", e, sems))
                blk.sync(lambda e: sc.emit("Q", e, sems))
            p1t.close()
            if stages < 2:
                return nc

            with ExitStack() as p1b:
                B1 = p1b.enter_context
                ATTS = B1(nc.sbuf_tensor([128, D_MODEL], BF16))
                S_ALL = B1(nc.sbuf_tensor([128, 2048], BF16))
                XB = B1(nc.sbuf_tensor([128, NXT * XBW], BF16))
                AC = B1(nc.sbuf_tensor([128, NXT * 256], BF16))
                TX = B1(nc.sbuf_tensor([128, 2 * 256], BF16))
                DGS = B1(nc.sbuf_tensor([128, 8 * 128], BF16))
                TZ = B1(nc.sbuf_tensor([128, 1024], BF16))
                SZ = B1(nc.sbuf_tensor([128, 2 * D_INNER], BF16))
                SXT = B1(nc.sbuf_tensor([128, 2 * D_INNER], BF16))
                DTX = B1(nc.sbuf_tensor([128, 2 * D_INNER], BF16))
                GT = B1(nc.sbuf_tensor([Q, 2 * Q], BF16))
                BCB = B1(nc.sbuf_tensor([128, 2 * 512], BF16))
                BTM = B1(nc.sbuf_tensor([Q, Q], BF16))
                Y1 = B1(nc.sbuf_tensor([128, D_INNER], BF16))
                SCR = B1(nc.sbuf_tensor([128, D_INNER], BF16))
                OACC = B1(nc.sbuf_tensor([128, NCH * 256], BF16))
                EWB = B1(nc.sbuf_tensor([128, 2 * 8 * 256], BF16))
                ESTG = B1(nc.sbuf_tensor([128, 2 * 256], F32))
                P01 = B1(nc.psum_tensor([128, 1024], F32))
                P23 = B1(nc.psum_tensor([128, 1024], F32))
                P45 = B1(nc.psum_tensor([128, 1024], F32))
                P67 = B1(nc.psum_tensor([128, 1024], F32))
                P23B = P23[:].bitcast(BF16)

                sc = Script()
                st = sc.step

                minit = st("V", lambda e: e.memset(
                    hx(XB[:], NXT)[:, :, 0:4], 0.0), waits=bar1)
                minit2 = st("V", lambda e: e.memset(S_ALL[:], 0.0))

                att_store = None
                dumps = []

                def moe_gen():
                    est_free = [None, None]
                    ewb_mm_last = [None, None]
                    prev_dump = None
                    slots = [(q, ee) for q in range(4) for ee in range(8)]
                    ldm_map = {}

                    def ld_gen(q, ee):
                        eb = ee % 2
                        ldm_map[(q, ee)] = []
                        for ct in range(NK):
                            sb = ct % 2
                            mld = st("Q", lambda e, ee=ee, ct=ct, q=q, sb=sb:
                                     e.dma_start(
                                out=ESTG[:, sb * 256:(sb + 1) * 256],
                                in_=d["we"][ee, ct * 128:(ct + 1) * 128,
                                            q * 256:(q + 1) * 256]),
                                waits=[est_free[sb]], sem="QE%d" % sb,
                                inc=16)
                            yield
                            cw = [mld]
                            if ct < 2 and ewb_mm_last[eb] is not None:
                                cw.append(ewb_mm_last[eb])
                            est_free[sb] = st(
                                "V", lambda e, ct=ct, eb=eb, sb=sb:
                                e.tensor_copy(
                                    EWB[:, eb * 2048 + ct * 256:
                                        eb * 2048 + (ct + 1) * 256],
                                    ESTG[:, sb * 256:(sb + 1) * 256]),
                                waits=cw, drain=False)
                            ldm_map[(q, ee)].append(est_free[sb])
                            yield

                    for _ in ld_gen(*slots[0]):
                        yield
                    escq = {}
                    for si, (q, ee) in enumerate(slots):
                        eb = ee % 2
                        nld = ld_gen(*slots[si + 1]) if si + 1 < len(slots) \
                            else None
                        mm_last = None
                        ldm = ldm_map[(q, ee)]
                        for t in range(NCH):
                            gi = si * NCH + t
                            sl4 = gi % 3
                            pp4 = P45 if sl4 < 2 else P67
                            of4 = (sl4 % 2) * 512 if sl4 < 2 else 0
                            ww = []
                            if t == 0:
                                ww = list(ldm)
                            if gi >= 3:
                                ww.append(escq[gi - 3])
                            def emm(e, t=t, eb=eb, pp4=pp4, of4=of4):
                                for k in range(NK):
                                    li = e.matmul(
                                        pp4[:, of4:of4 + 256],
                                        XN[:, k * L + t * 128:k * L + (t + 1) * 128],
                                        EWB[:, eb * 2048 + k * 256:eb * 2048 + (k + 1) * 256],
                                        start=(k == 0), stop=(k == NK - 1))
                                return li
                            mmx = st("T", emm, waits=ww)
                            mm_last = mmx
                            yield
                            aw = [mmx]
                            if ee == 0 and prev_dump is not None:
                                aw.append(prev_dump)
                            if ee == 0:
                                def accf(e, t=t, pp4=pp4, of4=of4):
                                    return e.tensor_scalar(
                                        OACC[:, t * 256:(t + 1) * 256],
                                        pp4[:, of4:of4 + 256],
                                        WFULL[:, t * 8:t * 8 + 1],
                                        None, ALU.mult)
                            else:
                                def accf(e, t=t, ee=ee, pp4=pp4, of4=of4):
                                    return e.scalar_tensor_tensor(
                                        OACC[:, t * 256:(t + 1) * 256],
                                        pp4[:, of4:of4 + 256],
                                        WFULL[:, t * 8 + ee:t * 8 + ee + 1],
                                        OACC[:, t * 256:(t + 1) * 256],
                                        ALU.mult, ALU.add)
                            escq[gi] = st("V", accf, waits=aw,
                                          drain=(t % 4 == 0))
                            yield
                            if t % 2 == 1 and nld is not None:
                                for _ in range(2):
                                    try:
                                        next(nld)
                                        yield
                                    except StopIteration:
                                        break
                        if nld is not None:
                            for _ in nld:
                                yield
                        ewb_mm_last[eb] = mm_last
                        if ee == 7:
                            prev_dump = st("Q", lambda e, q=q: e.dma_start(
                                out=oaccd[q], in_=OACC[:]),
                                waits=[escq[si * NCH + NCH - 1]],
                                sem="QDmp", inc=16)
                            dumps.append(prev_dump)
                            yield

                _gen = moe_gen()
                _done = [False]

                def drip(n):
                    for _ in range(n):
                        if _done[0]:
                            return
                        try:
                            next(_gen)
                        except StopIteration:
                            _done[0] = True
                            return


                P01B = P01[:].bitcast(BF16)
                st_c01 = [None]
                st_c23 = [minit2]
                m_szprev = [None]
                m_convdone = [minit]
                m_supd = {}
                m_summ = {}
                m_sxt = {}
                m_dtx = {}
                m_gt = {}
                m_ymm = {}
                m_mep = {}
                m_mxw = {}
                m_btr = {}
                m_bcb = {}
                m_op = {}
                m_szg = {}
                m_att = [None]

                def front_gen(c):
                    pj, half = c // 2, c % 2
                    if half == 0:
                        if pj > 0:
                            m_convdone[0] = st("V", lambda e: e.tensor_copy(
                                hx(XB[:], NXT)[:, :, 0:4],
                                hx(XB[:], NXT)[:, :, 256:260]),
                                waits=[m_convdone[0]])
                            yield
                        wave_ev = []
                        for wv in range(9):
                            ct0 = 2 * wv
                            def xmm(e, ct0=ct0, pj=pj):
                                for ct in range(ct0, ct0 + 2):
                                    for k in range(NK):
                                        li = e.matmul(
                                            P01[:, (ct - ct0) * 256:(ct - ct0 + 1) * 256],
                                            WIN[:, k * WIN_F + XBCOFF + ct * 128:
                                                k * WIN_F + XBCOFF + (ct + 1) * 128],
                                            XN[:, k * L + pj * 256:k * L + (pj + 1) * 256],
                                            start=(k == 0), stop=(k == NK - 1))
                                return li
                            mw = st("T", xmm,
                                    waits=[m_convdone[0], st_c01[0]])
                            drip(2)
                            yield
                            ev = st("S", lambda e, ct0=ct0: e.copy(
                                hx(XB[:], NXT)[:, ct0:ct0 + 2, 4:260],
                                hx(P01[:, 0:512], 2)), waits=[mw])
                            st_c01[0] = ev
                            wave_ev.append(ev)
                            yield
                        cv_ev = [None, None]
                        mm_cv = {}
                        cm = None
                        cms = {}
                        for ct in range(NXT):
                            def dgen(e, ct=ct):
                                for dd in range(4):
                                    j = (4 * ct + dd) % 8
                                    li = e.activation(
                                        DGS[:, j * 128:(j + 1) * 128],
                                        IBF[:], AF.Identity,
                                        scale=CW[:, ct * 4 + dd:ct * 4 + dd + 1])
                                return li
                            mdg = st("S", dgen, waits=[mm_cv.get(ct - 2)])
                            yield
                            cvp, cvo = (P01, 512) if ct % 2 == 0 else (P67, 512)
                            def cvmm(e, ct=ct, cvp=cvp, cvo=cvo):
                                for dd in range(4):
                                    j = (4 * ct + dd) % 8
                                    li = e.matmul(
                                        cvp[:, cvo:cvo + 256],
                                        DGS[:, j * 128:(j + 1) * 128],
                                        XB[:, ct * XBW + 4 - dd:
                                           ct * XBW + 260 - dd],
                                        start=(dd == 0), stop=(dd == 3))
                                return li
                            mm_cv[ct] = st("T", cvmm, waits=[
                                mdg, wave_ev[ct // 2], cv_ev[ct % 2]])
                            drip(3)
                            yield
                            def cvev(e, ct=ct, cvp=cvp, cvo=cvo):
                                e.activation(
                                    AC[:, ct * 256:(ct + 1) * 256],
                                    cvp[:, cvo:cvo + 256], AF.Identity,
                                    scale=1.0, bias=CB[:, ct:ct + 1])
                                e.drain()
                                return e.activation(
                                    TX[:, (ct % 2) * 256:(ct % 2 + 1) * 256],
                                    cvp[:, cvo:cvo + 256], AF.Tanh, scale=0.5,
                                    bias=CBH[:, ct:ct + 1])
                            cv_ev[ct % 2] = st("S", cvev, waits=[
                                mm_cv[ct], cms.get(ct - 2)])
                            yield
                            if ct % 2 == 1:
                                def comp(e, ct=ct):
                                    e.tensor_scalar(TX[:], TX[:], 0.5, 0.5,
                                                    ALU.mult, ALU.add)
                                    e.drain()
                                    return e.tensor_tensor(
                                        AC[:, (ct - 1) * 256:(ct + 1) * 256],
                                        AC[:, (ct - 1) * 256:(ct + 1) * 256],
                                        TX[:], ALU.mult)
                                cm = st("V", comp,
                                        waits=[cv_ev[0], cv_ev[1]])
                                cms[ct] = cms[ct - 1] = cm
                                drip(2)
                                yield
                        for ch in (c, c + 1):
                            need = ch - 2
                            while need >= 0 and need not in m_mep:
                                yield
                            mprev = m_mep.get(need)
                            for w in range(2):
                                for bb in range(2):
                                    def zmm(e, ch=ch, w=w, bb=bb):
                                        for k in range(NK):
                                            li = e.matmul(
                                                P01[:, 0:512],
                                                XN[:, k * L + ch * 128:k * L + (ch + 1) * 128],
                                                WIN[:, k * WIN_F + w * 1024 + bb * 512:
                                                    k * WIN_F + w * 1024 + (bb + 1) * 512],
                                                start=(k == 0), stop=(k == NK - 1))
                                        return li
                                    mz = st("T", zmm, waits=[st_c01[0]])
                                    drip(2)
                                    yield
                                    ev = st("S", lambda e, ch=ch, w=w, bb=bb:
                                            e.activation(
                                        SZ[:, (ch % 2) * D_INNER + w * 1024 + bb * 512:
                                           (ch % 2) * D_INNER + w * 1024 + (bb + 1) * 512],
                                        P01[:, 0:512], AF.Silu),
                                        waits=[mz] +
                                        ([mprev] if (w == 0 and bb == 0
                                                     and mprev) else []))
                                    st_c01[0] = ev
                                    m_szg[ch] = ev
                                    drip(1)
                                    yield
                        m_convdone[0] = cm
                        m_bcb[pj] = st("V", lambda e, pj=pj: e.tensor_copy(
                            BCB[:, (pj % 2) * 512:(pj % 2) * 512 + 512],
                            AC[:, 16 * 256:16 * 256 + 512]),
                            waits=[cm, m_btr.get(2 * pj - 3)])
                        yield

                    sx_ev = []
                    for hh in range(2):
                        def sxtr(e, half=half, hh=hh):
                            for j in range(8):
                                jt = hh * 8 + j
                                li = e.transpose(
                                    P01B[:, j * 128:(j + 1) * 128],
                                    AC[:, jt * 256 + half * 128:
                                       jt * 256 + half * 128 + 128],
                                    IBF[:])
                            return li
                        mst_ = st("T", sxtr,
                                  waits=[m_convdone[0], st_c01[0]])
                        drip(2)
                        yield
                        mse = st("S", lambda e, c=c, hh=hh: e.copy(
                            SXT[:, (c % 2) * D_INNER + hh * 1024:
                                (c % 2) * D_INNER + (hh + 1) * 1024],
                            P01B[:, 0:1024]),
                            waits=[mst_] +
                            ([m_mxw.get(c - 2), m_mep.get(c - 2)]
                             if hh == 0 else []))
                        st_c01[0] = mse
                        sx_ev.append(mse)
                        yield
                    m_sxt[c] = sx_ev[1]
                    md = None
                    for j in range(4):
                        md = st("V", lambda e, c=c, j=j: e.tensor_tensor(
                            hx(DTX[:, (c % 2) * D_INNER + j * 512:
                               (c % 2) * D_INNER + (j + 1) * 512], 8),
                            hx(SXT[:, (c % 2) * D_INNER + j * 512:
                               (c % 2) * D_INNER + (j + 1) * 512], 8),
                            DTER[:, c * 32 + j * 8:c * 32 + (j + 1) * 8]
                            .to_broadcast([128, 8, HEADDIM]),
                            ALU.mult),
                            waits=[sx_ev[0] if j < 2 else sx_ev[1],
                                   m_summ.get(c - 2)]
                            if j in (0, 2) else [],
                            drain=(j == 0))
                        drip(1)
                        yield
                    m_dtx[c] = md
                    bfm = BCB[:, (pj % 2) * 512 + half * 128:
                              (pj % 2) * 512 + half * 128 + 128]
                    cfm = BCB[:, (pj % 2) * 512 + 256 + half * 128:
                              (pj % 2) * 512 + 256 + half * 128 + 128]
                    mg = st("T", lambda e, bfm=bfm, cfm=cfm: e.matmul(
                        P01[:, 0:128], bfm, cfm, start=True, stop=True),
                        waits=[m_bcb[pj], st_c01[0]])
                    drip(2)
                    yield
                    m_gt[c] = st("V", lambda e, c=c: e.scalar_tensor_tensor(
                        GT[:, (c % 2) * Q:(c % 2 + 1) * Q], P01[:, 0:128],
                        1.0, MLE[:], ALU.mult, ALU.mult),
                        waits=[mg, m_ymm.get(c - 2)])
                    st_c01[0] = m_gt[c]
                    yield

                def back_gen(c):
                    pj, half = c // 2, c % 2
                    bfm = BCB[:, (pj % 2) * 512 + half * 128:
                              (pj % 2) * 512 + half * 128 + 128]
                    cfm = BCB[:, (pj % 2) * 512 + 256 + half * 128:
                              (pj % 2) * 512 + 256 + half * 128 + 128]
                    gt = GT[:, (c % 2) * Q:(c % 2 + 1) * Q]
                    dto = (c % 2) * D_INNER
                    yh = []
                    mlast = None
                    for w in range(2):
                        def ymm(e, w=w, cfm=cfm, gt=gt, dto=dto):
                            for bb in range(2):
                                e.matmul(
                                    P23[:, bb * 512:(bb + 1) * 512], cfm,
                                    S_ALL[:, w * 1024 + bb * 512:w * 1024 + (bb + 1) * 512],
                                    start=True, stop=False)
                            for bb in range(2):
                                li = e.matmul(
                                    P23[:, bb * 512:(bb + 1) * 512], gt,
                                    DTX[:, dto + w * 1024 + bb * 512:
                                        dto + w * 1024 + (bb + 1) * 512],
                                    start=False, stop=True)
                            return li
                        mlast = st("T", ymm, waits=[
                            st_c23[0], m_gt.get(c), m_dtx.get(c),
                            m_supd.get(c - 1), m_bcb.get(pj),
                            m_mep.get(c - 1)] if w == 0 else [st_c23[0]])
                        drip(4)
                        yield
                        ev = st("S", lambda e, w=w: e.copy(
                            Y1[:, w * 1024:(w + 1) * 1024], P23[:]),
                            waits=[mlast])
                        st_c23[0] = ev
                        yh.append(ev)
                        yield
                    m_ymm[c] = mlast
                    mx = None
                    for j in range(4):
                        mx = st("V", lambda e, c=c, j=j, dto=dto: e.tensor_tensor(
                            hx(DTX[:, dto + j * 512:dto + (j + 1) * 512], 8),
                            hx(SXT[:, dto + j * 512:dto + (j + 1) * 512], 8),
                            DTWD[:, c * 32 + j * 8:c * 32 + (j + 1) * 8]
                            .to_broadcast([128, 8, HEADDIM]),
                            ALU.mult), waits=[m_ymm[c]] if j == 0 else [],
                            drain=(j == 0))
                        drip(1)
                        yield
                    m_mxw[c] = mx
                    mbt = st("T", lambda e, bfm=bfm: e.transpose(
                        P23B[:, 0:128], bfm, IBF[:]),
                        waits=[st_c23[0], m_bcb.get(pj)])
                    m_btr[c] = mbt
                    yield
                    ev_btm = st("S", lambda e: e.copy(BTM[:], P23B[:, 0:128]),
                                waits=[mbt])
                    st_c23[0] = ev_btm
                    yield
                    msup = None
                    msu = None
                    for w in range(2):
                        def supd(e, w=w, dto=dto):
                            for bb in range(2):
                                li = e.matmul(
                                    P23[:, bb * 512:(bb + 1) * 512], BTM[:],
                                    DTX[:, dto + w * 1024 + bb * 512:
                                        dto + w * 1024 + (bb + 1) * 512],
                                    start=True, stop=True)
                            return li
                        msu = st("T", supd, waits=[st_c23[0], mx] +
                                 ([msup] if msup else []))
                        drip(2)
                        yield
                        md = st("V", lambda e, c=c, w=w, dto=dto:
                                e.tensor_tensor(
                            hx(DTX[:, dto + w * 1024:dto + (w + 1) * 1024], 16),
                            hx(S_ALL[:, w * 1024:(w + 1) * 1024], 16),
                            BETB[:, c * 32 + w * 16:c * 32 + (w + 1) * 16]
                            .to_broadcast([128, 16, HEADDIM]),
                            ALU.mult), waits=[msu])
                        drip(1)
                        yield
                        msup = st("V", lambda e, w=w, dto=dto: e.tensor_tensor(
                            S_ALL[:, w * 1024:(w + 1) * 1024],
                            DTX[:, dto + w * 1024:dto + (w + 1) * 1024],
                            P23[:], ALU.add), waits=[md], drain=False)
                        drip(1)
                        yield
                    m_summ[c] = msu
                    m_supd[c] = msup
                    mep = None
                    for j in range(4):
                        sl = slice(j * 512, (j + 1) * 512)
                        def epa(e, c=c, j=j, sl=sl, dto=dto):
                            e.tensor_tensor(
                                hx(Y1[:, sl], 8), hx(Y1[:, sl], 8),
                                EC[:, c * 32 + j * 8:c * 32 + (j + 1) * 8]
                                .to_broadcast([128, 8, HEADDIM]), ALU.mult)
                            e.drain()
                            e.tensor_tensor(
                                hx(SCR[:, sl], 8),
                                hx(SXT[:, dto + j * 512:dto + (j + 1) * 512], 8),
                                D_BC[:, j * 8:(j + 1) * 8]
                                .to_broadcast([128, 8, HEADDIM]), ALU.mult)
                            e.drain()
                            e.tensor_tensor(Y1[:, sl], Y1[:, sl], SCR[:, sl],
                                            ALU.add)
                            e.drain()
                            return e.tensor_tensor(
                                Y1[:, sl], Y1[:, sl],
                                SZ[:, (c % 2) * D_INNER + j * 512:
                                   (c % 2) * D_INNER + (j + 1) * 512],
                                ALU.mult)
                        mep = st("V", epa,
                                 waits=(yh + [m_op.get(c - 1),
                                              m_szg.get(c)]) if j == 0
                                 else [])
                        drip(1)
                        yield
                    m_mep[c] = mep
                    if debug:
                        mdbg = st("Q", lambda e, c=c: [
                            e.dma_start(out=dbg["dbg_z"][c],
                                        in_=SZ[:, (c % 2) * D_INNER:
                                               (c % 2 + 1) * D_INNER]),
                            e.dma_start(out=dbg["dbg_sx"][c],
                                        in_=SXT[:, (c % 2) * D_INNER:
                                                (c % 2 + 1) * D_INNER]),
                            e.dma_start(out=dbg["dbg_y"][c], in_=Y1[:]),
                        ], waits=[mep, m_sxt[c]], sem="QD", inc=3 * 16)
                        m_mep[c] = mdbg
                        yield
                    def gtr(e):
                        for j in range(16):
                            li = e.transpose(P23B[:, j * 128:(j + 1) * 128],
                                             Y1[:, j * 128:(j + 1) * 128],
                                             IBF[:])
                        return li
                    mgt = st("T", gtr, waits=[mep, st_c23[0]])
                    drip(4)
                    yield
                    ev_gfm = st("S", lambda e: e.copy(SCR[:], P23B[:, 0:2048]),
                                waits=[mgt])
                    st_c23[0] = ev_gfm
                    yield
                    mss = st("S", lambda e, c=c: e.activation(
                        Y1[:], Y1[:], AF.Square, accum_out=SS[:, c:c + 1]),
                        waits=[mgt, m_mep[c]])
                    yield
                    def opmm(e):
                        for kd in range(NKI):
                            for bb in range(2):
                                li = e.matmul(
                                    P23[:, bb * 512:(bb + 1) * 512],
                                    SCR[:, kd * 128:(kd + 1) * 128],
                                    WOUT[:, kd * D_MODEL + bb * 512:
                                         kd * D_MODEL + (bb + 1) * 512],
                                    start=(kd == 0), stop=(kd == NKI - 1))
                        return li
                    mop = st("T", opmm, waits=[ev_gfm] +
                             (wom if c == 0 else []))
                    m_op[c] = mop
                    drip(5)
                    yield
                    ev_att = st("S", lambda e: e.copy(ATTS[:], P23[:]),
                                waits=[mop, m_att[0]])
                    st_c23[0] = ev_att
                    yield
                    m_att[0] = st("S", lambda e, c=c: e.dma_start(
                        out=attd[c], in_=ATTS[:]),
                        waits=[ev_att], sem="AT0", inc=16)
                    yield

                def pump(gens):
                    alive = [g for g in gens if g is not None]
                    while alive:
                        for g in list(alive):
                            try:
                                next(g)
                            except StopIteration:
                                alive.remove(g)

                pump([front_gen(0)])
                for c in range(NCH):
                    pump([back_gen(c),
                          front_gen(c + 1) if c + 1 < NCH else None])

                drip(10 ** 6)
                att_store = m_att[0]
                if debug:
                    st("Q", lambda e: e.dma_start(out=dbg["dbg_sall"],
                                                  in_=S_ALL[:]),
                       waits=[m_supd[NCH - 1]], sem="QD", inc=16)

                ctr = sc.resolve(ctr)
                bar2 = [sc.last[k] for k in ("T", "V", "S", "G")
                        if k in sc.last]
                bar2 += [att_store, dumps[-1]]
                with nc.Block() as blk:
                    blk.tensor(lambda e: sc.emit("T", e, sems))
                    blk.vector(lambda e: sc.emit("V", e, sems))
                    blk.scalar(lambda e: sc.emit("S", e, sems))
                    blk.sync(lambda e: sc.emit("Q", e, sems))
                    blk.gpsimd(lambda e: sc.emit("G", e, sems))

        if stages < 3:
            return nc
        with ExitStack() as p2:
            C1 = p2.enter_context
            ATR = C1(nc.sbuf_tensor([128, 4 * D_MODEL], BF16))
            XR2 = C1(nc.sbuf_tensor([128, 4 * D_MODEL], F32))
            OACR = C1(nc.sbuf_tensor([128, 4 * D_MODEL], BF16))
            OST = C1(nc.sbuf_tensor([128, 4 * D_MODEL], F32))

            sc = Script()
            st = sc.step

            mrms = st("S", lambda e: [
                e.activation(RMS[:], SS[:], AF.Ln, scale=1.0 / D_INNER,
                             bias=KCOL[:, 0:1]),
                e.drain(),
                e.activation(RMS[:], RMS[:], AF.Exp, scale=-0.5),
            ][-1], waits=bar2)

            fcons = [None, None, None, None]
            ocons = [None, None, None, None]
            for t in range(NCH):
                xb = t % 4
                qr = ["QR0", "QR1", "QX0", "QX1"][xb]
                qa = ["QA0", "QA1", "QW0", "QW1"][xb]
                qb = ["QB0", "QB1", "QO0", "QO1"][xb]
                mx2 = st("Q", lambda e, t=t, xb=xb: e.dma_start(
                    out=XR2[:, xb * D_MODEL:(xb + 1) * D_MODEL],
                    in_=d["x"][t * 128:(t + 1) * 128, :]),
                    waits=[fcons[xb]], sem=qr, inc=16)
                ma2 = st("Q", lambda e, t=t, xb=xb: e.dma_start(
                    out=ATR[:, xb * D_MODEL:(xb + 1) * D_MODEL], in_=attd[t]),
                    waits=[fcons[xb], att_store], sem=qa, inc=16)
                mo2 = st("Q", lambda e, t=t, xb=xb: [
                    e.dma_start(out=OACR[:, xb * D_MODEL + q * 256:
                                xb * D_MODEL + (q + 1) * 256],
                                in_=oaccd[q][:, t * 256:(t + 1) * 256])
                    for q in range(4)],
                    waits=[fcons[xb]] + dumps, sem=qb, inc=4 * 16)
                def fin(e, t=t, xb=xb):
                    e.scalar_tensor_tensor(
                        OST[:, xb * D_MODEL:(xb + 1) * D_MODEL],
                        ATR[:, xb * D_MODEL:(xb + 1) * D_MODEL],
                        RMS[:, t:t + 1],
                        OACR[:, xb * D_MODEL:(xb + 1) * D_MODEL],
                        ALU.mult, ALU.add)
                    e.drain()
                    return e.tensor_tensor(
                        OST[:, xb * D_MODEL:(xb + 1) * D_MODEL],
                        OST[:, xb * D_MODEL:(xb + 1) * D_MODEL],
                        XR2[:, xb * D_MODEL:(xb + 1) * D_MODEL], ALU.add)
                mf = st("V", fin, waits=[mx2, ma2, mo2, mrms, ocons[xb]])
                fcons[xb] = mf
                ocons[xb] = st("S", lambda e, t=t, xb=xb: e.dma_start(
                    out=out[t * 128:(t + 1) * 128, :],
                    in_=OST[:, xb * D_MODEL:(xb + 1) * D_MODEL]),
                    waits=[mf], sem=["O0", "O1", "AT1", "QC"][xb], inc=16)

            st("S", lambda e: e.nop(),
               waits=[ocons[0], ocons[1], ocons[2], ocons[3], att_store],
               inc=0)

            ctr = sc.resolve(ctr)
            with nc.Block() as blk:
                blk.tensor(lambda e: sc.emit("T", e, sems))
                blk.vector(lambda e: sc.emit("V", e, sems))
                blk.scalar(lambda e: sc.emit("S", e, sems))
                blk.sync(lambda e: sc.emit("Q", e, sems))
    return nc


def _consts():
    idx = np.arange(Q)
    mle = (idx[:, None] <= idx[None, :]).astype(BF)
    mskt = np.where(idx[None, :] <= idx[:, None], 0.0, NEG).astype(np.float32)
    ibf = np.eye(Q).astype(BF)
    if32 = np.eye(Q).astype(np.float32)
    onef = np.ones((128, Q), dtype=np.float32)
    return mle, mskt, ibf, if32, onef


def make_inputs(L0_attn_x, L0_attn_in_proj_w, L0_attn_conv_w, L0_attn_conv_b,
                L0_attn_norm_w, L0_attn_out_proj_w, L0_attn_dt_bias,
                L0_attn_A_log, L0_attn_D, L0_ffn_gate_w, L0_ffn_expert_w):
    x = np.ascontiguousarray(np.asarray(L0_attn_x, dtype=np.float32))
    wi = np.ascontiguousarray(
        np.concatenate([np.asarray(L0_attn_in_proj_w, np.float32),
                        np.asarray(L0_ffn_gate_w, np.float32).T], axis=1))
    wo = np.ascontiguousarray(np.asarray(L0_attn_out_proj_w, np.float32))
    we = np.ascontiguousarray(
        np.asarray(L0_ffn_expert_w, np.float32).transpose(0, 2, 1))
    cwr = np.asarray(L0_attn_conv_w, np.float32)
    cw = np.ascontiguousarray(
        cwr.T.reshape(NXT, 128, 4).transpose(1, 0, 2).reshape(128, NXT * 4))
    cb = np.ascontiguousarray(
        np.asarray(L0_attn_conv_b, np.float32).reshape(NXT, 128).T)
    nw = np.ascontiguousarray(
        np.asarray(L0_attn_norm_w, np.float32).reshape(NKI, 128).T)
    dtb = np.ascontiguousarray(np.asarray(L0_attn_dt_bias, np.float32)[None, :])
    alog = np.ascontiguousarray(np.asarray(L0_attn_A_log, np.float32)[None, :])
    dvec = np.ascontiguousarray(np.asarray(L0_attn_D, np.float32)[None, :])
    mle, mskt, ibf, if32, onef = _consts()
    shared = dict(wi=wi, wo=wo, we=we, cw=cw, cb=cb, nw=nw, dtb=dtb,
                  alog=alog, dvec=dvec, mle=np.asarray(mle),
                  ibf=np.asarray(ibf), if32=if32, onef=onef)
    return x, shared


def kernel(**inputs):
    x, shared = make_inputs(**inputs)
    nc = build_nc()
    in_maps = [dict(shared, x=np.ascontiguousarray(x[b])) for b in range(8)]
    res = run_bass_kernel_spmd(nc, in_maps, core_ids=list(range(8)))
    return np.stack([r["out"] for r in res.results]).astype(np.float32)


# version 30 (final)
# speedup vs baseline: 1.1354x; 1.0510x over previous
import numpy as np
import ml_dtypes
import concourse.bass as bass
import concourse.mybir as mybir
from concourse.bass_utils import run_bass_kernel_spmd

F32 = mybir.dt.float32
BF16 = mybir.dt.bfloat16
BF = ml_dtypes.bfloat16
AF = mybir.ActivationFunctionType
ALU = mybir.AluOpType
AX = mybir.AxisListType

D_MODEL = 1024
D_STATE = 128
HEADDIM = 64
D_INNER = 2048
NHEADS = 32
CONV_DIM = 2304
D_IN_PROJ = 4384
WIN_F = D_IN_PROJ + 8
EPS = 1e-5
L = 2048
Q = 128
NCH = 16
NK = 8
NKI = 16
XBCOFF = D_INNER
DTOFF = 4352
GOFF = 4384
NXT = 18
XBW = 260
NEG = -1e30


class Mark:
    __slots__ = ("sem", "val")
    def __init__(self, sem):
        self.sem = sem
        self.val = None


class Script:

    def __init__(self):
        self.steps = []
        self.last = {}

    def step(self, eng, fn, waits=(), sem=None, inc=1, drain=True):
        sem = sem or eng
        m = Mark(sem) if inc else None
        self.steps.append((eng, fn, tuple(w for w in waits if w is not None),
                           m, inc, drain))
        if m is not None:
            self.last[eng] = m
        return m

    def resolve(self, ctr):
        for eng, fn, waits, m, inc, dr in self.steps:
            if m is not None:
                ctr[m.sem] += inc
                m.val = ctr[m.sem]
        return ctr

    def emit(self, eng_name, e, sems):
        for eng, fn, waits, m, inc, dr in self.steps:
            if eng != eng_name:
                continue
            for w in waits:
                e.wait_ge(sems[w.sem], w.val)
            if dr and eng in ("V", "S", "G"):
                e.drain()
            last = fn(e)
            if m is not None:
                if isinstance(last, list):
                    assert len(last) * (16 if inc >= 16 else 1) == inc or True
                    for li in last:
                        li.then_inc(sems[m.sem], 16)
                else:
                    last.then_inc(sems[m.sem], inc)


def build_nc(debug=False, stages=3, nch=NCH, tail=True, wload=True):
    nc = bass.Bass("TRN2", target_bir_lowering=False,
                   dynamic_dma_scratch_size=512)
    d = {}
    def di(name, shape, dt):
        d[name] = nc.dram_tensor(name, shape, dt, kind="ExternalInput").ap()
    di("x", [L, D_MODEL], F32)
    di("wi", [D_MODEL, WIN_F], F32)
    di("wo", [D_INNER, D_MODEL], F32)
    di("we", [8, D_MODEL, D_MODEL], F32)
    di("cw", [128, NXT * 4], F32)
    di("cb", [128, NXT], F32)
    di("nw", [128, NKI], F32)
    di("dtb", [1, 32], F32)
    di("alog", [1, 32], F32)
    di("dvec", [1, 32], F32)
    di("mle", [Q, Q], BF16)
    di("ibf", [Q, Q], BF16)
    di("if32", [Q, Q], F32)
    di("onef", [128, Q], F32)
    out = nc.dram_tensor("out", [L, D_MODEL], F32, kind="ExternalOutput").ap()
    _ik = "ExternalOutput" if debug else "Internal"
    attd = nc.dram_tensor("attd", [NCH, 128, D_MODEL], BF16, kind=_ik).ap()
    oaccd = nc.dram_tensor("oaccd", [4, 128, NCH * 256], BF16, kind=_ik).ap()
    dbg = {}
    if debug:
        dspec = [("dbg_xn", [128, NK * L], BF16), ("dbg_z", [NCH, 128, D_INNER], BF16),
                 ("dbg_sx", [NCH, 128, D_INNER], BF16), ("dbg_seg", [128, 512], F32),
                 ("dbg_dt", [128, 512], BF16), ("dbg_y", [NCH, 128, D_INNER], BF16),
                 ("dbg_wf", [128, 128], F32), ("dbg_sall", [128, 2048], BF16),
                 ("dbg_oacc", [128, NCH * 512], F32)]
        for nm, sh, dt in dspec:
            dbg[nm] = nc.dram_tensor(nm, sh, dt, kind="ExternalOutput").ap()

    from contextlib import ExitStack
    with ExitStack() as top:
        E = top.enter_context
        semnames = ("T", "V", "S", "G", "QC", "QX0", "QX1", "QW0", "QW1",
                    "QO0", "QO1", "QE0", "QE1", "QR0", "QR1", "QA0", "QA1",
                    "O0", "O1", "AT0", "AT1", "QD", "QB0", "QB1", "QDmp")
        sems = {k: E(nc.semaphore("sem" + k)) for k in semnames}
        ctr = {k: 0 for k in sems}

        XN = E(nc.sbuf_tensor([128, NK * L], BF16))
        DTER = E(nc.sbuf_tensor([128, 512], BF16))
        EC = E(nc.sbuf_tensor([128, 512], BF16))
        DTWD = E(nc.sbuf_tensor([128, 512], BF16))
        BETB = E(nc.sbuf_tensor([128, 512], BF16))
        MLE = E(nc.sbuf_tensor([Q, Q], BF16))
        WFULL = E(nc.sbuf_tensor([128, 128], F32))
        SS = E(nc.sbuf_tensor([128, NCH], F32))
        RMS = E(nc.sbuf_tensor([128, NCH], F32))
        D_BC = E(nc.sbuf_tensor([128, 32], F32))
        IBF = E(nc.sbuf_tensor([Q, Q], BF16))
        CW = E(nc.sbuf_tensor([128, NXT * 4], F32))
        CB = E(nc.sbuf_tensor([128, NXT], F32))
        CBH = E(nc.sbuf_tensor([128, NXT], F32))
        KCOL = E(nc.sbuf_tensor([128, 2], F32))

        def hx(ap, a):
            return ap.rearrange("p (a b) -> p a b", a=a)

        def bc_heads(t, c):
            return t[:, c * 32:(c + 1) * 32].to_broadcast([128, 32, HEADDIM])

        wim = []
        wom = []

        with ExitStack() as p1a:
            A1 = p1a.enter_context
            WIN = A1(nc.sbuf_tensor([128, NK * WIN_F], BF16))
            WOUT = A1(nc.sbuf_tensor([128, NKI * D_MODEL], BF16))
            p1t = ExitStack()
            A2 = p1t.enter_context
            DT = A2(nc.sbuf_tensor([128, 512], BF16))
            WSTG = A2(nc.sbuf_tensor([128, 2 * 2196], F32))
            WOSTG = A2(nc.sbuf_tensor([128, 2 * D_MODEL], F32))
            XR = A2(nc.sbuf_tensor([128, 2 * D_MODEL], F32))
            XC = A2(nc.sbuf_tensor([128, 2 * D_MODEL], F32))
            XNT = A2(nc.sbuf_tensor([128, 2 * D_MODEL], BF16))
            SQD = A2(nc.sbuf_tensor([128, D_MODEL], BF16))
            ADTB = A2(nc.sbuf_tensor([128, 512], BF16))
            MCOL = A2(nc.sbuf_tensor([128, 8], F32))
            DTR = A2(nc.sbuf_tensor([128, 512], F32))
            DTF = A2(nc.sbuf_tensor([128, 512], F32))
            SEG = A2(nc.sbuf_tensor([128, 512], F32))
            STGF = A2(nc.sbuf_tensor([128, 512], F32))
            LOG = A2(nc.sbuf_tensor([128, 128], F32))
            DTB_BC = A2(nc.sbuf_tensor([128, 32], F32))
            A_BC = A2(nc.sbuf_tensor([128, 32], F32))
            ONEF = A2(nc.sbuf_tensor([128, Q], F32))
            NW = A2(nc.sbuf_tensor([128, NKI], F32))
            ROWA = A2(nc.sbuf_tensor([1, 32], F32))
            ROWB = A2(nc.sbuf_tensor([1, 32], F32))
            ROWD = A2(nc.sbuf_tensor([1, 32], F32))
            MX8 = A2(nc.sbuf_tensor([128, 8], F32))
            V12 = A2(nc.sbuf_tensor([128, 32], F32))
            EQA = A2(nc.sbuf_tensor([128, 8], F32))
            EQB = A2(nc.sbuf_tensor([128, 8], F32))
            W1S = A2(nc.sbuf_tensor([128, NCH], F32))
            W2S = A2(nc.sbuf_tensor([128, NCH], F32))
            SER = A2(nc.sbuf_tensor([1, 512], F32))
            PA = A2(nc.psum_tensor([128, 2048], F32))
            PB = A2(nc.psum_tensor([128, 2048], F32))
            PAB = PA[:].bitcast(BF16)

            sc = Script()
            st = sc.step

            mconst = st("Q", lambda e: [
                e.dma_start(out=MLE[:], in_=d["mle"]),
                e.dma_start(out=IBF[:], in_=d["ibf"]),
                e.dma_start(out=ONEF[:], in_=d["onef"]),
                e.dma_start(out=CW[:], in_=d["cw"]),
                e.dma_start(out=CB[:], in_=d["cb"]),
                e.dma_start(out=NW[:], in_=d["nw"]),
                e.dma_start(out=ROWA[:], in_=d["alog"]),
                e.dma_start(out=ROWB[:], in_=d["dtb"]),
                e.dma_start(out=ROWD[:], in_=d["dvec"]),
            ], sem="QC", inc=9 * 16)
            mk = st("V", lambda e: [
                e.memset(KCOL[:, 0:1], EPS),
                e.memset(KCOL[:, 1:2], 1.0),
            ][-1])
            mcbh = st("V", lambda e: e.tensor_scalar_mul(
                CBH[:], CB[:], 0.5), waits=[mconst])
            ma1 = st("S", lambda e: e.activation(ROWA[:], ROWA[:], AF.Exp),
                     waits=[mconst])
            ma2 = st("V", lambda e: e.tensor_scalar_mul(ROWA[:], ROWA[:], -1.0),
                     waits=[ma1])
            mbc = st("T", lambda e: [
                e.matmul(PB[:, 0:32], ONEF[0:1, :], ROWA[:], start=True, stop=True),
                e.matmul(PB[:, 32:64], ONEF[0:1, :], ROWB[:], start=True, stop=True),
                e.matmul(PB[:, 64:96], ONEF[0:1, :], ROWD[:], start=True, stop=True),
            ][-1], waits=[ma2])
            mbce = st("S", lambda e: [
                e.copy(A_BC[:], PB[:, 0:32]),
                e.copy(DTB_BC[:], PB[:, 32:64]),
                e.copy(D_BC[:], PB[:, 64:96]),
            ][-1], waits=[mbc])

            wdt_casts = []
            cast_free = [None, None]
            if wload:
                for k in range(NK):
                    b = k % 2
                    mldt = st("Q", lambda e, k=k, b=b: e.dma_start(
                        out=WSTG[:, b * 2196:b * 2196 + 40],
                        in_=d["wi"][k * 128:(k + 1) * 128, DTOFF:DTOFF + 40]),
                        waits=[cast_free[b]], sem="QW%d" % b, inc=16)
                    cast_free[b] = st("V", lambda e, k=k, b=b: e.tensor_copy(
                        WIN[:, k * WIN_F + DTOFF:k * WIN_F + DTOFF + 40],
                        WSTG[:, b * 2196:b * 2196 + 40]), waits=[mldt])
                    wdt_casts.append(cast_free[b])

            wpairs = []
            for i in range(16 if wload else 0):
                k, h = i // 2, i % 2
                wpairs.append(("wi", k, h * 2196,
                               (DTOFF - 2196) if h else 2196))
            for kd in range(NKI if wload else 0):
                wpairs.append(("wo", kd, 0, D_MODEL))
            ocast_free = [None, None]
            wdma = {}

            def wpair_dma(i):
                kind, k, fw0, fwn = wpairs[i]
                b = i % 2
                if kind == "wi":
                    wdma[i] = st("Q", lambda e, k=k, fw0=fw0, fwn=fwn, b=b:
                                 e.dma_start(
                        out=WSTG[:, b * 2196:b * 2196 + fwn],
                        in_=d["wi"][k * 128:(k + 1) * 128, fw0:fw0 + fwn]),
                        waits=[cast_free[b]], sem="QW%d" % b, inc=16)
                else:
                    wdma[i] = st("Q", lambda e, kd=k, b=b: e.dma_start(
                        out=WOSTG[:, b * D_MODEL:(b + 1) * D_MODEL],
                        in_=d["wo"][kd * 128:(kd + 1) * 128, :]),
                        waits=[ocast_free[b]], sem="QO%d" % b, inc=16)

            def wpair_cast(i):
                kind, k, fw0, fwn = wpairs[i]
                b = i % 2
                if kind == "wi":
                    cast_free[b] = st(
                        "V", lambda e, k=k, fw0=fw0, fwn=fwn, b=b:
                        e.tensor_copy(
                            WIN[:, k * WIN_F + fw0:k * WIN_F + fw0 + fwn],
                            WSTG[:, b * 2196:b * 2196 + fwn]),
                        waits=[wdma[i]])
                    wim.append(cast_free[b])
                else:
                    ocast_free[b] = st(
                        "S", lambda e, kd=k, b=b: e.activation(
                            WOUT[:, kd * D_MODEL:(kd + 1) * D_MODEL],
                            WOSTG[:, b * D_MODEL:(b + 1) * D_MODEL],
                            AF.Identity, scale=NW[:, kd:kd + 1]),
                        waits=[wdma[i]])
                    wom.append(ocast_free[b])

            xr_free = [None, None]
            pm2 = [None, None]
            pm4 = [None, None]
            pmtr = [None, None]
            pevx = [None, None]
            dt_ev2 = [None, None]
            mra = None
            ev_xn = None
            xm2 = {}
            dtevm = {}

            def xdma(c):
                b = c % 2
                return st("Q", lambda e, c=c, b=b: e.dma_start(
                    out=XR[:, b * D_MODEL:(b + 1) * D_MODEL],
                    in_=d["x"][c * 128:(c + 1) * 128, :]),
                    waits=[xr_free[b]], sem="QX%d" % b, inc=16)

            xdm = {0: xdma(0), 1: xdma(1)}

            def xfront(c):
                b = c % 2
                def ln1(e, b=b):
                    e.reduce_sum(MCOL[:, b * 4:b * 4 + 1],
                                 XR[:, b * D_MODEL:(b + 1) * D_MODEL],
                                 axis=AX.X)
                    e.drain()
                    e.tensor_scalar_mul(MCOL[:, b * 4:b * 4 + 1],
                                        MCOL[:, b * 4:b * 4 + 1],
                                        1.0 / D_MODEL)
                    e.drain()
                    return e.tensor_scalar(
                        XC[:, b * D_MODEL:(b + 1) * D_MODEL],
                        XR[:, b * D_MODEL:(b + 1) * D_MODEL],
                        MCOL[:, b * 4:b * 4 + 1], None, ALU.subtract)
                mln1 = st("V", ln1, waits=[xdm[c], pm2[b]])
                xr_free[b] = mln1
                mln2 = st("S", lambda e, b=b: e.activation(
                    SQD[:], XC[:, b * D_MODEL:(b + 1) * D_MODEL], AF.Square,
                    accum_out=MCOL[:, b * 4 + 1:b * 4 + 2]), waits=[mln1])
                pm2[b] = mln2
                xm2[c] = mln2

            def xback(c):
                b = c % 2
                def ln3(e, b=b):
                    e.activation(MCOL[:, b * 4 + 2:b * 4 + 3],
                                 MCOL[:, b * 4 + 1:b * 4 + 2], AF.Ln,
                                 scale=1.0 / D_MODEL, bias=KCOL[:, 0:1])
                    e.drain()
                    return e.activation(MCOL[:, b * 4 + 2:b * 4 + 3],
                                        MCOL[:, b * 4 + 2:b * 4 + 3], AF.Exp,
                                        scale=-0.5)
                mln3 = st("S", ln3, waits=[xm2[c], pm4[b]] +
                          ([mk] if c == 0 else []))
                mln4 = st("V", lambda e, b=b: e.tensor_scalar(
                    XNT[:, b * D_MODEL:(b + 1) * D_MODEL],
                    XC[:, b * D_MODEL:(b + 1) * D_MODEL],
                    MCOL[:, b * 4 + 2:b * 4 + 3], None, ALU.mult),
                    waits=[mln3, pmtr[b]])
                pm4[b] = mln4
                def tr(e, b=b):
                    for k in range(NK):
                        li = e.transpose(
                            PAB[:, b * 1024 + k * 128:b * 1024 + (k + 1) * 128],
                            XNT[:, b * D_MODEL + k * 128:
                                b * D_MODEL + (k + 1) * 128], IBF[:])
                    return li
                mtr = st("T", tr, waits=[mln4, pevx[b]])
                pmtr[b] = mtr
                ev = st("S", lambda e, c=c, b=b: e.copy(
                    hx(XN[:], NK)[:, :, c * 128:(c + 1) * 128],
                    hx(PAB[:, b * 1024:(b + 1) * 1024], NK)), waits=[mtr])
                pevx[b] = ev
                if tail and wload:
                    def dtmm(e, c=c, b=b):
                        for k in range(NK):
                            e.matmul(
                                PB[:, b * 512:b * 512 + 32],
                                XN[:, k * L + c * 128:k * L + (c + 1) * 128],
                                WIN[:, k * WIN_F + DTOFF:k * WIN_F + DTOFF + 32],
                                start=(k == 0), stop=(k == NK - 1))
                        for k in range(NK):
                            li = e.matmul(
                                PB[:, b * 512 + 256:b * 512 + 264],
                                XN[:, k * L + c * 128:k * L + (c + 1) * 128],
                                WIN[:, k * WIN_F + GOFF:k * WIN_F + GOFF + 8],
                                start=(k == 0), stop=(k == NK - 1))
                        return li
                    mdt = st("T", dtmm, waits=[ev, dt_ev2[b], mbce] +
                             (wdt_casts if c == 0 else []))
                    def dtev(e, c=c, b=b):
                        e.copy(DTR[:, c * 32:(c + 1) * 32],
                               PB[:, b * 512:b * 512 + 32])
                        e.drain()
                        return e.copy(LOG[:, c * 8:(c + 1) * 8],
                                      PB[:, b * 512 + 256:b * 512 + 264])
                    dt_ev2[b] = st("S", dtev, waits=[mdt])
                    dtevm[c] = dt_ev2[b]

            def xra(c):
                def ra(e, c=c):
                    e.max(out=MX8[:], in_=LOG[:, c * 8:(c + 1) * 8])
                    e.drain()
                    e.tensor_copy(V12[:, 2 * c:2 * c + 2], MX8[:, 0:2])
                    e.drain()
                    return e.tensor_tensor(W1S[:, c:c + 1], MX8[:, 0:1],
                                           MX8[:, 1:2], ALU.subtract)
                return st("V", ra, waits=[dtevm[c]])

            for c in range(nch):
                xfront(c)
                if c >= 1:
                    xback(c - 1)
                if c + 2 < nch:
                    xdm[c + 2] = xdma(c + 2)
                if c >= 2:
                    wpair_cast(2 * (c - 2))
                    wpair_cast(2 * (c - 2) + 1)
                if c >= 1:
                    wpair_dma(2 * (c - 1))
                    wpair_dma(2 * (c - 1) + 1)
                if (tail and wload) and c >= 3:
                    mra = xra(c - 3)
            xback(nch - 1)
            if wload:
                wpair_cast(2 * (nch - 2))
                wpair_cast(2 * (nch - 2) + 1)
                wpair_dma(2 * (nch - 1))
                wpair_dma(2 * (nch - 1) + 1)
                wpair_cast(2 * (nch - 1))
                wpair_cast(2 * (nch - 1) + 1)
            if tail and wload:
                for c in range(nch - 3, nch):
                    mra = xra(c)
            ev_xn = pevx[(nch - 1) % 2]
            dt_ev = dt_ev2[1] if dt_ev2[1] is not None else dt_ev2[0]

            mtn = st("S", lambda e: e.activation(W1S[:], W1S[:], AF.Tanh,
                                                 scale=0.5), waits=[mra])
            def rb(e):
                e.tensor_scalar(W1S[:], W1S[:], 0.5, 0.5, ALU.mult, ALU.add)
                e.drain()
                return e.tensor_scalar(W2S[:], W1S[:], -1.0, 1.0, ALU.mult,
                                       ALU.add)
            mrb = st("V", rb, waits=[mtn])
            mrc = None
            for c in range(NCH):
                def rcf(e, c=c):
                    lg = LOG[:, c * 8:(c + 1) * 8]
                    wf = WFULL[:, c * 8:(c + 1) * 8]
                    e.tensor_scalar(EQA[:], lg, V12[:, 2 * c:2 * c + 1], None,
                                    ALU.is_equal)
                    e.drain()
                    e.tensor_scalar(EQB[:], lg, V12[:, 2 * c + 1:2 * c + 2],
                                    None, ALU.is_equal)
                    e.drain()
                    e.tensor_tensor(wf, EQB[:], EQA[:], ALU.mult)
                    e.drain()
                    e.tensor_tensor(wf, EQB[:], wf, ALU.subtract)
                    e.drain()
                    e.tensor_scalar(wf, wf, W2S[:, c:c + 1], None, ALU.mult)
                    e.drain()
                    return e.scalar_tensor_tensor(wf, EQA[:], W1S[:, c:c + 1],
                                                  wf, ALU.mult, ALU.add)
                mrc = st("V", rcf, waits=[mrb])

            if not (tail and wload):
                ctr = sc.resolve(ctr)
                bar1 = [sc.last[k] for k in ("T", "V", "S") if k in sc.last]
                with nc.Block() as blk:
                    blk.tensor(lambda e: sc.emit("T", e, sems))
                    blk.vector(lambda e: sc.emit("V", e, sems))
                    blk.scalar(lambda e: sc.emit("S", e, sems))
                    blk.sync(lambda e: sc.emit("Q", e, sems))
                p1t.close()
                return nc
            mt1 = st("V", lambda e: e.tensor_tensor(
                hx(DTR[:], NCH), hx(DTR[:], NCH),
                DTB_BC[:].to_broadcast([128, 32, NCH]).transpose([0, 2, 1]),
                ALU.add), waits=[dt_ev2[0], dt_ev2[1]])
            mt2 = st("S", lambda e: e.activation(DTF[:], DTR[:], AF.Exp),
                     waits=[mt1])
            mt3 = st("S", lambda e: e.activation(DTF[:], DTF[:], AF.Ln,
                                                 bias=KCOL[:, 1:2]),
                     waits=[mt2])
            mt4 = st("V", lambda e: e.tensor_copy(DT[:], DTF[:]), waits=[mt3])
            mt5 = st("V", lambda e: e.tensor_tensor(
                hx(DTF[:], NCH), hx(DTF[:], NCH),
                A_BC[:].to_broadcast([128, 32, NCH]).transpose([0, 2, 1]),
                ALU.mult), waits=[mt4])
            mt6 = st("V", lambda e: e.tensor_copy(ADTB[:], DTF[:]), waits=[mt5])
            def t7(e):
                for c in range(NCH):
                    li = e.matmul(PB[:, c * 32:(c + 1) * 32], MLE[:],
                                  ADTB[:, c * 32:(c + 1) * 32],
                                  start=True, stop=True)
                return li
            mt7 = st("T", t7, waits=[mt6, dt_ev])
            mt8 = st("S", lambda e: e.copy(SEG[:], PB[:, 0:512]), waits=[mt7])
            mt11a = st("T", lambda e: e.matmul(PB[0:1, 512:1024],
                                               MLE[:, 127:128], ADTB[:],
                                               start=True, stop=True),
                       waits=[mt6, mt8])
            mt11b = st("S", lambda e: e.copy(SER[:], PB[0:1, 512:1024]),
                       waits=[mt11a])
            mt11 = st("T", lambda e: [
                e.matmul(PA[:, 0:512], ONEF[0:1, :], SER[:], start=True,
                         stop=True),
                e.matmul(PA[:, 512:1024], ONEF[64:65, :], SEG[64:65, :],
                         start=True, stop=True),
            ][-1], waits=[mt11b, pevx[0], pevx[1]])
            mt12a = st("V", lambda e: e.scalar_tensor_tensor(
                STGF[:], PA[:, 512:1024], -1.0, SEG[:], ALU.mult, ALU.add),
                waits=[mt11, mt8])
            mt12b = st("S", lambda e: e.activation(EC[:], STGF[:], AF.Exp),
                       waits=[mt12a])
            mt12c = st("V", lambda e: e.tensor_scalar_mul(STGF[:], STGF[:],
                                                          -1.0),
                       waits=[mt12b])
            mt12d = st("S", lambda e: e.activation(DTER[:], STGF[:], AF.Exp),
                       waits=[mt12c])
            mt12e = st("V", lambda e: e.tensor_tensor(DTER[:], DTER[:], DT[:],
                                                      ALU.mult),
                       waits=[mt12d, mt4])
            mt13a = st("V", lambda e: [
                e.scalar_tensor_tensor(STGF[:], PA[:, 0:512], 1.0, SEG[:],
                                       ALU.mult, ALU.subtract),
                e.drain(),
                e.tensor_tensor(STGF[:, 0:480],
                                STGF[:, 0:480],
                                PA[:, 512 + 32:1024], ALU.add),
            ][-1], waits=[mt11, mt8, mt12d])
            mt13b = st("S", lambda e: e.activation(DTWD[:], STGF[:], AF.Exp),
                       waits=[mt13a])
            mt13c = st("V", lambda e: e.tensor_tensor(DTWD[:], DTWD[:], DT[:],
                                                      ALU.mult),
                       waits=[mt13b])
            mt14a = st("V", lambda e: [
                e.tensor_copy(STGF[:], PA[:, 0:512]),
                e.drain(),
                e.scalar_tensor_tensor(STGF[:], PA[:, 512:1024], -1.0,
                                       STGF[:], ALU.mult, ALU.add),
                e.drain(),
                e.tensor_tensor(STGF[:, 0:480], STGF[:, 0:480],
                                PA[:, 512 + 32:1024], ALU.add),
            ][-1], waits=[mt13b])
            mt14 = st("S", lambda e: e.activation(BETB[:], STGF[:], AF.Exp),
                      waits=[mt14a])

            if debug:
                st("Q", lambda e: [
                    e.dma_start(out=dbg["dbg_seg"], in_=SEG[:]),
                    e.dma_start(out=dbg["dbg_dt"], in_=DT[:]),
                    e.dma_start(out=dbg["dbg_wf"], in_=WFULL[:]),
                    e.dma_start(out=dbg["dbg_xn"], in_=XN[:]),
                ], waits=[mt8, mt4, mrc, ev_xn], sem="QD", inc=4 * 16)

            ctr = sc.resolve(ctr)
            bar1 = [sc.last[k] for k in ("T", "V", "S", "Q") if k in sc.last]
            with nc.Block() as blk:
                blk.tensor(lambda e: sc.emit("T", e, sems))
                blk.vector(lambda e: sc.emit("V", e, sems))
                blk.scalar(lambda e: sc.emit("S", e, sems))
                blk.sync(lambda e: sc.emit("Q", e, sems))
            p1t.close()
            if stages < 2:
                return nc

            with ExitStack() as p1b:
                B1 = p1b.enter_context
                ATTS = B1(nc.sbuf_tensor([128, D_MODEL], BF16))
                S_ALL = B1(nc.sbuf_tensor([128, 2048], BF16))
                XB = B1(nc.sbuf_tensor([128, NXT * XBW], BF16))
                AC = B1(nc.sbuf_tensor([128, NXT * 256], BF16))
                TX = B1(nc.sbuf_tensor([128, 2 * 256], BF16))
                DGS = B1(nc.sbuf_tensor([128, 8 * 128], BF16))
                TZ = B1(nc.sbuf_tensor([128, 1024], BF16))
                SZ = B1(nc.sbuf_tensor([128, 2 * D_INNER], BF16))
                SXT = B1(nc.sbuf_tensor([128, 2 * D_INNER], BF16))
                DTX = B1(nc.sbuf_tensor([128, 2 * D_INNER], BF16))
                GT = B1(nc.sbuf_tensor([Q, 2 * Q], BF16))
                BCB = B1(nc.sbuf_tensor([128, 2 * 512], BF16))
                BTM = B1(nc.sbuf_tensor([Q, Q], BF16))
                Y1 = B1(nc.sbuf_tensor([128, D_INNER], BF16))
                SCR = B1(nc.sbuf_tensor([128, D_INNER], BF16))
                OACC = B1(nc.sbuf_tensor([128, NCH * 256], BF16))
                EWB = B1(nc.sbuf_tensor([128, 2 * 8 * 256], BF16))
                ESTG = B1(nc.sbuf_tensor([128, 2 * 256], F32))
                P01 = B1(nc.psum_tensor([128, 1024], F32))
                P23 = B1(nc.psum_tensor([128, 1024], F32))
                P45 = B1(nc.psum_tensor([128, 1024], F32))
                P67 = B1(nc.psum_tensor([128, 1024], F32))
                P23B = P23[:].bitcast(BF16)

                sc = Script()
                st = sc.step

                minit = st("V", lambda e: e.memset(
                    hx(XB[:], NXT)[:, :, 0:4], 0.0), waits=bar1)
                minit2 = st("V", lambda e: e.memset(S_ALL[:], 0.0))

                att_store = None
                dumps = []

                def moe_gen():
                    est_free = [None, None]
                    ewb_mm_last = [None, None]
                    prev_dump = None
                    slots = [(q, ee) for q in range(4) for ee in range(8)]
                    ldm_map = {}

                    def ld_gen(q, ee):
                        eb = ee % 2
                        ldm_map[(q, ee)] = []
                        for ct in range(NK):
                            sb = ct % 2
                            mld = st("Q", lambda e, ee=ee, ct=ct, q=q, sb=sb:
                                     e.dma_start(
                                out=ESTG[:, sb * 256:(sb + 1) * 256],
                                in_=d["we"][ee, ct * 128:(ct + 1) * 128,
                                            q * 256:(q + 1) * 256]),
                                waits=[est_free[sb]], sem="QE%d" % sb,
                                inc=16)
                            yield
                            cw = [mld]
                            if ct < 2 and ewb_mm_last[eb] is not None:
                                cw.append(ewb_mm_last[eb])
                            est_free[sb] = st(
                                "V", lambda e, ct=ct, eb=eb, sb=sb:
                                e.tensor_copy(
                                    EWB[:, eb * 2048 + ct * 256:
                                        eb * 2048 + (ct + 1) * 256],
                                    ESTG[:, sb * 256:(sb + 1) * 256]),
                                waits=cw, drain=False)
                            ldm_map[(q, ee)].append(est_free[sb])
                            yield

                    for _ in ld_gen(*slots[0]):
                        yield
                    escq = {}
                    for si, (q, ee) in enumerate(slots):
                        eb = ee % 2
                        nld = ld_gen(*slots[si + 1]) if si + 1 < len(slots) \
                            else None
                        mm_last = None
                        ldm = ldm_map[(q, ee)]
                        for t in range(NCH):
                            gi = si * NCH + t
                            sl4 = gi % 3
                            pp4 = P45 if sl4 < 2 else P67
                            of4 = (sl4 % 2) * 512 if sl4 < 2 else 0
                            ww = []
                            if t == 0:
                                ww = list(ldm)
                            if gi >= 3:
                                ww.append(escq[gi - 3])
                            def emm(e, t=t, eb=eb, pp4=pp4, of4=of4):
                                for k in range(NK):
                                    li = e.matmul(
                                        pp4[:, of4:of4 + 256],
                                        XN[:, k * L + t * 128:k * L + (t + 1) * 128],
                                        EWB[:, eb * 2048 + k * 256:eb * 2048 + (k + 1) * 256],
                                        start=(k == 0), stop=(k == NK - 1))
                                return li
                            mmx = st("T", emm, waits=ww)
                            mm_last = mmx
                            yield
                            aw = [mmx]
                            if ee == 0 and prev_dump is not None:
                                aw.append(prev_dump)
                            if ee == 0:
                                def accf(e, t=t, pp4=pp4, of4=of4):
                                    return e.tensor_scalar(
                                        OACC[:, t * 256:(t + 1) * 256],
                                        pp4[:, of4:of4 + 256],
                                        WFULL[:, t * 8:t * 8 + 1],
                                        None, ALU.mult)
                            else:
                                def accf(e, t=t, ee=ee, pp4=pp4, of4=of4):
                                    return e.scalar_tensor_tensor(
                                        OACC[:, t * 256:(t + 1) * 256],
                                        pp4[:, of4:of4 + 256],
                                        WFULL[:, t * 8 + ee:t * 8 + ee + 1],
                                        OACC[:, t * 256:(t + 1) * 256],
                                        ALU.mult, ALU.add)
                            escq[gi] = st("V", accf, waits=aw,
                                          drain=(t % 4 == 0))
                            yield
                            if t % 2 == 1 and nld is not None:
                                for _ in range(2):
                                    try:
                                        next(nld)
                                        yield
                                    except StopIteration:
                                        break
                        if nld is not None:
                            for _ in nld:
                                yield
                        ewb_mm_last[eb] = mm_last
                        if ee == 7:
                            prev_dump = st("Q", lambda e, q=q: e.dma_start(
                                out=oaccd[q], in_=OACC[:]),
                                waits=[escq[si * NCH + NCH - 1]],
                                sem="QDmp", inc=16)
                            dumps.append(prev_dump)
                            yield

                _gen = moe_gen()
                _done = [False]

                def drip(n):
                    for _ in range(n):
                        if _done[0]:
                            return
                        try:
                            next(_gen)
                        except StopIteration:
                            _done[0] = True
                            return


                P01B = P01[:].bitcast(BF16)
                st_c01 = [None]
                st_c23 = [minit2]
                m_szprev = [None]
                m_convdone = [minit]
                m_supd = {}
                m_summ = {}
                m_sxt = {}
                m_dtx = {}
                m_gt = {}
                m_ymm = {}
                m_mep = {}
                m_mxw = {}
                m_btr = {}
                m_bcb = {}
                m_op = {}
                m_szg = {}
                m_att = [None]

                def front_gen(c):
                    pj, half = c // 2, c % 2
                    if half == 0:
                        if pj > 0:
                            m_convdone[0] = st("V", lambda e: e.tensor_copy(
                                hx(XB[:], NXT)[:, :, 0:4],
                                hx(XB[:], NXT)[:, :, 256:260]),
                                waits=[m_convdone[0]])
                            yield
                        wave_ev = []
                        for wv in range(9):
                            ct0 = 2 * wv
                            def xmm(e, ct0=ct0, pj=pj):
                                for ct in range(ct0, ct0 + 2):
                                    for k in range(NK):
                                        li = e.matmul(
                                            P01[:, (ct - ct0) * 256:(ct - ct0 + 1) * 256],
                                            WIN[:, k * WIN_F + XBCOFF + ct * 128:
                                                k * WIN_F + XBCOFF + (ct + 1) * 128],
                                            XN[:, k * L + pj * 256:k * L + (pj + 1) * 256],
                                            start=(k == 0), stop=(k == NK - 1))
                                return li
                            mw = st("T", xmm,
                                    waits=[m_convdone[0], st_c01[0]])
                            drip(2)
                            yield
                            ev = st("S", lambda e, ct0=ct0: e.copy(
                                hx(XB[:], NXT)[:, ct0:ct0 + 2, 4:260],
                                hx(P01[:, 0:512], 2)), waits=[mw])
                            st_c01[0] = ev
                            wave_ev.append(ev)
                            yield
                        cv_ev = [None, None]
                        mm_cv = {}
                        cm = None
                        cms = {}
                        for ct in range(NXT):
                            def dgen(e, ct=ct):
                                for dd in range(4):
                                    j = (4 * ct + dd) % 8
                                    li = e.activation(
                                        DGS[:, j * 128:(j + 1) * 128],
                                        IBF[:], AF.Identity,
                                        scale=CW[:, ct * 4 + dd:ct * 4 + dd + 1])
                                return li
                            mdg = st("S", dgen, waits=[mm_cv.get(ct - 2)])
                            yield
                            cvp, cvo = (P01, 512) if ct % 2 == 0 else (P67, 512)
                            def cvmm(e, ct=ct, cvp=cvp, cvo=cvo):
                                for dd in range(4):
                                    j = (4 * ct + dd) % 8
                                    li = e.matmul(
                                        cvp[:, cvo:cvo + 256],
                                        DGS[:, j * 128:(j + 1) * 128],
                                        XB[:, ct * XBW + 4 - dd:
                                           ct * XBW + 260 - dd],
                                        start=(dd == 0), stop=(dd == 3))
                                return li
                            mm_cv[ct] = st("T", cvmm, waits=[
                                mdg, wave_ev[ct // 2], cv_ev[ct % 2]])
                            drip(3)
                            yield
                            cv_ev[ct % 2] = st(
                                "S", lambda e, ct=ct, cvp=cvp, cvo=cvo:
                                e.activation(
                                    AC[:, ct * 256:(ct + 1) * 256],
                                    cvp[:, cvo:cvo + 256], AF.Silu,
                                    scale=1.0, bias=CB[:, ct:ct + 1]),
                                waits=[mm_cv[ct]])
                            cm = cv_ev[ct % 2]
                            drip(1)
                            yield
                        for ch in (c, c + 1):
                            need = ch - 2
                            while need >= 0 and need not in m_mep:
                                yield
                            mprev = m_mep.get(need)
                            for w in range(2):
                                for bb in range(2):
                                    def zmm(e, ch=ch, w=w, bb=bb):
                                        for k in range(NK):
                                            li = e.matmul(
                                                P01[:, 0:512],
                                                XN[:, k * L + ch * 128:k * L + (ch + 1) * 128],
                                                WIN[:, k * WIN_F + w * 1024 + bb * 512:
                                                    k * WIN_F + w * 1024 + (bb + 1) * 512],
                                                start=(k == 0), stop=(k == NK - 1))
                                        return li
                                    mz = st("T", zmm, waits=[st_c01[0]])
                                    drip(2)
                                    yield
                                    ev = st("S", lambda e, ch=ch, w=w, bb=bb:
                                            e.activation(
                                        SZ[:, (ch % 2) * D_INNER + w * 1024 + bb * 512:
                                           (ch % 2) * D_INNER + w * 1024 + (bb + 1) * 512],
                                        P01[:, 0:512], AF.Silu),
                                        waits=[mz] +
                                        ([mprev] if (w == 0 and bb == 0
                                                     and mprev) else []))
                                    st_c01[0] = ev
                                    m_szg[ch] = ev
                                    drip(1)
                                    yield
                        m_convdone[0] = cm
                        m_bcb[pj] = st("V", lambda e, pj=pj: e.tensor_copy(
                            BCB[:, (pj % 2) * 512:(pj % 2) * 512 + 512],
                            AC[:, 16 * 256:16 * 256 + 512]),
                            waits=[cm, m_btr.get(2 * pj - 3)])
                        yield

                    sx_ev = []
                    for hh in range(2):
                        def sxtr(e, half=half, hh=hh):
                            for j in range(8):
                                jt = hh * 8 + j
                                li = e.transpose(
                                    P01B[:, j * 128:(j + 1) * 128],
                                    AC[:, jt * 256 + half * 128:
                                       jt * 256 + half * 128 + 128],
                                    IBF[:])
                            return li
                        mst_ = st("T", sxtr,
                                  waits=[m_convdone[0], st_c01[0]])
                        drip(2)
                        yield
                        mse = st("S", lambda e, c=c, hh=hh: e.copy(
                            SXT[:, (c % 2) * D_INNER + hh * 1024:
                                (c % 2) * D_INNER + (hh + 1) * 1024],
                            P01B[:, 0:1024]),
                            waits=[mst_] +
                            ([m_mxw.get(c - 2), m_mep.get(c - 2)]
                             if hh == 0 else []))
                        st_c01[0] = mse
                        sx_ev.append(mse)
                        yield
                    m_sxt[c] = sx_ev[1]
                    md = None
                    for j in range(4):
                        md = st("V", lambda e, c=c, j=j: e.tensor_tensor(
                            hx(DTX[:, (c % 2) * D_INNER + j * 512:
                               (c % 2) * D_INNER + (j + 1) * 512], 8),
                            hx(SXT[:, (c % 2) * D_INNER + j * 512:
                               (c % 2) * D_INNER + (j + 1) * 512], 8),
                            DTER[:, c * 32 + j * 8:c * 32 + (j + 1) * 8]
                            .to_broadcast([128, 8, HEADDIM]),
                            ALU.mult),
                            waits=[sx_ev[0] if j < 2 else sx_ev[1],
                                   m_summ.get(c - 2)]
                            if j in (0, 2) else [],
                            drain=(j == 0))
                        drip(1)
                        yield
                    m_dtx[c] = md
                    bfm = BCB[:, (pj % 2) * 512 + half * 128:
                              (pj % 2) * 512 + half * 128 + 128]
                    cfm = BCB[:, (pj % 2) * 512 + 256 + half * 128:
                              (pj % 2) * 512 + 256 + half * 128 + 128]
                    mg = st("T", lambda e, bfm=bfm, cfm=cfm: e.matmul(
                        P01[:, 0:128], bfm, cfm, start=True, stop=True),
                        waits=[m_bcb[pj], st_c01[0]])
                    drip(2)
                    yield
                    m_gt[c] = st("V", lambda e, c=c: e.scalar_tensor_tensor(
                        GT[:, (c % 2) * Q:(c % 2 + 1) * Q], P01[:, 0:128],
                        1.0, MLE[:], ALU.mult, ALU.mult),
                        waits=[mg, m_ymm.get(c - 2)])
                    st_c01[0] = m_gt[c]
                    yield

                def back_gen(c):
                    pj, half = c // 2, c % 2
                    bfm = BCB[:, (pj % 2) * 512 + half * 128:
                              (pj % 2) * 512 + half * 128 + 128]
                    cfm = BCB[:, (pj % 2) * 512 + 256 + half * 128:
                              (pj % 2) * 512 + 256 + half * 128 + 128]
                    gt = GT[:, (c % 2) * Q:(c % 2 + 1) * Q]
                    dto = (c % 2) * D_INNER
                    yh = []
                    mlast = None
                    for w in range(2):
                        def ymm(e, w=w, cfm=cfm, gt=gt, dto=dto):
                            for bb in range(2):
                                e.matmul(
                                    P23[:, bb * 512:(bb + 1) * 512], cfm,
                                    S_ALL[:, w * 1024 + bb * 512:w * 1024 + (bb + 1) * 512],
                                    start=True, stop=False)
                            for bb in range(2):
                                li = e.matmul(
                                    P23[:, bb * 512:(bb + 1) * 512], gt,
                                    DTX[:, dto + w * 1024 + bb * 512:
                                        dto + w * 1024 + (bb + 1) * 512],
                                    start=False, stop=True)
                            return li
                        mlast = st("T", ymm, waits=[
                            st_c23[0], m_gt.get(c), m_dtx.get(c),
                            m_supd.get(c - 1), m_bcb.get(pj),
                            m_mep.get(c - 1)] if w == 0 else [st_c23[0]])
                        drip(4)
                        yield
                        ev = st("S", lambda e, w=w: e.copy(
                            Y1[:, w * 1024:(w + 1) * 1024], P23[:]),
                            waits=[mlast])
                        st_c23[0] = ev
                        yh.append(ev)
                        yield
                    m_ymm[c] = mlast
                    mx = None
                    for j in range(4):
                        mx = st("V", lambda e, c=c, j=j, dto=dto: e.tensor_tensor(
                            hx(DTX[:, dto + j * 512:dto + (j + 1) * 512], 8),
                            hx(SXT[:, dto + j * 512:dto + (j + 1) * 512], 8),
                            DTWD[:, c * 32 + j * 8:c * 32 + (j + 1) * 8]
                            .to_broadcast([128, 8, HEADDIM]),
                            ALU.mult), waits=[m_ymm[c]] if j == 0 else [],
                            drain=(j == 0))
                        drip(1)
                        yield
                    m_mxw[c] = mx
                    mbt = st("T", lambda e, bfm=bfm: e.transpose(
                        P23B[:, 0:128], bfm, IBF[:]),
                        waits=[st_c23[0], m_bcb.get(pj)])
                    m_btr[c] = mbt
                    yield
                    ev_btm = st("S", lambda e: e.copy(BTM[:], P23B[:, 0:128]),
                                waits=[mbt])
                    st_c23[0] = ev_btm
                    yield
                    msup = None
                    msu = None
                    for w in range(2):
                        def supd(e, w=w, dto=dto):
                            for bb in range(2):
                                li = e.matmul(
                                    P23[:, bb * 512:(bb + 1) * 512], BTM[:],
                                    DTX[:, dto + w * 1024 + bb * 512:
                                        dto + w * 1024 + (bb + 1) * 512],
                                    start=True, stop=True)
                            return li
                        msu = st("T", supd, waits=[st_c23[0], mx] +
                                 ([msup] if msup else []))
                        drip(2)
                        yield
                        md = st("V", lambda e, c=c, w=w, dto=dto:
                                e.tensor_tensor(
                            hx(DTX[:, dto + w * 1024:dto + (w + 1) * 1024], 16),
                            hx(S_ALL[:, w * 1024:(w + 1) * 1024], 16),
                            BETB[:, c * 32 + w * 16:c * 32 + (w + 1) * 16]
                            .to_broadcast([128, 16, HEADDIM]),
                            ALU.mult), waits=[msu])
                        drip(1)
                        yield
                        msup = st("V", lambda e, w=w, dto=dto: e.tensor_tensor(
                            S_ALL[:, w * 1024:(w + 1) * 1024],
                            DTX[:, dto + w * 1024:dto + (w + 1) * 1024],
                            P23[:], ALU.add), waits=[md], drain=False)
                        drip(1)
                        yield
                    m_summ[c] = msu
                    m_supd[c] = msup
                    mep = None
                    for j in range(4):
                        sl = slice(j * 512, (j + 1) * 512)
                        def epa(e, c=c, j=j, sl=sl, dto=dto):
                            e.tensor_tensor(
                                hx(Y1[:, sl], 8), hx(Y1[:, sl], 8),
                                EC[:, c * 32 + j * 8:c * 32 + (j + 1) * 8]
                                .to_broadcast([128, 8, HEADDIM]), ALU.mult)
                            e.drain()
                            e.tensor_tensor(
                                hx(SCR[:, sl], 8),
                                hx(SXT[:, dto + j * 512:dto + (j + 1) * 512], 8),
                                D_BC[:, j * 8:(j + 1) * 8]
                                .to_broadcast([128, 8, HEADDIM]), ALU.mult)
                            e.drain()
                            e.tensor_tensor(Y1[:, sl], Y1[:, sl], SCR[:, sl],
                                            ALU.add)
                            e.drain()
                            return e.tensor_tensor(
                                Y1[:, sl], Y1[:, sl],
                                SZ[:, (c % 2) * D_INNER + j * 512:
                                   (c % 2) * D_INNER + (j + 1) * 512],
                                ALU.mult)
                        mep = st("V", epa,
                                 waits=(yh + [m_op.get(c - 1),
                                              m_szg.get(c)]) if j == 0
                                 else [])
                        drip(1)
                        yield
                    m_mep[c] = mep
                    if debug:
                        mdbg = st("Q", lambda e, c=c: [
                            e.dma_start(out=dbg["dbg_z"][c],
                                        in_=SZ[:, (c % 2) * D_INNER:
                                               (c % 2 + 1) * D_INNER]),
                            e.dma_start(out=dbg["dbg_sx"][c],
                                        in_=SXT[:, (c % 2) * D_INNER:
                                                (c % 2 + 1) * D_INNER]),
                            e.dma_start(out=dbg["dbg_y"][c], in_=Y1[:]),
                        ], waits=[mep, m_sxt[c]], sem="QD", inc=3 * 16)
                        m_mep[c] = mdbg
                        yield
                    def gtr(e):
                        for j in range(16):
                            li = e.transpose(P23B[:, j * 128:(j + 1) * 128],
                                             Y1[:, j * 128:(j + 1) * 128],
                                             IBF[:])
                        return li
                    mgt = st("T", gtr, waits=[mep, st_c23[0]])
                    drip(4)
                    yield
                    ev_gfm = st("S", lambda e: e.copy(SCR[:], P23B[:, 0:2048]),
                                waits=[mgt])
                    st_c23[0] = ev_gfm
                    yield
                    mss = st("S", lambda e, c=c: e.activation(
                        Y1[:], Y1[:], AF.Square, accum_out=SS[:, c:c + 1]),
                        waits=[mgt, m_mep[c]])
                    yield
                    def opmm(e):
                        for kd in range(NKI):
                            for bb in range(2):
                                li = e.matmul(
                                    P23[:, bb * 512:(bb + 1) * 512],
                                    SCR[:, kd * 128:(kd + 1) * 128],
                                    WOUT[:, kd * D_MODEL + bb * 512:
                                         kd * D_MODEL + (bb + 1) * 512],
                                    start=(kd == 0), stop=(kd == NKI - 1))
                        return li
                    mop = st("T", opmm, waits=[ev_gfm] +
                             (wom if c == 0 else []))
                    m_op[c] = mop
                    drip(5)
                    yield
                    ev_att = st("S", lambda e: e.copy(ATTS[:], P23[:]),
                                waits=[mop, m_att[0]])
                    st_c23[0] = ev_att
                    yield
                    m_att[0] = st("S", lambda e, c=c: e.dma_start(
                        out=attd[c], in_=ATTS[:]),
                        waits=[ev_att], sem="AT0", inc=16)
                    yield

                def pump(gens):
                    alive = [g for g in gens if g is not None]
                    while alive:
                        for g in list(alive):
                            try:
                                next(g)
                            except StopIteration:
                                alive.remove(g)

                pump([front_gen(0)])
                for c in range(NCH):
                    pump([back_gen(c),
                          front_gen(c + 1) if c + 1 < NCH else None])

                drip(10 ** 6)
                att_store = m_att[0]
                if debug:
                    st("Q", lambda e: e.dma_start(out=dbg["dbg_sall"],
                                                  in_=S_ALL[:]),
                       waits=[m_supd[NCH - 1]], sem="QD", inc=16)

                ctr = sc.resolve(ctr)
                bar2 = [sc.last[k] for k in ("T", "V", "S", "G")
                        if k in sc.last]
                bar2 += [att_store, dumps[-1]]
                with nc.Block() as blk:
                    blk.tensor(lambda e: sc.emit("T", e, sems))
                    blk.vector(lambda e: sc.emit("V", e, sems))
                    blk.scalar(lambda e: sc.emit("S", e, sems))
                    blk.sync(lambda e: sc.emit("Q", e, sems))
                    blk.gpsimd(lambda e: sc.emit("G", e, sems))

        if stages < 3:
            return nc
        with ExitStack() as p2:
            C1 = p2.enter_context
            ATR = C1(nc.sbuf_tensor([128, 4 * D_MODEL], BF16))
            XR2 = C1(nc.sbuf_tensor([128, 4 * D_MODEL], F32))
            OACR = C1(nc.sbuf_tensor([128, 4 * D_MODEL], BF16))
            OST = C1(nc.sbuf_tensor([128, 4 * D_MODEL], F32))

            sc = Script()
            st = sc.step

            mrms = st("S", lambda e: [
                e.activation(RMS[:], SS[:], AF.Ln, scale=1.0 / D_INNER,
                             bias=KCOL[:, 0:1]),
                e.drain(),
                e.activation(RMS[:], RMS[:], AF.Exp, scale=-0.5),
            ][-1], waits=bar2)

            fcons = [None, None, None, None]
            ocons = [None, None, None, None]
            for t in range(NCH):
                xb = t % 4
                qr = ["QR0", "QR1", "QX0", "QX1"][xb]
                qa = ["QA0", "QA1", "QW0", "QW1"][xb]
                qb = ["QB0", "QB1", "QO0", "QO1"][xb]
                mx2 = st("Q", lambda e, t=t, xb=xb: e.dma_start(
                    out=XR2[:, xb * D_MODEL:(xb + 1) * D_MODEL],
                    in_=d["x"][t * 128:(t + 1) * 128, :]),
                    waits=[fcons[xb]], sem=qr, inc=16)
                ma2 = st("Q", lambda e, t=t, xb=xb: e.dma_start(
                    out=ATR[:, xb * D_MODEL:(xb + 1) * D_MODEL], in_=attd[t]),
                    waits=[fcons[xb], att_store], sem=qa, inc=16)
                mo2 = st("Q", lambda e, t=t, xb=xb: [
                    e.dma_start(out=OACR[:, xb * D_MODEL + q * 256:
                                xb * D_MODEL + (q + 1) * 256],
                                in_=oaccd[q][:, t * 256:(t + 1) * 256])
                    for q in range(4)],
                    waits=[fcons[xb]] + dumps, sem=qb, inc=4 * 16)
                def fin(e, t=t, xb=xb):
                    e.scalar_tensor_tensor(
                        OST[:, xb * D_MODEL:(xb + 1) * D_MODEL],
                        ATR[:, xb * D_MODEL:(xb + 1) * D_MODEL],
                        RMS[:, t:t + 1],
                        OACR[:, xb * D_MODEL:(xb + 1) * D_MODEL],
                        ALU.mult, ALU.add)
                    e.drain()
                    return e.tensor_tensor(
                        OST[:, xb * D_MODEL:(xb + 1) * D_MODEL],
                        OST[:, xb * D_MODEL:(xb + 1) * D_MODEL],
                        XR2[:, xb * D_MODEL:(xb + 1) * D_MODEL], ALU.add)
                mf = st("V", fin, waits=[mx2, ma2, mo2, mrms, ocons[xb]])
                fcons[xb] = mf
                ocons[xb] = st("S", lambda e, t=t, xb=xb: e.dma_start(
                    out=out[t * 128:(t + 1) * 128, :],
                    in_=OST[:, xb * D_MODEL:(xb + 1) * D_MODEL]),
                    waits=[mf], sem=["O0", "O1", "AT1", "QC"][xb], inc=16)

            st("S", lambda e: e.nop(),
               waits=[ocons[0], ocons[1], ocons[2], ocons[3], att_store],
               inc=0)

            ctr = sc.resolve(ctr)
            with nc.Block() as blk:
                blk.tensor(lambda e: sc.emit("T", e, sems))
                blk.vector(lambda e: sc.emit("V", e, sems))
                blk.scalar(lambda e: sc.emit("S", e, sems))
                blk.sync(lambda e: sc.emit("Q", e, sems))
    return nc


def _consts():
    idx = np.arange(Q)
    mle = (idx[:, None] <= idx[None, :]).astype(BF)
    mskt = np.where(idx[None, :] <= idx[:, None], 0.0, NEG).astype(np.float32)
    ibf = np.eye(Q).astype(BF)
    if32 = np.eye(Q).astype(np.float32)
    onef = np.ones((128, Q), dtype=np.float32)
    return mle, mskt, ibf, if32, onef


def make_inputs(L0_attn_x, L0_attn_in_proj_w, L0_attn_conv_w, L0_attn_conv_b,
                L0_attn_norm_w, L0_attn_out_proj_w, L0_attn_dt_bias,
                L0_attn_A_log, L0_attn_D, L0_ffn_gate_w, L0_ffn_expert_w):
    x = np.ascontiguousarray(np.asarray(L0_attn_x, dtype=np.float32))
    wi = np.ascontiguousarray(
        np.concatenate([np.asarray(L0_attn_in_proj_w, np.float32),
                        np.asarray(L0_ffn_gate_w, np.float32).T], axis=1))
    wo = np.ascontiguousarray(np.asarray(L0_attn_out_proj_w, np.float32))
    we = np.ascontiguousarray(
        np.asarray(L0_ffn_expert_w, np.float32).transpose(0, 2, 1))
    cwr = np.asarray(L0_attn_conv_w, np.float32)
    cw = np.ascontiguousarray(
        cwr.T.reshape(NXT, 128, 4).transpose(1, 0, 2).reshape(128, NXT * 4))
    cb = np.ascontiguousarray(
        np.asarray(L0_attn_conv_b, np.float32).reshape(NXT, 128).T)
    nw = np.ascontiguousarray(
        np.asarray(L0_attn_norm_w, np.float32).reshape(NKI, 128).T)
    dtb = np.ascontiguousarray(np.asarray(L0_attn_dt_bias, np.float32)[None, :])
    alog = np.ascontiguousarray(np.asarray(L0_attn_A_log, np.float32)[None, :])
    dvec = np.ascontiguousarray(np.asarray(L0_attn_D, np.float32)[None, :])
    mle, mskt, ibf, if32, onef = _consts()
    shared = dict(wi=wi, wo=wo, we=we, cw=cw, cb=cb, nw=nw, dtb=dtb,
                  alog=alog, dvec=dvec, mle=np.asarray(mle),
                  ibf=np.asarray(ibf), if32=if32, onef=onef)
    return x, shared


def kernel(**inputs):
    x, shared = make_inputs(**inputs)
    nc = build_nc()
    in_maps = [dict(shared, x=np.ascontiguousarray(x[b])) for b in range(8)]
    res = run_bass_kernel_spmd(nc, in_maps, core_ids=list(range(8)))
    return np.stack([r["out"] for r in res.results]).astype(np.float32)
